# version 05
# baseline (speedup 1.0000x reference)
import contextlib
import numpy as np
import concourse.bass as bass
import concourse.mybir as mybir
from concourse.bass_utils import run_bass_kernel_spmd

F32 = mybir.dt.float32
BF16 = mybir.dt.bfloat16
AF = mybir.ActivationFunctionType
ALU = mybir.AluOpType
AX = mybir.AxisListType

D = 1024
NCORES = 8

MAGIC = float(np.float32(1.5 * 2**23))
INV2PI = float(np.float32(1.0 / (2 * np.pi)))
TWOPI = float(np.float32(2 * np.pi))
HALFPI = float(np.float32(np.pi / 2))

ENGS = ("sp", "pe", "dve", "act", "pool")


class Prog:

    ROLL_CAP = 50000

    def __init__(self, nc):
        self.nc = nc
        self.steps = {e: [] for e in ENGS}
        self.counts = {}
        self.sem_names = []
        self.physmap = {}
        self.rollidx = {}

    def sem(self, name):
        if name not in self.counts:
            self.counts[name] = 0
            self.sem_names.append(name)
        return name

    def _phys(self, base):
        return self.physmap.get(base, base)

    def tok(self, base):
        p = self._phys(base)
        return (p, self.counts.get(p, 0))

    def emit(self, eng, fn, waits=(), inc=None, inc_n=1, drain=False,
             mmk=None):
        flat = []
        for w in waits:
            if w is None:
                continue
            if isinstance(w, list):
                flat.extend(x for x in w if x is not None)
            else:
                flat.append(w)
        waits = flat
        if inc is not None:
            phys = self._phys(inc)
            self.sem(phys)
            if self.counts[phys] + inc_n > self.ROLL_CAP:
                self.rollidx[inc] = self.rollidx.get(inc, 0) + 1
                phys = f"{inc}_r{self.rollidx[inc]}"
                self.physmap[inc] = phys
                self.sem(phys)
            self.counts[phys] += inc_n
            tok = (phys, self.counts[phys])
            inc = phys
        else:
            tok = None
        self.steps[eng].append((fn, waits, inc, inc_n if inc else 0, drain,
                                mmk))
        return tok

    def sp(self, fn, waits=(), stream="s0", drain=False):
        sem = "dma_" + stream
        self.sem(self._phys(sem))
        prev = self.tok(sem)
        prev = prev if prev[1] else None
        return self.emit("sp", fn, list(waits) + [prev], sem, 16, drain)

    def pooldma(self, fn, waits=(), stream="p0", drain=False):
        sem = "dma_" + stream
        self.sem(self._phys(sem))
        prev = self.tok(sem)
        prev = prev if prev[1] else None
        return self.emit("pool", fn, list(waits) + [prev], sem, 16, False)

    def pe(self, fn, waits=(), inc=None, drain=False, mmk=None):
        return self.emit("pe", fn, waits, inc, 1, drain, mmk)

    def dve(self, fn, waits=(), inc=None, drain=True):
        return self.emit("dve", fn, waits, inc, 1, drain)

    def act(self, fn, waits=(), inc=None, drain=True):
        return self.emit("act", fn, waits, inc, 1, drain)

    def pool(self, fn, waits=(), inc=None, inc_n=1, drain=True):
        return self.emit("pool", fn, waits, inc, inc_n, drain)

    def barrier(self, name="bar"):
        toks = []
        for e in ENGS:
            if self.steps[e]:
                toks.append(self.emit(e, lambda en: en.nop(), inc=self.sem(name + "_" + e)))
        for e in ENGS:
            if self.steps[e]:
                self.emit(e, lambda en: en.nop(), waits=[t for t in toks])
        return toks

    def build(self, st):
        nc = self.nc
        sems = {n: st.enter_context(nc.semaphore(n)) for n in self.sem_names}
        block = st.enter_context(nc.Block())

        pecfg_sem = st.enter_context(nc.semaphore("pecfg"))
        pecfg_n = [0]

        def _compat(a, b):
            if a is None or b is None:
                return True
            if a[1] == 1 or b[1] == 1:
                return True
            return a == b

        def mk_body(eng_name):
            def body(e):
                waited = {}
                last_mmk = [None]
                for fn, waits, inc, inc_n, drain, mmk in self.steps[eng_name]:
                    for sname, val in waits:
                        if waited.get(sname, 0) < val:
                            e.wait_ge(sems[sname], val)
                            waited[sname] = val
                    if (eng_name == "pe" and mmk is not None
                            and not _compat(last_mmk[0], mmk)):
                        pecfg_n[0] += 1
                        e.nop().then_inc(pecfg_sem, 1)
                        e.wait_ge(pecfg_sem, pecfg_n[0])
                    if mmk is not None:
                        last_mmk[0] = mmk
                    if drain:
                        e.drain()
                    inst = fn(e)
                    if inc is not None:
                        assert inst is not None, f"step for {inc} returned None"
                        inst.then_inc(sems[inc], inc_n)

            return body

        for eng_name, reg in (("sp", block.sync), ("pe", block.tensor),
                              ("dve", block.vector), ("act", block.scalar),
                              ("pool", block.gpsimd)):
            if self.steps[eng_name]:
                reg(mk_body(eng_name))


class BankRot:

    def __init__(self, pp, bank_tok, pool):
        self.pp = pp
        self.bank_tok = bank_tok
        self.pool = pool
        self.i = 0
        self.cur = None

    def alloc(self):
        b = self.pool[self.i % len(self.pool)]
        self.i += 1
        self.cur = b
        return self.pp[b], self.bank_tok[b]

    def set(self, tok):
        self.bank_tok[self.cur] = tok


class Arena:

    def __init__(self, nc, st, cols_f32):
        self.t = st.enter_context(nc.sbuf_tensor("arena", [128, cols_f32], F32))
        self.cols = cols_f32
        self.off = 0
        self.hi = 0

    def reset(self):
        self.hi = max(self.hi, self.off)
        self.off = 0

    def sb(self, name, shape, dtype=F32):
        P_ = shape[0]
        n_el = 1
        for d in shape[1:]:
            n_el *= d
        esz = 2 if dtype == BF16 else 4
        ncol = (n_el * esz + 31) // 32 * 8
        assert self.off + ncol <= self.cols, \
            f"arena overflow at {name}: {self.off + ncol} > {self.cols}"
        view = self.t[0:P_, self.off:self.off + ncol]
        self.off += ncol
        if dtype == BF16:
            view = view.bitcast(BF16)[:, 0:n_el]
        else:
            view = view[:, 0:n_el]
        if len(shape) > 2:
            view = view.rearrange(
                "p (" + " ".join(f"d{i}" for i in range(1, len(shape))) + ") -> p "
                + " ".join(f"d{i}" for i in range(1, len(shape))),
                **{f"d{i}": shape[i] for i in range(1, len(shape))})
        return view


def host_constants():
    import ml_dtypes

    c = {}
    c["c_ident_f"] = np.eye(128, dtype=np.float32)
    c["c_ident_b"] = np.eye(128, dtype=ml_dtypes.bfloat16)
    c["c_ones_row"] = np.ones((1, 512), dtype=np.float32)
    c["c_ones_row_b"] = np.ones((1, 512), dtype=ml_dtypes.bfloat16)
    tau = np.arange(128)[:, None]
    t = np.arange(128)[None, :]
    c["c_mask_incl"] = np.where(tau <= t, 0.0, -1e30).astype(np.float32)
    c["c_mask_strict"] = np.where(tau < t, 0.0, -1e30).astype(np.float32)
    sel = np.zeros((16, 16 * 128), dtype=np.float32)
    for h in range(16):
        sel[h, h * 128:(h + 1) * 128] = 1.0
    c["c_selh"] = sel
    blk = np.zeros((128, 8 * 16), dtype=np.float32)
    for i in range(8):
        for j2 in range(2):
            h = 2 * i + j2
            blk[j2 * 64:(j2 + 1) * 64, i * 16 + h] = 1.0
    c["c_blk16"] = blk
    misc = np.zeros((128, 8), dtype=np.float32)
    misc[:, 0] = 1e-5
    misc[:, 1] = 1e-6
    misc[:, 2] = 1e-12
    misc[:, 3] = 0.5
    misc[:, 4] = HALFPI
    c["c_misc"] = misc
    c["c_onesq"] = np.ones((128, 128), dtype=np.float32)
    c["c_ones_col"] = np.ones((128, 1), dtype=np.float32)
    return c


CONST_SHAPES = {k: v.shape for k, v in host_constants().items()}


def emit_prologue(P, st, aps):
    nc = P.nc
    sb = lambda n_, s_, d_=F32: st.enter_context(nc.sbuf_tensor(n_, s_, d_))
    toks = {}
    aps["sb_ident_f"] = sb("sb_ident_f", [128, 128])
    aps["sb_ident_b"] = sb("sb_ident_b", [128, 128], BF16)
    aps["sb_ones_row"] = sb("sb_ones_row", [1, 512])
    aps["sb_ones_row_b"] = sb("sb_ones_row_b", [1, 512], BF16)
    aps["sb_mask_incl"] = sb("sb_mask_incl", [128, 128])
    aps["sb_mask_strict"] = sb("sb_mask_strict", [128, 128])
    aps["sb_selh"] = sb("sb_selh", [16, 16 * 128])
    aps["sb_blk16"] = sb("sb_blk16", [128, 8 * 16])
    aps["sb_misc"] = sb("sb_misc", [128, 8])
    aps["sb_onesq"] = sb("sb_onesq", [128, 128])
    aps["sb_ones_col"] = sb("sb_ones_col", [128, 1])
    names = ["ident_f", "ident_b", "ones_row", "ones_row_b", "mask_incl",
             "mask_strict", "selh", "blk16", "misc", "onesq", "ones_col"]
    all_toks = []
    for n in names:
        tok = P.sp(lambda e, n=n: e.dma_start(aps["sb_" + n][:], aps["c_" + n]),
                   stream="cst")
        toks[n + "_ready"] = tok
        all_toks.append(tok)
    toks["consts_ready"] = all_toks[-1]
    return toks


def emit_ln_transpose(P, aps, xc, xln, xT, mcol, sqs, pbank, toks, tag,
                      war_xln=None, war_xT=None, war_psum=None, war_sqs=None,
                      t_x=None, eps=1e-5, transpose=True, xlnf=None,
                      xT_cols=None, rot=None):
    t_mean = P.dve(lambda e: e.tensor_reduce(mcol[:, 0:1], xc[:], AX.X, ALU.add),
                   waits=[t_x], inc="dve")
    t_sq = P.act(lambda e: e.activation(sqs[:], xc[:], AF.Square,
                                        accum_out=mcol[:, 1:2]),
                 waits=[t_x, war_sqs, t_mean], inc="act")
    t_st = P.dve(lambda e: e.tensor_scalar(mcol[:, 0:2], mcol[:, 0:2],
                                           1.0 / D, None, ALU.mult),
                 waits=[t_mean, t_sq], inc="dve")
    t_v1 = P.dve(lambda e: e.scalar_tensor_tensor(
        mcol[:, 1:2], mcol[:, 0:1], mcol[:, 0:1], mcol[:, 1:2],
        ALU.mult, ALU.subtract), waits=[t_st], inc="dve")
    eps_col = {1e-5: 0, 1e-6: 1, 1e-12: 2}[eps]
    t_v2 = P.act(lambda e: e.activation(mcol[:, 1:2], mcol[:, 1:2], AF.Sqrt,
                                        bias=aps["sb_misc"][:, eps_col:eps_col + 1],
                                        scale=-1.0),
                 waits=[t_v1, toks["misc_ready"]], inc="act")
    t_v3 = P.dve(lambda e: e.reciprocal(mcol[:, 1:2], mcol[:, 1:2]),
                 waits=[t_v2], inc="dve")
    if xlnf is not None:
        t_lnf = P.dve(lambda e: e.tensor_scalar(xlnf[:], xc[:], mcol[:, 0:1],
                                                mcol[:, 1:2], ALU.subtract,
                                                ALU.mult),
                      waits=[t_v3, war_xln], inc="dve")
        t_ln = P.dve(lambda e: e.tensor_copy(xln[:], xlnf[:]),
                     waits=[t_lnf], inc="dve")
    else:
        t_ln = P.dve(lambda e: e.tensor_scalar(xln[:], xc[:], mcol[:, 0:1],
                                               mcol[:, 1:2], ALU.subtract,
                                               ALU.mult),
                     waits=[t_v3, war_xln], inc="dve")
    if not transpose:
        return t_ln, None, []
    if xT_cols is None:
        xT_cols = lambda t: xT[:, t * 128:(t + 1) * 128]
    tr_toks = []
    if war_psum is None:
        war_psum = []
    elif not isinstance(war_psum, list):
        war_psum = [war_psum]
    t_prev_ev = war_psum
    for t in range(8):
        if rot is not None:
            bk, war = rot.alloc()
            view = bk[:, 0:512].bitcast(BF16)[:, 0:128]
        else:
            bk, war = None, None
            view = pbank[:, 0:256].bitcast(BF16)[:, 0:128]
        tt = P.pe(lambda e, t=t, view=view: e.transpose(
            view, xln[:, t * 128:(t + 1) * 128], aps["sb_ident_b"][:]),
            waits=[t_ln, toks["ident_b_ready"],
                   war_xT if t == 0 else None, war]
                  + (t_prev_ev if isinstance(t_prev_ev, list) else [t_prev_ev]
                     if rot is None else []),
            inc="pe", mmk=(0, 128))
        tev = P.dve(lambda e, t=t, view=view: e.tensor_copy(
            xT_cols(t), view), waits=[tt], inc="dve")
        if rot is not None:
            rot.set(tev)
        else:
            t_prev_ev = tev
        tr_toks.append(tev)
    return t_ln, tr_toks[-1], tr_toks


def emit_l1_ffn(P, st, aps, toks, S_len):
    nc = P.nc
    x_in = aps["x_in"]
    x_out = aps["x_out"]
    ar = aps["arena"]
    ar.reset()
    sb = lambda n_, s_, d_=F32: ar.sb(n_, s_, d_)
    pp = aps["psum"]

    NCH = S_len // 128

    wq = sb("f1_wq", [128, 8 * D], BF16)
    abf = sb("f1_a", [128, 8 * 16], BF16)
    bbf = sb("f1_b", [16, D], BF16)
    biasr = sb("f1_bias", [1, D], F32)
    stage = sb("f1_stage", [128, 2 * D], F32)
    sc = sb("f1_sc", [128, 2], F32)
    srow = sb("f1_srow", [1, 2], F32)
    tmp8 = sb("f1_tmp8", [128, 9], F32)
    xc = [sb(f"f1_xc{i}", [128, D], F32) for i in range(2)]
    sqs = sb("f1_sqs", [128, D], F32)
    xln = sb("f1_xln", [128, D], BF16)
    xT = [sb(f"f1_xT{i}", [128, D], BF16) for i in range(2)]
    xaT = sb("f1_xaT", [16, 128], BF16)
    mcol = sb("f1_mcol", [128, 2], F32)
    yc = [sb(f"f1_yc{i}", [128, D], F32) for i in range(2)]

    start = list(toks.get("start", []))

    t_bias = P.sp(lambda e: e.dma_start(biasr[:], aps["L1_ffn_bias"][None, :]),
                  waits=start, stream="cst")
    t_la = P.sp(lambda e: e.dma_start(
        stage[:, 0:128].rearrange("p (k r) -> p k r", r=16),
        aps["L1_ffn_a"].rearrange("(k p) r -> p k r", p=128)), waits=start,
        stream="cst")
    t_ca = P.dve(lambda e: e.tensor_copy(
        abf[:].rearrange("p (k r) -> p k r", r=16),
        stage[:, 0:128].rearrange("p (k r) -> p k r", r=16)),
        waits=[t_la], inc="dve")
    t_lb = P.sp(lambda e: e.dma_start(stage[0:16, 1024:1024 + D], aps["L1_ffn_b"]),
                waits=start, stream="cst")
    t_cb = P.dve(lambda e: e.tensor_copy(bbf[:], stage[0:16, 1024:1024 + D]),
                 waits=[t_lb], inc="dve")

    red_toks = []
    for k in range(8):
        w = [red_toks[k - 2]] if k >= 2 else ([t_ca, t_cb] + start)
        tk = P.sp(lambda e, k=k: e.dma_start(
            stage[:, (k % 2) * D:(k % 2 + 1) * D],
            aps["L1_ffn_kernel"][k * 128:(k + 1) * 128, :]), waits=w,
            stream=f"wt{k % 2}")
        tr = P.dve(lambda e, k=k: e.reduce_max(
            tmp8[:, k:k + 1], stage[:, (k % 2) * D:(k % 2 + 1) * D], AX.X,
            apply_absolute_value=True), waits=[tk], inc="dve")
        red_toks.append(tr)
    t_red = P.dve(lambda e: e.reduce_max(tmp8[:, 8:9], tmp8[:, 0:8], AX.X,
                                         apply_absolute_value=True),
                  waits=[red_toks[-1]], inc="dve")
    t_tr = P.pe(lambda e: e.transpose(pp[0][0:1, 0:128], tmp8[:, 8:9],
                                      aps["sb_ident_f"][:]),
                waits=[t_red, toks["ident_f_ready"]], inc="pe", mmk=(0, 128))
    t_mx = P.dve(lambda e: e.reduce_max(srow[0:1, 0:1], pp[0][0:1, 0:128], AX.X),
                 waits=[t_tr], inc="dve")
    t_sc = P.dve(lambda e: e.tensor_scalar(srow[0:1, 0:1], srow[0:1, 0:1],
                                           1.0 / 7.0, None, ALU.mult),
                 waits=[t_mx], inc="dve")
    t_rc = P.dve(lambda e: e.reciprocal(srow[0:1, 1:2], srow[0:1, 0:1]),
                 waits=[t_sc], inc="dve")
    t_bc = P.pe(lambda e: e.matmul(pp[0][:, 0:2], aps["sb_ones_row"][0:1, 0:128],
                                   srow[0:1, 0:2], start=True, stop=True),
                waits=[t_rc, toks["ones_row_ready"], t_mx], inc="pe", mmk=(0, 1))
    t_sc2 = P.dve(lambda e: e.tensor_copy(sc[:], pp[0][:, 0:2]),
                  waits=[t_bc], inc="dve")

    t_q_prev = [t_sc2]
    qt = []
    for k in range(8):
        sl = (k % 2) * D
        w = [qt[k - 2]] if k >= 2 else [t_red]
        tk = P.sp(lambda e, sl=sl, k=k: e.dma_start(
            stage[:, sl:sl + D], aps["L1_ffn_kernel"][k * 128:(k + 1) * 128, :]),
            waits=w + t_q_prev, stream=f"wt{k % 2}")
        t1 = P.dve(lambda e, sl=sl: e.tensor_scalar(
            stage[:, sl:sl + D], stage[:, sl:sl + D], sc[:, 1:2], MAGIC,
            ALU.mult, ALU.add), waits=[tk, t_sc2], inc="dve")
        t2 = P.dve(lambda e, sl=sl: e.tensor_scalar(
            stage[:, sl:sl + D], stage[:, sl:sl + D], MAGIC, None, ALU.subtract),
            waits=[t1], inc="dve")
        t3 = P.dve(lambda e, sl=sl: e.tensor_scalar(
            stage[:, sl:sl + D], stage[:, sl:sl + D], 7.0, -8.0, ALU.min,
            ALU.max), waits=[t2], inc="dve")
        t4 = P.dve(lambda e, sl=sl, k=k: e.tensor_scalar(
            wq[:, k * D:(k + 1) * D], stage[:, sl:sl + D], sc[:, 0:1], None,
            ALU.mult), waits=[t3], inc="dve")
        qt.append(t4)
    t_wq = qt[-1]

    ev = {}
    for ci in range(NCH):
        cb = ci % 2
        row = slice(ci * 128, (ci + 1) * 128)
        t_x = P.sp(lambda e, cb=cb, row=row: e.dma_start(
            xc[cb][:], x_in[row, :]),
            waits=[ev.get(("store", ci - 2))] + start, stream=f"x{cb}")
        t_ln, t_trl, _ = emit_ln_transpose(
            P, aps, xc[cb], xln, xT[cb], mcol, sqs, pp[6], toks, f"f1c{ci}",
            war_xln=ev.get(("mmlast", ci - 1)),
            war_xT=ev.get(("mmlast", ci - 2)),
            war_psum=ev.get(("trev", ci - 1)),
            war_sqs=None, t_x=t_x)
        ev[("trev", ci)] = t_trl

        for k in range(8):
            P.pe(lambda e, k=k, cb=cb: e.matmul(
                pp[7][0:16, 0:128], abf[:, k * 16:(k + 1) * 16],
                xT[cb][:, k * 128:(k + 1) * 128],
                start=(k == 0), stop=(k == 7)),
                waits=([t_trl, t_ca, ev.get(("xaT_ev", ci - 1))]
                       if k == 0 else []),
                inc=("pe" if k == 7 else None), mmk=(0, 128))
        t_xa = P.tok("pe")
        t_xae = P.dve(lambda e: e.tensor_copy(xaT[:], pp[7][0:16, 0:128]),
                      waits=[t_xa], inc="dve")
        ev[("xaT_ev", ci)] = t_xae

        mm_toks = []
        for g in range(2):
            for k in range(8):
                P.pe(lambda e, k=k, g=g, cb=cb: e.matmul(
                    pp[g][:, 0:512], xT[cb][:, k * 128:(k + 1) * 128],
                    wq[:, k * D + g * 512:k * D + (g + 1) * 512],
                    start=(k == 0), stop=False),
                    waits=([t_wq, t_trl, ev.get(("mm_ev", ci - 1, g))]
                           if k == 0 else []), mmk=(0, 128))
            P.pe(lambda e, g=g: e.matmul(
                pp[g][:, 0:512], aps["sb_ones_row"][0:1, 0:128],
                biasr[0:1, g * 512:(g + 1) * 512], start=False, stop=False),
                waits=[t_bias], mmk=(0, 1))
            tmm = P.pe(lambda e, g=g: e.matmul(
                pp[g][:, 0:512], xaT[0:16, 0:128],
                bbf[0:16, g * 512:(g + 1) * 512], start=False, stop=True),
                waits=[t_xae, t_cb], inc="pe", mmk=(0, 16))
            mm_toks.append(tmm)
        ev[("mmlast", ci)] = mm_toks[-1]

        evs = []
        for g in range(2):
            tev = P.dve(lambda e, g=g, cb=cb: e.tensor_add(
                yc[cb][:, g * 512:(g + 1) * 512], pp[g][:, 0:512],
                xc[cb][:, g * 512:(g + 1) * 512]),
                waits=[mm_toks[g], ev.get(("store", ci - 2))], inc="dve")
            ev[("mm_ev", ci, g)] = tev
            evs.append(tev)
        t_store = P.sp(lambda e, cb=cb, row=row: e.dma_start(
            x_out[row, :], yc[cb][:]), waits=evs, stream=f"st{cb}")
        ev[("store", ci)] = t_store

    fin = [P.tok("dma_st0"), P.tok("dma_st1")]
    fin = [t for t in fin if t[1] > 0]
    P.emit("sp", lambda e: e.nop(), waits=fin)
    return fin


def emit_l1_attn(P, st, aps, toks, S_len):
    nc = P.nc
    x_in = aps["x_in"]
    x_out = aps["x_out"]
    ar = aps["arena"]
    ar.reset()
    sb = lambda n_, s_, d_=F32: ar.sb(n_, s_, d_)
    pp = aps["psum"]
    NSUP = S_len // 512

    Bt = sb("a1_Bt", [128, 64 * 128], BF16)
    Wt = sb("a1_Wt", [128, 64 * 128], BF16)
    wband = [sb(f"a1_wband{i}", [128, 1024], F32) for i in range(2)]
    wbandb = sb("a1_wbandb", [128, 1024], BF16)
    acol = sb("a1_acol", [128, 8], F32)
    A_bc = sb("a1_Abc", [128, 8 * 512], F32)
    carry = sb("a1_carry", [128, 8], F32)
    brow = sb("a1_brow", [1, 1024], F32)
    xcs = [sb(f"a1_xcs{i}", [128, 4 * 1024], F32) for i in range(2)]
    xlnb = sb("a1_xlnb", [128, 1024], BF16)
    xT_s = [sb(f"a1_xT{i}", [128, 8 * 512], BF16) for i in range(2)]
    hT = sb("a1_hT", [128, 512], F32)
    hbf = [sb(f"a1_hbf{i}", [128, 8 * 512], BF16) for i in range(2)]
    mcol = sb("a1_mcol", [128, 2], F32)
    sqs = sb("a1_sqs", [128, 1024], F32)
    yc = [sb(f"a1_yc{i}", [128, 1024], F32) for i in range(2)]

    start = list(toks.get("start", []))

    a8r = sb("a1_a8r", [8, 128], F32)
    t_Al = P.sp(lambda e: e.dma_start(
        a8r[:], aps["L1_attn_A"].rearrange("(k p) -> k p", p=128)),
        waits=start, stream="cst")
    t_At = P.pe(lambda e: e.transpose(pp[5][:, 0:8], a8r[:],
                                      aps["sb_ident_f"][0:8, 0:8]),
                waits=[t_Al, toks["ident_f_ready"]], inc="pe", mmk=(0, 8))
    t_A = P.dve(lambda e: e.tensor_copy(acol[:], pp[5][:, 0:8]),
                waits=[t_At], inc="dve")
    abc_toks = []
    for i in range(8):
        tt = P.dve(lambda e, i=i: e.tensor_copy(
            A_bc[:, i * 512:(i + 1) * 512],
            acol[:, i:i + 1].broadcast_to((128, 512))), waits=[t_A], inc="dve")
        abc_toks.append(tt)
    t_abc = abc_toks[-1]
    t_cz = P.dve(lambda e: e.memset(carry[:], 0.0), waits=start, inc="dve")
    t_b = P.sp(lambda e: e.dma_start(brow[:], aps["L1_attn_b"][None, :]),
               waits=start, stream="cst")

    tr_done = []
    t_prev_ev = None
    for which, src, dstf in (
        ("B", aps["L1_attn_B"],
         lambda dt, nt: Bt[:, (dt * 8 + nt) * 128:(dt * 8 + nt) * 128 + 128]),
        ("W", aps["L1_attn_W"],
         lambda nt, ot: Wt[:, nt * 1024 + ot * 128: nt * 1024 + ot * 128 + 128]),
    ):
        for rt in range(8):
            t_ld = P.sp(lambda e, rt=rt, src=src: e.dma_start(
                wband[rt % 2][:], src[rt * 128:(rt + 1) * 128, :]),
                waits=[tr_done[-2]] if len(tr_done) >= 2 else start,
                stream=f"wt{rt % 2}")
            t_cast = P.dve(lambda e, rt=rt: e.tensor_copy(
                wbandb[:], wband[rt % 2][:]),
                waits=[t_ld, tr_done[-1] if tr_done else None], inc="dve")
            for ct in range(8):
                tt = P.pe(lambda e, ct=ct: e.transpose(
                    pp[6][:, 0:256].bitcast(BF16)[:, 0:128],
                    wbandb[:, ct * 128:(ct + 1) * 128], aps["sb_ident_b"][:]),
                    waits=[t_cast, toks["ident_b_ready"], t_prev_ev], inc="pe",
                    mmk=(0, 128))
                tev = P.dve(lambda e, ct=ct, rt=rt, dstf=dstf: e.tensor_copy(
                    dstf(ct, rt), pp[6][:, 0:256].bitcast(BF16)[:, 0:128]),
                    waits=[tt], inc="dve")
                t_prev_ev = tev
            tr_done.append(t_prev_ev)
    t_wt = tr_done[-1]

    ev = {}
    for sc in range(NSUP):
        par = sc % 2
        sub_ln = []
        for sub in range(4):
            ci = sc * 4 + sub
            cb = ci % 2
            t_x = P.sp(lambda e, par=par, sub=sub, ci=ci: e.dma_start(
                xcs[par][:, sub * 1024:(sub + 1) * 1024],
                x_in[ci * 128:(ci + 1) * 128, :]),
                waits=[ev.get(("sup_done", sc - 2))] + start, stream=f"x{cb}")
            t_ln, t_trl, _ = emit_ln_transpose(
                P, aps, xcs[par][:, sub * 1024:(sub + 1) * 1024], xlnb,
                None, mcol, sqs, pp[6], toks, f"a1c{ci}",
                war_xln=ev.get(("bmm", sc - 1)) if sub == 0 else None,
                war_xT=ev.get(("bmm", sc - 2)) if sub == 0 else None,
                war_psum=(t_wt if (sc == 0 and sub == 0) else
                          ev.get(("trev", sc, sub - 1)) or
                          ev.get(("trev", sc - 1, 3))),
                t_x=t_x,
                xT_cols=lambda t, par=par, sub=sub: xT_s[par][
                    :, t * 512 + sub * 128: t * 512 + sub * 128 + 128])
            ev[("trev", sc, sub)] = t_trl
            sub_ln.append((t_ln, t_trl))
        t_tr_all = sub_ln[-1][1]

        scan_toks = []
        t_bmm_last = None
        for i in range(8):
            bank = i % 2
            w0 = [t_tr_all, t_wt, ev.get(("scan", sc, i - 2)),
                  ev.get(("scan", sc - 1, 6 + bank))]
            for k in range(8):
                tok = P.pe(lambda e, i=i, k=k, par=par, bank=bank: e.matmul(
                    pp[bank][:, 0:512],
                    Bt[:, (k * 8 + i) * 128:(k * 8 + i) * 128 + 128],
                    xT_s[par][:, k * 512:(k + 1) * 512],
                    start=(k == 0), stop=(k == 7)),
                    waits=w0 if k == 0 else [],
                    inc=("pe" if k == 7 else None), mmk=(0, 128))
            t_bmm_last = P.tok("pe")
            t_scan = P.dve(lambda e, i=i, par=par, bank=bank: e.tensor_tensor_scan(
                hT[:], A_bc[:, i * 512:(i + 1) * 512], pp[bank][:, 0:512],
                carry[:, i:i + 1], ALU.mult, ALU.add),
                waits=[t_bmm_last, t_abc, t_cz,
                       ev.get(("hcast", sc - 1, i))], inc="dve")
            t_cy = P.dve(lambda e, i=i: e.tensor_copy(
                carry[:, i:i + 1], hT[:, 511:512]), waits=[t_scan], inc="dve")
            t_cast = P.dve(lambda e, i=i, par=par: e.tensor_copy(
                hbf[par][:, i * 512:(i + 1) * 512], hT[:]),
                waits=[t_cy, ev.get(("omm", sc - 2))], inc="dve")
            ev[("scan", sc, i)] = t_scan
            ev[("hcast", sc, i)] = t_cast
            scan_toks.append(t_cast)
        ev[("bmm", sc)] = t_bmm_last

        sub_evs = []
        for sub in range(4):
            ci = sc * 4 + sub
            cb = ci % 2
            mm_toks = []
            for g in range(2):
                bank = 2 + g
                w0 = [scan_toks[-1], ev.get(("omm_ev", sc, sub - 1, g)),
                      ev.get(("omm_ev", sc - 1, 3, g))]
                for nt in range(8):
                    P.pe(lambda e, nt=nt, g=g, par=par, sub=sub, bank=bank:
                         e.matmul(
                             pp[bank][:, 0:512],
                             hbf[par][:, nt * 512 + sub * 128:
                                      nt * 512 + sub * 128 + 128],
                             Wt[:, nt * 1024 + g * 512: nt * 1024 + (g + 1) * 512],
                             start=(nt == 0), stop=False),
                         waits=w0 if nt == 0 else [], mmk=(0, 128))
                tmm = P.pe(lambda e, g=g, bank=bank: e.matmul(
                    pp[bank][:, 0:512], aps["sb_ones_row"][0:1, 0:128],
                    brow[0:1, g * 512:(g + 1) * 512], start=False, stop=True),
                    waits=[t_b], inc="pe", mmk=(0, 1))
                mm_toks.append(tmm)
            ev[("omm", sc)] = mm_toks[-1]
            evs = []
            for g in range(2):
                tev = P.dve(lambda e, g=g, par=par, sub=sub, cb=cb: e.tensor_add(
                    yc[cb][:, g * 512:(g + 1) * 512], pp[2 + g][:, 0:512],
                    xcs[par][:, sub * 1024 + g * 512: sub * 1024 + (g + 1) * 512]),
                    waits=[mm_toks[g], ev.get(("store", ci - 2))], inc="dve")
                ev[("omm_ev", sc, sub, g)] = tev
                evs.append(tev)
            t_store = P.sp(lambda e, cb=cb, ci=ci: e.dma_start(
                x_out[ci * 128:(ci + 1) * 128, :], yc[cb][:]),
                waits=evs, stream=f"st{cb}")
            ev[("store", ci)] = t_store
            sub_evs.append(t_store)
        ev[("sup_done", sc)] = sub_evs[-1]

    fin = [P.tok("dma_st0"), P.tok("dma_st1")]
    fin = [t for t in fin if t[1] > 0]
    P.emit("sp", lambda e: e.nop(), waits=fin)
    return fin


def emit_l0_ffn(P, st, aps, toks, S_len):
    nc = P.nc
    x_in = aps["x_in"]
    x_out = aps["x_out"]
    ar = aps["arena"]
    ar.reset()
    sb = lambda n_, s_, d_=F32: ar.sb(n_, s_, d_)
    pp = aps["psum"]
    NCH = S_len // 128
    F = 4096

    w1sb = sb("f0_w1", [128, 8 * F], BF16)
    w2sb = sb("f0_w2", [128, 32 * 1024], BF16)
    wband = [sb(f"f0_wband{i}", [128, 1024], F32) for i in range(2)]
    b1col = sb("f0_b1col", [128, 32], F32)
    b2row = sb("f0_b2row", [1, 1024], F32)
    gb_bc = sb("f0_gbbc", [128, 2 * 1024], F32)
    xc = [sb(f"f0_xc{i}", [128, 1024], F32) for i in range(2)]
    xlnf = [sb(f"f0_xlnf{i}", [128, 1024], F32) for i in range(2)]
    xlnb = sb("f0_xlnb", [128, 1024], BF16)
    xT = [sb(f"f0_xT{i}", [128, 1024], BF16) for i in range(2)]
    hbf0 = sb("f0_hbf0", [128, 32 * 128], BF16)
    hbf = [hbf0, hbf0]
    x2 = sb("f0_x2", [128, 1024], F32)
    mcol = sb("f0_mcol", [128, 2], F32)
    mcol2 = sb("f0_mcol2", [128, 2], F32)
    yc = [sb(f"f0_yc{i}", [128, 1024], F32) for i in range(2)]
    grow = wband[0][0:1, 0:1024]
    grow2 = wband[1][0:1, 0:1024]

    start = list(toks.get("start", []))

    b1r = sb("f0_b1r", [32, 128], F32)
    t_b1l = P.sp(lambda e: e.dma_start(
        b1r[:], aps["L0_ffn_b1"][0:1, :].rearrange("o (f p) -> (o f) p", p=128)),
        waits=start, stream="cst")
    t_b1t = P.pe(lambda e: e.transpose(pp[5][:, 0:32], b1r[:],
                                       aps["sb_ident_f"][0:32, 0:32]),
                 waits=[t_b1l, toks["ident_f_ready"]], inc="pe", mmk=(0, 32))
    t_b1 = P.dve(lambda e: e.tensor_copy(b1col[:], pp[5][:, 0:32]),
                 waits=[t_b1t], inc="dve")
    t_b2 = P.sp(lambda e: e.dma_start(b2row[:], aps["L0_ffn_b2"][0:1, :]),
                waits=start, stream="cst")
    t_g = P.sp(lambda e: e.dma_start(grow[0:1, 0:1024], aps["L0_ffn_gamma"][None, :]),
               waits=start, stream="cst")
    t_be = P.sp(lambda e: e.dma_start(grow2[0:1, 0:1024], aps["L0_ffn_beta"][None, :]),
                waits=start, stream="cst")
    gb_toks = []
    for j in range(4):
        src = (grow[0:1, j * 512:(j + 1) * 512] if j < 2 else
               grow2[0:1, (j - 2) * 512:(j - 1) * 512])
        tmm = P.pe(lambda e, j=j, src=src: e.matmul(
            pp[7][:, 0:512], aps["sb_ones_row"][0:1, 0:128],
            src, start=True, stop=True),
            waits=[t_g, t_be, toks["ones_row_ready"],
                   gb_toks[-1] if gb_toks else None], inc="pe", mmk=(0, 1))
        tev = P.dve(lambda e, j=j: e.tensor_copy(
            gb_bc[:, j * 512:(j + 1) * 512], pp[7][:, 0:512]),
            waits=[tmm], inc="dve")
        gb_toks.append(tev)
    t_gb = gb_toks[-1]

    wt_toks = []
    for dt in range(8):
        for hh in range(4):
            i2 = dt * 4 + hh
            t_ld = P.sp(lambda e, dt=dt, hh=hh, i2=i2: e.dma_start(
                wband[i2 % 2][:],
                aps["L0_ffn_w1"][0:1, dt * 128:(dt + 1) * 128,
                                 hh * 1024:(hh + 1) * 1024].squeeze(0)),
                waits=[wt_toks[-2]] if len(wt_toks) >= 2 else (start + [t_gb]),
                stream=f"wt{i2 % 2}")
            t_cast = P.dve(lambda e, dt=dt, hh=hh, i2=i2: e.tensor_copy(
                w1sb[:, dt * F + hh * 1024: dt * F + (hh + 1) * 1024],
                wband[i2 % 2][:]),
                waits=[t_ld], inc="dve")
            wt_toks.append(t_cast)
    for ft1 in range(32):
        t_ld = P.sp(lambda e, ft1=ft1: e.dma_start(
            wband[ft1 % 2][:],
            aps["L0_ffn_w2"][0:1, ft1 * 128:(ft1 + 1) * 128, :].squeeze(0)),
            waits=[wt_toks[-2]] if len(wt_toks) >= 2 else (start + [t_gb]),
            stream=f"wt{ft1 % 2}")
        t_cast = P.dve(lambda e, ft1=ft1: e.tensor_copy(
            w2sb[:, ft1 * 1024:(ft1 + 1) * 1024],
            wband[ft1 % 2][:]),
            waits=[t_ld], inc="dve")
        wt_toks.append(t_cast)
    t_w = wt_toks[-1]

    ev = {}
    for ci in range(NCH):
        cb = ci % 2
        row = slice(ci * 128, (ci + 1) * 128)
        t_x = P.sp(lambda e, cb=cb, row=row: e.dma_start(
            xc[cb][:], x_in[row, :]),
            waits=[ev.get(("store", ci - 2))] + start, stream=f"x{cb}")
        t_ln, t_trl, _ = emit_ln_transpose(
            P, aps, xc[cb], xlnb, xT[cb], mcol, yc[cb], pp[6], toks, f"f0c{ci}",
            war_xln=ev.get(("h_mm", ci - 1)),
            war_xT=ev.get(("h_mm", ci - 2)),
            war_psum=([ev.get(("hev", ci - 1, 30)), ev.get(("hev", ci - 1, 31))]
                      if ci > 0 else [t_gb]),
            war_sqs=ev.get(("store", ci - 2)),
            t_x=t_x, xlnf=xlnf[cb])
        ev[("trev", ci)] = t_trl

        hev = []
        bankslot = []
        for ft in range(32):
            if ft < 24:
                bankslot.append((ft // 4, ft % 4))
            else:
                bankslot.append((6 + (ft - 24) // 4, (ft - 24) % 4))
        hmm_toks = []
        for ft in range(32):
            bank, slot = bankslot[ft]
            w0 = [t_trl, t_w, ev.get(("hev", ci - 1, ft))]
            for k in range(8):
                P.pe(lambda e, ft=ft, k=k, cb=cb, bank=bank, slot=slot: e.matmul(
                    pp[bank][:, slot * 128:(slot + 1) * 128],
                    w1sb[:, k * F + ft * 128: k * F + (ft + 1) * 128],
                    xT[cb][:, k * 128:(k + 1) * 128],
                    start=(k == 0), stop=(k == 7)),
                    waits=w0 if k == 0 else [],
                    inc=("pe" if k == 7 else None), mmk=(0, 128))
            hmm_toks.append(P.tok("pe"))
        bank_last = {}
        for ft in range(32):
            bank_last[bankslot[ft][0]] = hmm_toks[ft]
        for ft in range(32):
            bank, slot = bankslot[ft]
            w0 = [bank_last[bank], t_b1, ev.get(("w2mm", ci - 1))]
            if bank % 2 == 0:
                tev = P.dve(lambda e, ft=ft, cb=cb, bank=bank, slot=slot:
                            e.tensor_scalar(
                                hbf[cb][:, ft * 128:(ft + 1) * 128],
                                pp[bank][:, slot * 128:(slot + 1) * 128],
                                b1col[:, ft:ft + 1], 0.0, ALU.add, ALU.max),
                            waits=w0, inc="dve")
            else:
                tev = P.act(lambda e, ft=ft, cb=cb, bank=bank, slot=slot:
                            e.activation(
                                hbf[cb][:, ft * 128:(ft + 1) * 128],
                                pp[bank][:, slot * 128:(slot + 1) * 128],
                                AF.Relu, bias=b1col[:, ft:ft + 1]),
                            waits=w0, inc="act")
            ev[("hev", ci, ft)] = tev
            hev.append(tev)
        ev[("h_mm", ci)] = P.tok("pe")

        mm_toks = []
        for g in range(2):
            bank = g
            w0 = hev + [ev.get(("x2ev", ci - 1, g))]
            for ft in range(32):
                P.pe(lambda e, ft=ft, g=g, cb=cb, bank=bank: e.matmul(
                    pp[bank][:, 0:512],
                    hbf[cb][:, ft * 128:(ft + 1) * 128],
                    w2sb[:, ft * 1024 + g * 512: ft * 1024 + (g + 1) * 512],
                    start=(ft == 0), stop=False),
                    waits=w0 if ft == 0 else [], mmk=(0, 128))
            tmm = P.pe(lambda e, g=g, bank=bank: e.matmul(
                pp[bank][:, 0:512], aps["sb_ones_row"][0:1, 0:128],
                b2row[0:1, g * 512:(g + 1) * 512], start=False, stop=True),
                waits=[t_b2], inc="pe", mmk=(0, 1))
            mm_toks.append(tmm)
        ev[("w2mm", ci)] = mm_toks[-1]

        x2ev = []
        for g in range(2):
            tev = P.dve(lambda e, g=g, cb=cb: e.tensor_add(
                x2[:, g * 512:(g + 1) * 512], pp[g][:, 0:512],
                xlnf[cb][:, g * 512:(g + 1) * 512]),
                waits=[mm_toks[g]], inc="dve")
            ev[("x2ev", ci, g)] = tev
            x2ev.append(tev)
        t_mean = P.dve(lambda e: e.tensor_reduce(
            mcol2[:, 0:1], x2[:], AX.X, ALU.add), waits=x2ev, inc="dve")
        t_sq = P.act(lambda e, cb=cb: e.activation(yc[cb][:], x2[:], AF.Square,
                                                   accum_out=mcol2[:, 1:2]),
                     waits=x2ev + [t_mean], inc="act")
        t_st2 = P.dve(lambda e: e.tensor_scalar(
            mcol2[:, 0:2], mcol2[:, 0:2], 1.0 / D, None, ALU.mult),
            waits=[t_mean, t_sq], inc="dve")
        t_v1 = P.dve(lambda e: e.scalar_tensor_tensor(
            mcol2[:, 1:2], mcol2[:, 0:1], mcol2[:, 0:1], mcol2[:, 1:2],
            ALU.mult, ALU.subtract), waits=[t_st2], inc="dve")
        t_v2 = P.act(lambda e: e.activation(
            mcol2[:, 1:2], mcol2[:, 1:2], AF.Sqrt,
            bias=aps["sb_misc"][:, 0:1], scale=-1.0),
            waits=[t_v1, toks["misc_ready"]], inc="act")
        t_v3 = P.dve(lambda e: e.reciprocal(mcol2[:, 1:2], mcol2[:, 1:2]),
                     waits=[t_v2], inc="dve")
        t_n = P.dve(lambda e: e.tensor_scalar(
            x2[:], x2[:], mcol2[:, 0:1], mcol2[:, 1:2], ALU.subtract, ALU.mult),
            waits=[t_v3], inc="dve")
        t_gm = P.dve(lambda e: e.tensor_mul(x2[:], x2[:], gb_bc[:, 0:1024]),
                     waits=[t_n, t_gb], inc="dve")
        t_bt = P.dve(lambda e: e.tensor_add(x2[:], x2[:], gb_bc[:, 1024:2048]),
                     waits=[t_gm], inc="dve")
        t_y = P.dve(lambda e, cb=cb: e.tensor_add(yc[cb][:], x2[:], xc[cb][:]),
                    waits=[t_bt, ev.get(("store", ci - 2))], inc="dve")
        t_store = P.sp(lambda e, cb=cb, row=row: e.dma_start(
            x_out[row, :], yc[cb][:]), waits=[t_y], stream=f"st{cb}")
        ev[("store", ci)] = t_store

    fin = [P.tok("dma_st0"), P.tok("dma_st1")]
    fin = [t for t in fin if t[1] > 0]
    P.emit("sp", lambda e: e.nop(), waits=fin)
    return fin


def emit_l0_attn(P, st, aps, toks, S_len, dbg=None):
    import os
    L0CUT = int(os.environ.get("L0CUT", "99"))
    nc = P.nc
    x_in = aps["x_in"]
    x_out = aps["x_out"]
    ar = aps["arena"]
    ar.reset()
    sb = lambda n_, s_, d_=F32: ar.sb(n_, s_, d_)
    pp = aps["psum"]
    NSUP = S_len // 512
    NCTRL = 6688
    NG = 53

    wcbf = nc.dram_tensor("wcbf", [1024, NCTRL], BF16, kind="Internal").ap()
    ogbuf = nc.dram_tensor("ogbuf", [S_len, 1024], BF16, kind="Internal").ap()

    wband = [sb(f"a0_wband{i}", [128, 1024], F32) for i in range(2)]
    wo = sb("a0_wo", [128, 16 * 1024], BF16)
    wcp = [sb(f"a0_wcp{i}", [128, 8 * 128], BF16) for i in range(2)]
    bgates = sb("a0_bgates", [1, 3616], BF16)
    bout = sb("a0_bout", [1, 1024], BF16)
    rmm = sb("a0_rmm", [128, 1], F32)
    rmv = sb("a0_rmv", [128, 8], F32)
    wcv = sb("a0_wcv", [128, 3 * 32], F32)
    bq24r = sb("a0_bq24r", [24, 128], F32)
    bq24 = sb("a0_bq24", [128, 24], F32)
    sbias = sb("a0_sbias", [128, 24], F32)
    wbc = sb("a0_wbc", [128, 16], F32)
    bt4 = [sb(f"a0_bt{i}", [128, 4 * 128], BF16) for i in range(4)]
    win = [sb(f"a0_win{i}", [128, 8 * 515], BF16) for i in range(3)]
    xc = [sb(f"a0_xc{i}", [128, 1024], F32) for i in range(2)]
    xtreg = sb("a0_xtreg", [128, 2048], F32)
    xT4 = [xtreg[:, i * 512:(i + 1) * 512].bitcast(BF16)[:, 0:1024]
           for i in range(4)]
    xlnb = sb("a0_xlnb", [128, 1024], BF16)
    mcol = sb("a0_mcol", [128, 2], F32)
    sqs = sb("a0_sqs", [128, 1024], F32)
    upd4 = sb("a0_upd4", [128, 4 * 1024], BF16)
    cnd4 = sb("a0_cnd4", [128, 4 * 1024], BF16)
    ang4 = sb("a0_ang4", [128, 4 * 512], BF16)
    ogst = sb("a0_ogst", [128, 1024], BF16)
    abe = sb("a0_abe", [128, 4 * 32], F32)
    qkv256 = sb("a0_qkv256", [128, 256], BF16)
    rc = sb("a0_rc", [128, 1024], F32)
    trg = sb("a0_trg", [128, 1024], F32)
    wbandb = trg[:, 0:512].bitcast(BF16)[:, 0:1024]

    vin = [sb(f"a0_vin{i}", [128, 4 * 128], F32) for i in range(4)]
    plA = [sb(f"a0_plA{i}", [128, 4 * 128], F32) for i in range(6)]
    plB4 = sb("a0_plB4", [128, 4 * 128], F32)
    plB5 = sb("a0_plB5", [128, 4 * 128], F32)
    plB = vin + [plB4, plB5]
    vx0 = sb("a0_vx0", [128, 4 * 9], F32)
    vy0 = sb("a0_vy0", [128, 4 * 9], F32)
    vecbf = [sb(f"a0_vecbf{i}", [128, 4 * 512], BF16) for i in range(2)]
    cl16 = sb("a0_cl16", [16, 128], F32)
    lnahm = sb("a0_lnahm", [16, 128], F32)
    skhm = sb("a0_skhm", [16, 128], F32)
    khat = sb("a0_khat", [128, 128], F32)
    clm16 = sb("a0_clm16", [16, 128], F32)
    hm = sb("a0_hm", [128, 128], F32)
    tok16 = sb("a0_tok16", [128, 128], F32)
    lna = sb("a0_lna", [128, 16], F32)
    beta = sb("a0_beta", [128, 16], F32)
    sktok = sb("a0_sktok", [128, 16], F32)
    fA = sb("a0_fA", [128, 16], F32)
    fash = sb("a0_fash", [128, 16], F32)
    fstate = sb("a0_fstate", [128, 16], F32)
    small6 = sb("a0_small6", [128, 16], F32)
    NTp = [sb(f"a0_NT{i}", [128, 128], F32) for i in range(2)]
    nkpP = [[sb(f"a0_nkp{p_}_{i}", [128, 128], F32) for i in range(2)]
            for p_ in range(2)]
    nksP = [[sb(f"a0_nks{p_}_{i}", [128, 128], F32) for i in range(2)]
            for p_ in range(2)]
    lamzP = [sb(f"a0_lamz{i}", [128, 128], BF16) for i in range(2)]
    vTg = sb("a0_vTg", [128, 512], F32)
    vT = vTg[:, 0:512].bitcast(BF16)[:, 0:1024]
    gate = vTg[:, 0:512]
    kTm = sb("a0_kTm", [128, 512], F32)
    kT = kTm[:, 0:512].bitcast(BF16)[:, 0:1024]
    Rt = xtreg[:, 0:1024]
    YtP = [[sb(f"a0_Yt{p_}_{i}", [128, 64], F32) for i in range(2)]
           for p_ in range(2)]
    Wt_ = sb("a0_Wt", [128, 16 * 64], BF16)
    kstP = [sb(f"a0_kst{i}", [128, 64], BF16) for i in range(2)]
    acbc = sb("a0_acbc", [128, 16], F32)
    acrow = sb("a0_acrow", [1, 16], F32)
    S0 = sb("a0_S0", [128, 8 * 64], F32)
    S0b = sb("a0_S0b", [128, 8 * 64], BF16)

    mT = xtreg[:, 1024:1536].bitcast(BF16)[:, 0:1024]
    rstdv = sb("a0_rstdv", [128, 1], F32)
    ocomb = sb("a0_ocomb", [128, 1024], F32)
    mtile = kTm[:, 0:512].bitcast(BF16)[:, 0:1024]
    acc = [ocomb[:, 0:512], ocomb[:, 512:1024]]
    yc = [rc, trg]

    reads = wband[0]
    rscr = wband[1]
    lgscrP = [rscr[:, 0:256], rscr[:, 256:512]]
    cscr1P = [rscr[:, 512:576], rscr[:, 576:640]]
    cscr2P = [rscr[:, 640:704], rscr[:, 704:768]]
    start = list(toks.get("start", []))
    ev = {}
    IDF = aps["sb_ident_f"]
    IDB = aps["sb_ident_b"]
    ONESR = aps["sb_ones_row"]
    ONESRB = aps["sb_ones_row_b"]
    ONESQ = aps["sb_onesq"]
    MISC = aps["sb_misc"]
    MSKI = aps["sb_mask_incl"]
    MSKS = aps["sb_mask_strict"]
    SELH = aps["sb_selh"]
    BLK16 = aps["sb_blk16"]

    def join(tag):
        return [P.tok(s_) for s_ in ("pe", "dve", "act", "pool")
                if P.tok(s_)[1] > 0]

    pk_toks = []
    for k in range(8):
        for cc in range(7):
            w_ = min(1024, NCTRL - cc * 1024)
            i2 = k * 7 + cc
            tl = P.sp(lambda e, k=k, cc=cc, w_=w_, i2=i2: e.dma_start(
                wband[i2 % 2][:, 0:w_],
                aps["L0_attn_W_ctrl"][k * 128:(k + 1) * 128,
                                      cc * 1024:cc * 1024 + w_]),
                waits=[pk_toks[-2][0]] if len(pk_toks) >= 2 else start,
                stream=f"wt{i2 % 2}")
            tc = P.dve(lambda e, w_=w_, i2=i2: e.tensor_copy(
                wbandb[:, 0:w_], wband[i2 % 2][:, 0:w_]),
                waits=[tl, pk_toks[-1][1] if pk_toks else None], inc="dve")
            ts_ = P.sp(lambda e, k=k, cc=cc, w_=w_: e.dma_start(
                wcbf[k * 128:(k + 1) * 128, cc * 1024:cc * 1024 + w_],
                wbandb[:, 0:w_]), waits=[tc], stream="wcst")
            pk_toks.append((tc, ts_))
    t_wc_done = pk_toks[-1][1]
    t_wc_done2 = pk_toks[-1][1]

    t_rmm0 = P.sp(lambda e: e.dma_start(
        rmm[0:64, 0:1], aps["L0_attn_rms_m_w"].unsqueeze(1)),
        waits=start, stream="cst")
    t_rmm = P.sp(lambda e: e.dma_start(
        rmm[64:128, 0:1], aps["L0_attn_rms_m_w"].unsqueeze(1)),
        waits=start, stream="cst")
    t_rmv = P.sp(lambda e: e.dma_start(
        rmv[:].rearrange("p (s g) -> p s g", s=2),
        aps["L0_attn_rms_v_w"].rearrange("(g p s) -> p s g", p=128, s=2),
        allow_slow_non_contiguous=True), waits=start, stream="cst")

    wo_toks = []
    for mt_ in range(8):
        tl = P.sp(lambda e, mt_=mt_: e.dma_start(
            wband[mt_ % 2][:], aps["L0_attn_W_out"][mt_ * 128:(mt_ + 1) * 128, :]),
            waits=[wo_toks[-2]] if len(wo_toks) >= 2 else [t_wc_done, t_rmm],
            stream=f"wt{mt_ % 2}")
        tc = P.dve(lambda e, mt_=mt_: e.tensor_scalar(
            wo[:, mt_ * 1024:(mt_ + 1) * 1024], wband[mt_ % 2][:],
            rmm[:, 0:1], None, ALU.mult), waits=[tl], inc="dve")
        wo_toks.append(tc)
    for tv in range(8):
        sig = tv // 4
        pg = tv % 4
        tl = P.sp(lambda e, sig=sig, pg=pg, tv=tv: e.dma_start(
            wband[tv % 2][:],
            aps["L0_attn_W_out"][1024 + pg * 256 + sig:
                                 1024 + (pg + 1) * 256:2, :]),
            waits=[wo_toks[-2]], stream=f"wt{tv % 2}")
        tc = P.dve(lambda e, tv=tv: e.tensor_scalar(
            wo[:, (8 + tv) * 1024:(9 + tv) * 1024], wband[tv % 2][:],
            rmv[:, tv:tv + 1], None, ALU.mult), waits=[tl, t_rmv], inc="dve")
        wo_toks.append(tc)
    t_wo = wo_toks[-1]

    t_bg1 = P.pooldma(lambda e: e.dma_start(bgates[0:1, 0:2560],
                                            aps["L0_attn_b_ctrl"][None, 0:2560]),
                      waits=start, stream="pcst")
    t_bg = P.pooldma(lambda e: e.dma_start(bgates[0:1, 2560:3616],
                                           aps["L0_attn_b_ctrl"][None, 5632:6688]),
                     waits=start, stream="pcst")
    t_bo = P.pooldma(lambda e: e.dma_start(bout[:], aps["L0_attn_b_out"][None, :]),
                     waits=start, stream="pcst")
    t_bqr = P.sp(lambda e: e.dma_start(
        bq24r[:], aps["L0_attn_b_ctrl"][2560:5632].rearrange("(j p) -> j p", p=128)),
        waits=start, stream="cst")
    t_bqt = P.pe(lambda e: e.transpose(pp[5][:, 0:24], bq24r[:],
                                       IDF[0:24, 0:24]),
                 waits=[t_bqr, toks["ident_f_ready"]], inc="pe", mmk=(0, 24))
    t_bq = P.dve(lambda e: e.tensor_copy(bq24[:], pp[5][:, 0:24]),
                 waits=[t_bqt], inc="dve")

    cv_toks = []
    for cj, nm in enumerate(["qconv_w", "kconv_w", "vconv_w"]):
        tl = P.sp(lambda e, cj=cj, nm=nm: e.dma_start(
            wcv[:, cj * 32:(cj + 1) * 32].rearrange("p (c t) -> p c t", t=4),
            aps["L0_attn_" + nm].rearrange("(c p) t -> p c t", p=128)),
            waits=start, stream="cst")
        tsum = P.dve(lambda e, cj=cj: e.tensor_reduce(
            sqs[:, cj * 8:(cj + 1) * 8].unsqueeze(2),
            wcv[:, cj * 32:(cj + 1) * 32].rearrange("p (c t) -> p c t", t=4),
            AX.X, ALU.add), waits=[tl], inc="dve")
        cv_toks.append(tsum)
    t_sbias = P.dve(lambda e: e.tensor_mul(sbias[:], bq24[:], sqs[:, 0:24]),
                    waits=cv_toks + [t_bq], inc="dve")
    t_cv = cv_toks[-1]

    t_wb = P.sp(lambda e: e.dma_start(
        wbc[:].rearrange("p (g i j) -> p g i j", i=2, j=2),
        aps["L0_attn_W_block"].rearrange("(g p) i j -> p g i j", p=128)),
        waits=start, stream="cst")
    bt_toks = []
    for ii in range(2):
        for jj in range(2):
            idx = ii * 2 + jj
            tt = P.dve(lambda e, ii=ii, jj=jj, idx=idx: e.tensor_copy(
                bt4[idx][:].rearrange("p (g t) -> p g t", t=128),
                wbc[:].rearrange("p (g i j) -> p g i j", i=2, j=2)
                [:, :, ii, jj].unsqueeze(2).broadcast_to((128, 4, 128))),
                waits=[t_wb], inc="dve")
            bt_toks.append(tt)
    t_bt = bt_toks[-1]

    t_s0 = P.dve(lambda e: e.memset(S0[:], 0.0), waits=start, inc="dve")
    t_hm0 = P.dve(lambda e: e.memset(hm[:], 0.0), waits=start, inc="dve")
    t_s0b = P.dve(lambda e: e.memset(S0b[:], 0.0), waits=start, inc="dve")
    hsl = lambda h: slice((h % 2) * 64, (h % 2) * 64 + 64)
    csl = lambda h: slice((h // 2) * 64, (h // 2) * 64 + 64)
    t_vc = P.dve(lambda e: e.memset(vx0[:], 0.0), waits=start, inc="dve")
    t_vc2 = P.dve(lambda e: e.memset(vy0[:], 0.0), waits=start, inc="dve")
    tail_toks = []
    for cj in range(3):
        tt = P.dve(lambda e, cj=cj: e.tensor_scalar(
            win[cj][:].rearrange("p (c t) -> p c t", t=515)[:, :, 0:3],
            bq24[:, cj * 8:(cj + 1) * 8].unsqueeze(2).broadcast_to((128, 8, 3)),
            -1.0, None, ALU.mult), waits=[t_bq] + start, inc="dve")
        tail_toks.append(tt)
    t_tails = tail_toks[-1]

    pp6_last = [toks["consts_ready"]]
    bank_tok = {b_: toks["consts_ready"] for b_ in (2, 3, 4, 5, 6, 7)}
    rot_a1 = BankRot(pp, bank_tok, [6, 7])
    rot_qkv = BankRot(pp, bank_tok, [6, 7, 2, 3])
    rot_vg = BankRot(pp, bank_tok, [4, 5])
    rot_vk = BankRot(pp, bank_tok, [6, 7])
    rot_mt = BankRot(pp, bank_tok, [6, 7])

    def dbg_store(name, buf_ap, waitl):
        if dbg is None or name not in dbg:
            return
        tok = P.sp(lambda e: e.dma_start(dbg[name], buf_ap), waits=waitl,
                   stream="dbgs")
        ev[("dbg_last",)] = tok

    cuts = [0, 1024, 1536, 2560, 3584, 4608, 5632, 5648, 5664, 6688]
    names = ["ug", "ang", "cand", "q", "k", "v", "a", "be", "og"]

    for sup in range(NSUP):
        prev_join = ev.get(("join", sup - 1), [])
        ln_toks = []
        for c in range(4):
            ci = sup * 4 + c
            t_x = P.sp(lambda e, c=c, ci=ci: e.dma_start(
                xc[c % 2][:], x_in[ci * 128:(ci + 1) * 128, :]),
                waits=[ln_toks[c - 2] if c >= 2 else None] + start
                      + ([prev_join] if c < 2 else []), stream=f"x{ci % 2}")
            t_ln, t_trl, _ = emit_ln_transpose(
                P, aps, xc[c % 2][:], xlnb, xT4[c][:], mcol, sqs, pp[6], toks,
                f"a0c{ci}",
                war_xln=prev_join if c == 0 else None,
                war_xT=prev_join if c == 0 else None,
                war_sqs=prev_join if c == 0 else None,
                t_x=t_x, rot=rot_a1)
            ln_toks.append(t_trl)
        t_lnall = ln_toks[-1]

        sched = ([(ng, c) for ng in range(44) for c in range(4)]
                 + [(ng, c) for c in range(4) for ng in range(44, NG)])
        piece_tok = {}
        load_i = [0]
        gev = {}
        seg_ev = {}
        gidx = [0]
        last_piece = [None]
        for ng, c in sched:
            col0 = ng * 128
            wn = min(128, NCTRL - col0)
            pkey = (ng, c if ng >= 44 else -1)
            if last_piece[0] != pkey:
                li = load_i[0]
                tld_a = P.sp(lambda e, col0=col0, wn=wn, li=li: e.dma_start(
                    wcp[li % 2][:].rearrange("p (k n) -> p k n", n=128)
                    [:, 0:4, 0:wn],
                    wcbf[0:512, col0:col0 + wn]
                    .rearrange("(k p) n -> p k n", p=128)),
                    waits=[gev.get(gidx[0] - 2), t_wc_done, t_wc_done2],
                    stream=f"wc{(li % 2) * 2}")
                tld_b = P.sp(lambda e, col0=col0, wn=wn, li=li: e.dma_start(
                    wcp[li % 2][:].rearrange("p (k n) -> p k n", n=128)
                    [:, 4:8, 0:wn],
                    wcbf[512:1024, col0:col0 + wn]
                    .rearrange("(k p) n -> p k n", p=128)),
                    waits=[gev.get(gidx[0] - 2), t_wc_done, t_wc_done2],
                    stream=f"wc{(li % 2) * 2 + 1}")
                piece_tok[pkey] = ([tld_a, tld_b], li % 2)
                load_i[0] += 1
                last_piece[0] = pkey
            tld, pbuf = piece_tok[pkey]
            g_ = gidx[0]
            gidx[0] += 1
            bank = g_ % 2
            segs = []
            for si in range(9):
                lo = max(col0, cuts[si])
                hi = min(col0 + wn, cuts[si + 1])
                if lo < hi:
                    segs.append((names[si], lo, hi))
            has_bias = any(n_ in ("ug", "ang", "cand", "a", "be", "og")
                           for n_, _, _ in segs)
            w0 = [tld, t_lnall, gev.get(g_ - 2),
                  prev_join if ng == 0 else None]
            for k in range(8):
                last_mm = (k == 7) and not has_bias
                P.pe(lambda e, k=k, bank=bank, wn=wn, pbuf=pbuf, c=c,
                     has_bias=has_bias: e.matmul(
                    pp[bank][:, 0:wn],
                    xT4[c][:, k * 128:(k + 1) * 128],
                    wcp[pbuf][:].rearrange("p (k n) -> p k n", n=128)[:, k, 0:wn],
                    start=(k == 0), stop=(k == 7 and not has_bias)),
                    waits=w0 if k == 0 else [],
                    inc=("pe" if last_mm else None), mmk=(0, 128))
            if has_bias:
                bias_sl = []
                for n_, lo, hi in segs:
                    if n_ in ("q", "k", "v"):
                        continue
                    boff = lo if lo < 2560 else (2560 + lo - 5632)
                    bias_sl.append((lo - col0, boff, hi - lo))
                for bi, (po, boff, bw) in enumerate(bias_sl):
                    P.pe(lambda e, bank=bank, po=po, boff=boff, bw=bw,
                         last=(bi == len(bias_sl) - 1): e.matmul(
                        pp[bank][:, po:po + bw], ONESRB[0:1, 0:128],
                        bgates[0:1, boff:boff + bw], start=False, stop=last),
                        waits=[t_bg, t_bg1],
                        inc=("pe" if bi == len(bias_sl) - 1 else None),
                        mmk=(0, 1))
            tmm = P.tok("pe")

            evs = []
            for n_, lo, hi in segs:
                po = lo - col0
                ww = hi - lo
                if n_ == "ug":
                    tevv = P.act(lambda e, c=c, lo=lo, po=po, ww=ww,
                                 bank=bank: e.activation(
                        upd4[:, c * 1024 + lo: c * 1024 + lo + ww],
                        pp[bank][:, po:po + ww], AF.Sigmoid),
                        waits=[tmm, prev_join if ng == 0 else None],
                        inc="act")
                elif n_ == "cand":
                    tevv = P.act(lambda e, c=c, lo=lo, po=po, ww=ww,
                                 bank=bank: e.activation(
                        cnd4[:, c * 1024 + lo - 1536:
                             c * 1024 + lo - 1536 + ww],
                        pp[bank][:, po:po + ww], AF.Silu),
                        waits=[tmm, prev_join if ng == 12 else None],
                        inc="act")
                elif n_ == "ang":
                    tevv = P.dve(lambda e, c=c, lo=lo, po=po, ww=ww,
                                 bank=bank: e.tensor_copy(
                        ang4[:, c * 512 + lo - 1024:
                             c * 512 + lo - 1024 + ww],
                        pp[bank][:, po:po + ww]),
                        waits=[tmm, prev_join if ng == 8 else None],
                        inc="dve")
                elif n_ in ("q", "k", "v"):
                    tevv = P.dve(lambda e, po=po, ww=ww, bank=bank:
                                 e.tensor_copy(
                        qkv256[:, 0:ww], pp[bank][:, po:po + ww]),
                        waits=[tmm, ev.get(("qkvtr_prev",))], inc="dve")
                    cglob = (lo - 2560) // 128
                    cj = cglob // 8
                    c8 = cglob % 8
                    bk, war = rot_qkv.alloc()
                    view = bk[:, 0:512].bitcast(BF16)[:, 0:128]
                    ttr = P.pe(lambda e, view=view: e.transpose(
                        view, qkv256[:, 0:128], IDB[:]),
                        waits=[tevv, war], inc="pe", mmk=(0, 128))
                    tte = P.dve(lambda e, cj=cj, c8=c8, c=c, view=view:
                                e.tensor_copy(
                        win[cj][:].rearrange("p (ch t) -> p ch t", t=515)
                        [:, c8, 3 + c * 128: 3 + c * 128 + 128], view),
                        waits=[ttr, (t_tails if sup == 0 else None),
                               prev_join if ng == 20 else None,
                               ev.get(("dbg_last",))], inc="dve")
                    rot_qkv.set(tte)
                    tevv = tte
                    ev[("qkvtr_prev",)] = tevv
                elif n_ in ("a", "be"):
                    dst0 = c * 32 + (0 if n_ == "a" else 16)
                    tevv = P.dve(lambda e, dst0=dst0, po=po, ww=ww,
                                 bank=bank: e.tensor_copy(
                        abe[:, dst0:dst0 + ww], pp[bank][:, po:po + ww]),
                        waits=[tmm, prev_join if ng == 44 else None],
                        inc="dve")
                else:
                    tevv = P.dve(lambda e, lo=lo, po=po, ww=ww,
                                 bank=bank: e.tensor_copy(
                        ogst[:, lo - 5664: lo - 5664 + ww],
                        pp[bank][:, po:po + ww]),
                        waits=[tmm, prev_join if (ng == 44 and c == 0) else None,
                               ev.get(("ogstore", sup, c - 1)),
                               ev.get(("ogload", sup - 1, 3))],
                        inc="dve")
                    if lo + ww == 6688:
                        tso = P.sp(lambda e, c=c, sup=sup: e.dma_start(
                            ogbuf[(sup * 4 + c) * 128:
                                  (sup * 4 + c + 1) * 128, :], ogst[:]),
                            waits=[tevv], stream="og")
                        ev[("ogstore", sup, c)] = tso
                evs.append(tevv)
            gev[g_] = evs[-1]
            for n_, lo, hi in segs:
                seg_ev[(n_, c)] = evs[-1]
        ev[("ctrl_last", sup)] = P.tok("pe")

        def vec_gen(c):
            ci = sup * 4 + c
            wv_prev = ev.get(("vecdone", ci - 1))
            w0v = [ev.get(("dn_done", ci - 1)), ev.get(("ctrl_last", sup)),
                   ev.get(("storedma", ci - 2)),
                   ev.get(("storedma", ci - 1)), wv_prev]
            vscr = vin[0][:]
            vsin = ocomb
            sq4s = trg[:, 0:512]
            cscr = trg[:, 512:528]
            t_rc = P.dve(lambda e, c=c: e.tensor_mul(
                rc[:], upd4[:, c * 1024:(c + 1) * 1024],
                cnd4[:, c * 1024:(c + 1) * 1024]),
                waits=[seg_ev[("ug", c)], seg_ev[("cand", c)]] + w0v,
                inc="dve")
            t_rc = P.dve(lambda e, c=c: e.tensor_sub(
                rc[:], cnd4[:, c * 1024:(c + 1) * 1024], rc[:]),
                waits=[t_rc], inc="dve")
            yield
            t_t1 = P.dve(lambda e, c=c: e.tensor_scalar(
                trg[:, 0:512], ang4[:, c * 512:(c + 1) * 512], INV2PI, MAGIC,
                ALU.mult, ALU.add),
                waits=[seg_ev[("ang", c)]] + w0v,
                inc="dve")
            t_t1 = P.dve(lambda e: e.tensor_scalar(
                trg[:, 0:512], trg[:, 0:512], MAGIC, TWOPI, ALU.subtract,
                ALU.mult), waits=[t_t1], inc="dve")
            t_t1 = P.dve(lambda e, c=c: e.scalar_tensor_tensor(
                trg[:, 0:512], trg[:, 0:512], -1.0,
                ang4[:, c * 512:(c + 1) * 512], ALU.mult, ALU.add),
                waits=[t_t1], inc="dve")
            t_t2 = P.dve(lambda e, c=c: e.tensor_scalar(
                trg[:, 512:1024], ang4[:, c * 512:(c + 1) * 512], 1.0, HALFPI,
                ALU.mult, ALU.add), waits=[seg_ev[("ang", c)]] + w0v,
                inc="dve")
            t_t2 = P.dve(lambda e, vscr=vscr: e.tensor_scalar(
                vscr[:, 0:512], trg[:, 512:1024], INV2PI, MAGIC, ALU.mult,
                ALU.add), waits=[t_t2, t_rc], inc="dve")
            t_t2 = P.dve(lambda e, vscr=vscr: e.tensor_scalar(
                vscr[:, 0:512], vscr[:, 0:512], MAGIC, TWOPI, ALU.subtract,
                ALU.mult), waits=[t_t2], inc="dve")
            t_t2 = P.dve(lambda e, vscr=vscr: e.tensor_sub(
                trg[:, 512:1024], trg[:, 512:1024], vscr[:, 0:512]),
                waits=[t_t2], inc="dve")
            t_sin = P.act(lambda e, vsin=vsin: e.activation(
                vsin[:, 0:512], trg[:, 0:512], AF.Sin),
                waits=[t_t1, wv_prev], inc="act")
            t_cos = P.act(lambda e, vsin=vsin: e.activation(
                vsin[:, 512:1024], trg[:, 512:1024], AF.Sin),
                waits=[t_t2], inc="act")
            yield
            vin_toks = []
            for pg in range(4):
                for sig in range(2):
                    bk, war = rot_vg.alloc()
                    view = bk[:, 0:512].bitcast(BF16)[:, 0:128]
                    tt = P.pe(lambda e, c=c, pg=pg, sig=sig, view=view:
                              e.transpose(
                        view, upd4[:, c * 1024 + pg * 256 + sig:
                                   c * 1024 + (pg + 1) * 256:2], IDB[:]),
                        waits=[war, seg_ev[("ug", c)]], inc="pe",
                        mmk=(0, 128))
                    tev = P.dve(lambda e, pg=pg, sig=sig, view=view:
                                e.tensor_copy(
                        vin[sig][:, pg * 128:(pg + 1) * 128], view),
                        waits=[tt, wv_prev], inc="dve")
                    rot_vg.set(tev)
                    vin_toks.append(tev)
                for sig in range(2):
                    bk, war = rot_vg.alloc()
                    tt = P.pe(lambda e, pg=pg, sig=sig, bk=bk: e.transpose(
                        bk[:, 0:128],
                        rc[:, pg * 256 + sig:(pg + 1) * 256:2], IDF[:]),
                        waits=[war, t_rc], inc="pe", mmk=(0, 128))
                    tev = P.dve(lambda e, pg=pg, sig=sig, bk=bk: e.tensor_copy(
                        plA[4 + sig][:, pg * 128:(pg + 1) * 128],
                        bk[:, 0:128]), waits=[tt, wv_prev], inc="dve")
                    rot_vg.set(tev)
                    vin_toks.append(tev)
                for sig, src in ((3, 0), (2, 512)):
                    bk, war = rot_vg.alloc()
                    tt = P.pe(lambda e, pg=pg, src=src, bk=bk, vsin=vsin:
                              e.transpose(
                        bk[:, 0:128],
                        vsin[:, src + pg * 128: src + (pg + 1) * 128],
                        IDF[:]),
                        waits=[war, t_sin, t_cos], inc="pe",
                        mmk=(0, 128))
                    tev = P.dve(lambda e, pg=pg, sig=sig, bk=bk: e.tensor_copy(
                        vin[sig][:, pg * 128:(pg + 1) * 128], bk[:, 0:128]),
                        waits=[tt, wv_prev], inc="dve")
                    rot_vg.set(tev)
                    vin_toks.append(tev)
                yield
            t_vin = vin_toks[-1]

            bt_w = [t_bt, t_vin]
            pb = []
            for dst, g_, (tr1, bb1), (tr2, bb2), sgn in (
                    (0, 0, (2, 0), (3, 1), -1.0),
                    (1, 0, (2, 2), (3, 3), -1.0),
                    (2, 1, (3, 0), (2, 1), 1.0),
                    (3, 1, (3, 2), (2, 3), 1.0)):
                ta = P.dve(lambda e, tr1=tr1, bb1=bb1: e.tensor_mul(
                    plB4[:], vin[tr1][:], bt4[bb1][:]),
                    waits=bt_w + (pb[-1:] if pb else []), inc="dve")
                tb = P.dve(lambda e, tr2=tr2, bb2=bb2: e.tensor_mul(
                    plB5[:], vin[tr2][:], bt4[bb2][:]), waits=[ta], inc="dve")
                tc_ = P.dve(lambda e, sgn=sgn: e.tensor_tensor(
                    plB4[:], plB4[:], plB5[:],
                    ALU.add if sgn > 0 else ALU.subtract),
                    waits=[ta, tb], inc="dve")
                td = P.dve(lambda e, dst=dst, g_=g_: e.tensor_mul(
                    plA[dst][:], plB4[:], vin[g_][:]), waits=[tc_], inc="dve")
                pb.append(td)
                if dst == 1:
                    yield
            t_planes = pb[-1]
            yield

            def v4(t_):
                return t_[:].rearrange("p (g sub t) -> p g sub t", g=4, t=16)

            sq4 = sq4s.rearrange("p (g sub t) -> p g sub t", g=4, t=16)
            cur, nxt = plA, plB
            t_lvl = t_planes
            for s_ in (1, 2, 4, 8):
                lw = [t_lvl]
                last = None
                for pl in range(6):
                    last = P.dve(lambda e, pl=pl, s_=s_, cur=cur, nxt=nxt:
                                 e.tensor_copy(
                        v4(nxt[pl])[:, :, :, 0:s_],
                        v4(cur[pl])[:, :, :, 0:s_]), waits=lw, inc="dve")
                for d_, a_, b_, c2, d2 in ((0, 0, 0, 1, 2), (1, 0, 1, 1, 3),
                                           (2, 2, 0, 3, 2), (3, 2, 1, 3, 3)):
                    t1 = P.dve(lambda e, a_=a_, b_=b_, s_=s_, cur=cur:
                               e.tensor_mul(
                        sq4[:, :, :, s_:16],
                        v4(cur[a_])[:, :, :, s_:16],
                        v4(cur[b_])[:, :, :, 0:16 - s_]), waits=lw + [last],
                        inc="dve")
                    t2 = P.dve(lambda e, d_=d_, c2=c2, d2=d2, s_=s_, cur=cur,
                               nxt=nxt: e.tensor_mul(
                        v4(nxt[d_])[:, :, :, s_:16],
                        v4(cur[c2])[:, :, :, s_:16],
                        v4(cur[d2])[:, :, :, 0:16 - s_]), waits=lw, inc="dve")
                    last = P.dve(lambda e, d_=d_, s_=s_, nxt=nxt: e.tensor_add(
                        v4(nxt[d_])[:, :, :, s_:16],
                        v4(nxt[d_])[:, :, :, s_:16],
                        sq4[:, :, :, s_:16]), waits=[t1, t2], inc="dve")
                for d_, a_, b_ in ((4, 0, 1), (5, 2, 3)):
                    t1 = P.dve(lambda e, a_=a_, s_=s_, cur=cur: e.tensor_mul(
                        sq4[:, :, :, s_:16],
                        v4(cur[a_])[:, :, :, s_:16],
                        v4(cur[4])[:, :, :, 0:16 - s_]), waits=lw + [last],
                        inc="dve")
                    t2 = P.dve(lambda e, d_=d_, b_=b_, s_=s_, cur=cur, nxt=nxt:
                               e.tensor_mul(
                        v4(nxt[d_])[:, :, :, s_:16],
                        v4(cur[b_])[:, :, :, s_:16],
                        v4(cur[5])[:, :, :, 0:16 - s_]), waits=lw, inc="dve")
                    t3 = P.dve(lambda e, d_=d_, s_=s_, nxt=nxt: e.tensor_add(
                        v4(nxt[d_])[:, :, :, s_:16],
                        v4(nxt[d_])[:, :, :, s_:16],
                        sq4[:, :, :, s_:16]), waits=[t1, t2], inc="dve")
                    last = P.dve(lambda e, d_=d_, s_=s_, cur=cur, nxt=nxt:
                                 e.tensor_add(
                        v4(nxt[d_])[:, :, :, s_:16],
                        v4(nxt[d_])[:, :, :, s_:16],
                        v4(cur[d_])[:, :, :, s_:16]), waits=[t3], inc="dve")
                t_lvl = last
                cur, nxt = nxt, cur
                yield
            fin = cur
            oth = nxt

            vv = lambda t_: t_[:].rearrange("p (g t) -> p g t", t=128)
            vx9 = vx0[:].rearrange("p (g n) -> p g n", n=9)
            vy9 = vy0[:].rearrange("p (g n) -> p g n", n=9)
            csc = lambda a, b: cscr[:, a:b].rearrange("p (g o) -> p g o", o=1)
            t_car = t_lvl
            for s8 in range(8):
                tcol = s8 * 16 + 15
                e1 = P.dve(lambda e, s8=s8, tcol=tcol, fin=fin: e.tensor_mul(
                    csc(0, 4),
                    vv(fin[0])[:, :, tcol:tcol + 1], vx9[:, :, s8:s8 + 1]),
                    waits=[t_car], inc="dve")
                e2 = P.dve(lambda e, s8=s8, tcol=tcol, fin=fin: e.tensor_mul(
                    csc(4, 8),
                    vv(fin[1])[:, :, tcol:tcol + 1], vy9[:, :, s8:s8 + 1]),
                    waits=[e1], inc="dve")
                e3 = P.dve(lambda e: e.tensor_add(cscr[:, 0:4], cscr[:, 0:4],
                                                  cscr[:, 4:8]),
                           waits=[e1, e2], inc="dve")
                f1 = P.dve(lambda e, s8=s8, tcol=tcol, fin=fin: e.tensor_mul(
                    csc(8, 12),
                    vv(fin[2])[:, :, tcol:tcol + 1], vx9[:, :, s8:s8 + 1]),
                    waits=[e3], inc="dve")
                f2 = P.dve(lambda e, s8=s8, tcol=tcol, fin=fin: e.tensor_mul(
                    csc(12, 16),
                    vv(fin[3])[:, :, tcol:tcol + 1], vy9[:, :, s8:s8 + 1]),
                    waits=[f1], inc="dve")
                f3 = P.dve(lambda e: e.tensor_add(cscr[:, 8:12], cscr[:, 8:12],
                                                  cscr[:, 12:16]),
                           waits=[f1, f2], inc="dve")
                e4 = P.dve(lambda e, s8=s8, tcol=tcol, fin=fin: e.tensor_add(
                    vx9[:, :, s8 + 1:s8 + 2],
                    csc(0, 4),
                    vv(fin[4])[:, :, tcol:tcol + 1]), waits=[e3, f3],
                    inc="dve")
                f4 = P.dve(lambda e, s8=s8, tcol=tcol, fin=fin: e.tensor_add(
                    vy9[:, :, s8 + 1:s8 + 2],
                    csc(8, 12),
                    vv(fin[5])[:, :, tcol:tcol + 1]), waits=[f3, e4],
                    inc="dve")
                t_car = f4
                if s8 % 3 == 2:
                    yield

            v4f = v4
            vx0v = vx9[:, :, 0:8].unsqueeze(3).broadcast_to((128, 4, 8, 16))
            vy0v = vy9[:, :, 0:8].unsqueeze(3).broadcast_to((128, 4, 8, 16))
            a1 = P.dve(lambda e, fin=fin: e.tensor_mul(
                v4f(oth[2]), v4f(fin[0]), vx0v), waits=[t_car], inc="dve")
            a2 = P.dve(lambda e, fin=fin: e.tensor_mul(
                v4f(oth[3]), v4f(fin[1]), vy0v), waits=[a1], inc="dve")
            a3 = P.dve(lambda e: e.tensor_add(v4f(oth[2]), v4f(oth[2]),
                                              v4f(oth[3])),
                       waits=[a1, a2], inc="dve")
            a4 = P.dve(lambda e, fin=fin: e.tensor_add(
                v4f(oth[0]), v4f(oth[2]), v4f(fin[4])), waits=[a3], inc="dve")
            yield
            b1 = P.dve(lambda e, fin=fin: e.tensor_mul(
                v4f(oth[4]), v4f(fin[2]), vx0v), waits=[a4], inc="dve")
            b2 = P.dve(lambda e, fin=fin: e.tensor_mul(
                v4f(oth[5]), v4f(fin[3]), vy0v), waits=[b1], inc="dve")
            b3 = P.dve(lambda e: e.tensor_add(v4f(oth[4]), v4f(oth[4]),
                                              v4f(oth[5])),
                       waits=[b1, b2], inc="dve")
            b4 = P.dve(lambda e, fin=fin: e.tensor_add(
                v4f(oth[1]), v4f(oth[4]), v4f(fin[5])), waits=[b3], inc="dve")
            cx = P.dve(lambda e, c=c: e.tensor_copy(
                vecbf[0][:].rearrange("p (g t) -> p g t", t=512)
                [:, :, c * 128:(c + 1) * 128], vv(oth[0])),
                waits=[a4, prev_join if c == 0 else None], inc="dve")
            cy = P.dve(lambda e, c=c: e.tensor_copy(
                vecbf[1][:].rearrange("p (g t) -> p g t", t=512)
                [:, :, c * 128:(c + 1) * 128], vv(oth[1])),
                waits=[b4, prev_join if c == 0 else None], inc="dve")
            cc1 = P.dve(lambda e: e.tensor_copy(vx9[:, :, 0:1], vx9[:, :, 8:9]),
                        waits=[cx, b4], inc="dve")
            cc2 = P.dve(lambda e: e.tensor_copy(vy9[:, :, 0:1], vy9[:, :, 8:9]),
                        waits=[cy, cc1], inc="dve")
            ev[("vecdone", ci)] = cc2
            if ci == 3:
                dbg_store("vecx", vecbf[0][:], [cc2])
                dbg_store("vecy", vecbf[1][:], [cc2])

        vec_gens = {c: vec_gen(c) for c in range(4 if L0CUT >= 2 else 0)}

        def vdrive(c, n):
            g = vec_gens.get(c)
            if g is None:
                return
            for _ in range(n):
                try:
                    next(g)
                except StopIteration:
                    vec_gens[c] = None
                    return

        if L0CUT < 4:
            for c in range(4 if L0CUT >= 2 else 0):
                vdrive(c, 99)
        if L0CUT < 3:
            for c in range(4):
                ci = sup * 4 + c
                tst = P.sp(lambda e, ci=ci, c=c: e.dma_start(
                    x_out[ci * 128:(ci + 1) * 128, :], xc[c % 2][:]),
                    waits=[P.tok("pe"), P.tok("dve"),
                           P.tok("act")],
                    stream=f"st{ci % 2}")
                ev[("store", ci)] = tst
                ev[("storedma", ci)] = tst
            ev[("join", sup)] = join(sup)
            continue
        wconv = [ev[("qkvtr_prev",)], t_cv, t_sbias,
                 ] + ev.get(("convsilu", sup - 1), [])
        silu_toks = []
        for cj in range(3):
            wv = win[cj][:].rearrange("p (ch t) -> p ch t", t=515)
            for c8 in range(8):
                ab = c8 % 2
                tprev = silu_toks[-2] if len(silu_toks) >= 2 else None
                t_ = None
                for j in range(4):
                    if j == 0:
                        t_ = P.dve(lambda e, cj=cj, c8=c8, ab=ab, wv=wv:
                                   e.tensor_scalar(
                            acc[ab][:], wv[:, c8, 0:512],
                            wcv[:, cj * 32 + c8 * 4: cj * 32 + c8 * 4 + 1],
                            None, ALU.mult), waits=wconv + [tprev], inc="dve")
                    else:
                        t_ = P.dve(lambda e, cj=cj, c8=c8, ab=ab, j=j, wv=wv:
                                   e.scalar_tensor_tensor(
                            acc[ab][:], wv[:, c8, j:j + 512],
                            wcv[:, cj * 32 + c8 * 4 + j:
                                 cj * 32 + c8 * 4 + j + 1],
                            acc[ab][:], ALU.mult, ALU.add), waits=[t_],
                            inc="dve")
                t_tl = P.dve(lambda e, cj=cj, c8=c8, wv=wv: e.tensor_copy(
                    wv[:, c8, 0:3], wv[:, c8, 512:515]), waits=[t_], inc="dve")
                tsl = P.act(lambda e, cj=cj, c8=c8, ab=ab, wv=wv: e.activation(
                    wv[:, c8, 3:515], acc[ab][:], AF.Silu,
                    bias=sbias[:, cj * 8 + c8: cj * 8 + c8 + 1]),
                    waits=[t_, t_tl], inc="act")
                silu_toks.append(tsl)
        t_conv = silu_toks[-1]
        ev[("convsilu", sup)] = silu_toks[-2:]
        if sup == 0:
            dbg_store("win_q", win[0][:], [t_conv])
            dbg_store("win_k", win[1][:], [t_conv])
            dbg_store("upd", upd4[:], [t_conv])
            dbg_store("ang", ang4[:], [t_conv])

        if L0CUT < 4:
            for c in range(4):
                ci = sup * 4 + c
                tst = P.sp(lambda e, ci=ci, c=c: e.dma_start(
                    x_out[ci * 128:(ci + 1) * 128, :], xc[c % 2][:]),
                    waits=[P.tok("pe"), P.tok("dve"),
                           P.tok("act")],
                    stream=f"st{ci % 2}")
                ev[("store", ci)] = tst
                ev[("storedma", ci)] = tst
            ev[("join", sup)] = join(sup)
            continue
        for c in range(4):
            ci = sup * 4 + c
            qwv = win[0][:].rearrange("p (ch t) -> p ch t", t=515)
            kwv = win[1][:].rearrange("p (ch t) -> p ch t", t=515)
            vwv = win[2][:].rearrange("p (ch t) -> p ch t", t=515)
            tc0 = 3 + c * 128
            dprev = ev.get(("dn_done", ci - 1))
            wD = [t_conv, dprev]

            def cutstore(tokl):
                tst = P.sp(lambda e, ci=ci, c=c: e.dma_start(
                    x_out[ci * 128:(ci + 1) * 128, :], xc[c % 2][:]),
                    waits=tokl + [P.tok("pe"),
                                  P.tok("act"),
                                  P.tok("dve")],
                    stream=f"st{ci % 2}")
                ev[("store", ci)] = tst
                ev[("storedma", ci)] = tst
                ev[("dn_done", ci)] = P.tok("dve")

            l1 = P.dve(lambda e, c=c: e.tensor_scalar(
                small6[:], abe[:, c * 32:c * 32 + 16], 0.0, None, ALU.min),
                waits=wD, inc="dve")
            l2 = P.act(lambda e, c=c: e.activation(
                lna[:], abe[:, c * 32:c * 32 + 16], AF.Abs), waits=wD,
                inc="act")
            l3 = P.act(lambda e: e.activation(lna[:], lna[:], AF.Exp,
                                              scale=-1.0), waits=[l2],
                       inc="act")
            l4 = P.act(lambda e: e.activation(lna[:], lna[:], AF.Ln, bias=1.0),
                       waits=[l3], inc="act")
            l5 = P.dve(lambda e: e.tensor_sub(lna[:], small6[:], lna[:]),
                       waits=[l1, l4], inc="dve")
            t_beta = P.act(lambda e, c=c: e.activation(
                beta[:], abe[:, c * 32 + 16:c * 32 + 32], AF.Sigmoid),
                waits=wD, inc="act")

            t_lt = P.pe(lambda e: e.transpose(pp[2][0:16, 0:128], lna[:],
                                              IDF[:]),
                        waits=[l5, dprev], inc="pe", mmk=(0, 128))
            t_le = P.dve(lambda e: e.tensor_copy(lnahm[:],
                                                 pp[2][0:16, 0:128]),
                         waits=[t_lt, t_hm0], inc="dve")
            t_cl = P.dve(lambda e: e.tensor_tensor_scan(
                cl16[:], ONESQ[0:16, :], lnahm[:], 0.0, ALU.mult,
                ALU.add), waits=[t_le, toks["onesq_ready"]], inc="dve")
            t_clm = P.dve(lambda e: e.tensor_sub(clm16[:], cl16[:],
                                                 lnahm[:]),
                          waits=[t_cl], inc="dve")
            if L0CUT == 40:
                cutstore([t_clm, t_beta])
                continue

            t_qs = P.dve(lambda e, c=c, qwv=qwv, tc0=tc0: e.tensor_mul(
                sqs[:].rearrange("p (ch t) -> p ch t", t=128),
                qwv[:, :, tc0:tc0 + 128], qwv[:, :, tc0:tc0 + 128]),
                waits=wD, inc="dve")
            for i8 in range(8):
                P.pe(lambda e, i8=i8: e.matmul(
                    pp[4][0:16, 0:128], BLK16[:, i8 * 16:(i8 + 1) * 16],
                    sqs[:, i8 * 128:(i8 + 1) * 128],
                    start=(i8 == 0), stop=(i8 == 7)),
                    waits=([t_qs, toks["blk16_ready"], dprev]
                           if i8 == 0 else []),
                    inc=("pe" if i8 == 7 else None), mmk=(0, 128))
            t_qsm = P.tok("pe")
            h3 = P.dve(lambda e: e.tensor_scalar(
                hm[64:80, :], pp[4][0:16, 0:128], 1e-12, None, ALU.add),
                waits=[t_qsm], inc="dve")
            h3b = P.dve(lambda e: e.reciprocal(hm[64:80, :], hm[64:80, :]),
                        waits=[h3], inc="dve")
            h3c = P.act(lambda e: e.activation(hm[64:80, :], hm[64:80, :],
                                               AF.Sqrt), waits=[h3b],
                        inc="act")
            t_ks = P.dve(lambda e, c=c, kwv=kwv, tc0=tc0: e.tensor_mul(
                sqs[:].rearrange("p (ch t) -> p ch t", t=128),
                kwv[:, :, tc0:tc0 + 128], kwv[:, :, tc0:tc0 + 128]),
                waits=[h3], inc="dve")
            for i8 in range(8):
                P.pe(lambda e, i8=i8: e.matmul(
                    pp[4][32:48, 0:128], BLK16[:, i8 * 16:(i8 + 1) * 16],
                    sqs[:, i8 * 128:(i8 + 1) * 128],
                    start=(i8 == 0), stop=(i8 == 7)),
                    waits=([t_ks] if i8 == 0 else []),
                    inc=("pe" if i8 == 7 else None), mmk=(0, 128))
            t_ksm = P.tok("pe")
            h2 = P.dve(lambda e: e.tensor_scalar(
                hm[32:48, :], pp[4][32:48, 0:128], 1e-12, None, ALU.add),
                waits=[t_ksm], inc="dve")
            h2b = P.dve(lambda e: e.reciprocal(hm[32:48, :], hm[32:48, :]),
                        waits=[h2], inc="dve")
            if L0CUT == 41:
                cutstore([h2b])
                continue
            t_skhm = P.act(lambda e: e.activation(skhm[:], hm[32:48, :],
                                                  AF.Sqrt), waits=[h2b],
                           inc="act")
            h1 = P.dve(lambda e: e.tensor_copy(hm[0:16, :], cl16[:]),
                       waits=[t_cl, h2b, h3c], inc="dve")
            t_ht = P.pe(lambda e: e.transpose(pp[2][:, 0:128], hm[:], IDF[:]),
                        waits=[h1, t_le], inc="pe", mmk=(0, 128))
            t_he = P.dve(lambda e: e.tensor_copy(tok16[:], pp[2][:, 0:128]),
                         waits=[t_ht], inc="dve")
            t_sk = P.act(lambda e: e.activation(sktok[:], tok16[:, 32:48],
                                                AF.Sqrt), waits=[t_he],
                         inc="act")
            if L0CUT == 42:
                cutstore([t_sk])
                continue
            t_fA = P.act(lambda e: e.activation(fA[:], tok16[:, 0:16], AF.Exp),
                         waits=[t_he], inc="act")
            t_f1 = P.dve(lambda e: e.tensor_sub(fash[:], tok16[:, 0:16],
                                                lna[:]),
                         waits=[t_he, l5], inc="dve")
            t_fash = P.act(lambda e: e.activation(fash[:], fash[:], AF.Exp),
                           waits=[t_f1], inc="act")
            t_f2 = P.dve(lambda e: e.tensor_mul(fash[:], fash[:], sktok[:]),
                         waits=[t_fash, t_sk], inc="dve")
            t_c127 = P.pe(lambda e: e.transpose(pp[2][0:1, 0:16],
                                                cl16[:, 127:128],
                                                IDF[0:16, 0:16]),
                          waits=[t_cl, t_he], inc="pe", mmk=(0, 16))
            t_c127e = P.dve(lambda e: e.tensor_copy(acrow[:],
                                                    pp[2][0:1, 0:16]),
                            waits=[t_c127], inc="dve")
            t_acm = P.pe(lambda e: e.matmul(
                pp[2][:, 0:16], ONESR[0:1, 0:128], acrow[:], start=True,
                stop=True), waits=[t_c127e], inc="pe", mmk=(0, 1))
            t_fs1 = P.dve(lambda e: e.tensor_sub(fstate[:], pp[2][:, 0:16],
                                                 tok16[:, 0:16]),
                          waits=[t_acm], inc="dve")
            t_fs2 = P.act(lambda e: e.activation(fstate[:], fstate[:], AF.Exp),
                          waits=[t_fs1], inc="act")
            t_fs = P.dve(lambda e: e.tensor_mul(fstate[:], fstate[:],
                                                sktok[:]),
                         waits=[t_fs2, t_sk], inc="dve")
            t_ace = P.act(lambda e: e.activation(acrow[:], acrow[:], AF.Exp),
                          waits=[t_fs1], inc="act")
            t_acb = P.pe(lambda e: e.matmul(
                pp[2][:, 16:32], ONESR[0:1, 0:128], acrow[:], start=True,
                stop=True), waits=[t_ace, t_fs1], inc="pe", mmk=(0, 1))
            t_acbe = P.dve(lambda e: e.tensor_copy(acbc[:],
                                                   pp[2][:, 16:32]),
                           waits=[t_acb], inc="dve")

            if L0CUT == 43:
                cutstore([t_acbe, t_fs])
                continue
            vk_toks = []
            for wvv, dst in ((vwv, vT), (kwv, kT)):
                for c8 in range(8):
                    bk, war = rot_vk.alloc()
                    view = bk[:, 0:512].bitcast(BF16)[:, 0:128]
                    tt = P.pe(lambda e, wvv=wvv, c8=c8, tc0=tc0, view=view:
                              e.transpose(
                        view, wvv[:, c8, tc0:tc0 + 128], IDB[:]),
                        waits=[war, t_conv], inc="pe", mmk=(0, 128))
                    tev = P.dve(lambda e, dst=dst, c8=c8, view=view:
                                e.tensor_copy(
                        dst[:, c8 * 128:(c8 + 1) * 128], view),
                        waits=[tt, dprev, ev.get(("dbg_last",))], inc="dve")
                    rot_vk.set(tev)
                    vk_toks.append(tev)
            t_vk = vk_toks[-1]

            if L0CUT == 44:
                cutstore([t_vk])
                continue
            wave_toks = []
            for wv in range(2):
                for pi, hh in enumerate([0, 2, 4, 6, 1, 3, 5, 7]):
                    h = wv * 8 + hh
                    ht = h // 2
                    hb = (h % 2) * 64
                    slot = h % 8
                    P.pe(lambda e, h=h, ht=ht, hb=hb, slot=slot,
                         kwv=kwv, tc0=tc0: e.matmul(
                        pp[7][:, slot * 64:(slot + 1) * 64],
                        kwv[hb:hb + 64, ht, tc0:tc0 + 128],
                        S0b[hsl(h), csl(h)], start=True, stop=True),
                        waits=([t_conv, t_s0b, dprev, t_vk,
                                ev.get(("ctrl_last", sup)),
                                ev.get(("dbg_last",))]
                               if (wv == 0 and pi == 0)
                               else ([wave_toks[0]] if pi == 0 else [])),
                        inc=("pe" if pi == 7 else None), mmk=(hb, 64))
                t_pw = P.tok("pe")
                rw = P.dve(lambda e, wv=wv: e.tensor_tensor(
                    rscr[:, wv * 512:(wv + 1) * 512]
                    .rearrange("p (h j) -> p h j", j=64),
                    pp[7][:, 0:512].rearrange("p (h j) -> p h j", j=64),
                    fash[:, wv * 8:(wv + 1) * 8].unsqueeze(2)
                    .broadcast_to((128, 8, 64)),
                    ALU.mult), waits=[t_pw, t_f2], inc="dve")
                wave_toks.append(rw)
            r4 = P.dve(lambda e: e.tensor_sub(Rt[:], vT[:], rscr[:]),
                       waits=wave_toks + [t_vk], inc="dve")
            bank_tok[7] = r4
            if ci == 0:
                dbg_store("Rt0", Rt[:], [r4])
                dbg_store("cl0", cl16[:], [t_cl, r4])
                dbg_store("tok160", tok16[:], [t_he, r4])

            if L0CUT < 5:
                tst = P.sp(lambda e, ci=ci, c=c: e.dma_start(
                    x_out[ci * 128:(ci + 1) * 128, :], xc[c % 2][:]),
                    waits=[r4, P.tok("pe"),
                           P.tok("act")],
                    stream=f"st{ci % 2}")
                ev[("store", ci)] = tst
                ev[("storedma", ci)] = tst
                ev[("dn_done", ci)] = r4
                continue
            vdrive(c, 4)
            rd_prev = [None, None]
            st_prev = [None, None]
            hprevP = [[r4], [r4]]
            for pr in range(8):
                hs = [2 * pr, 2 * pr + 1]
                QcP = [qwv[(h % 2) * 64:(h % 2) * 64 + 64, h // 2,
                           tc0:tc0 + 128] for h in hs]
                KcP = [kwv[(h % 2) * 64:(h % 2) * 64 + 64, h // 2,
                           tc0:tc0 + 128] for h in hs]
                t_bl = {}
                t_bge = {}
                t_bsk2 = {}
                for q in range(2):
                    h = hs[q]
                    t_bl[q] = P.pe(lambda e, h=h, q=q: e.matmul(
                        pp[2 + q][:, 0:128], SELH[:, h * 128:(h + 1) * 128],
                        cl16[:], start=True, stop=True),
                        waits=[t_cl, toks["selh_ready"], t_acbe,
                               ev.get(("dbg_last",))] + hprevP[q],
                        inc="pe", mmk=(0, 16))
                    t_bge[q] = P.pe(lambda e, h=h, q=q: e.matmul(
                        pp[2 + q][:, 128:256], SELH[:, h * 128:(h + 1) * 128],
                        clm16[:], start=True, stop=True), waits=[t_clm],
                        inc="pe", mmk=(0, 16))
                    t_bsk2[q] = P.pe(lambda e, h=h, q=q: e.matmul(
                        pp[4 + q][:, 128:256], SELH[:, h * 128:(h + 1) * 128],
                        skhm[:], start=True, stop=True),
                        waits=[t_skhm], inc="pe", mmk=(0, 16))
                t_kh = {}
                s_lam = {}
                s_gam = {}
                t_exp = {}
                for q in range(2):
                    h = hs[q]
                    hb = (h % 2) * 64
                    t_kh[q] = P.dve(lambda e, q=q, hb=hb, Kc=KcP[q]:
                                    e.tensor_tensor(
                        khat[hb:hb + 64, :], Kc, pp[4 + q][hb:hb + 64, 128:256],
                        ALU.mult), waits=[t_bsk2[q]], inc="dve")
                    s_lam[q] = P.dve(lambda e, h=h, q=q:
                                     e.scalar_tensor_tensor(
                        lgscrP[q][:, 0:128], pp[2 + q][:, 0:128],
                        tok16[:, h:h + 1], MSKI[:], ALU.subtract, ALU.add),
                        waits=[t_bl[q], t_he, toks["mask_incl_ready"]],
                        inc="dve")
                    s_gam[q] = P.dve(lambda e, h=h, q=q:
                                     e.scalar_tensor_tensor(
                        lgscrP[q][:, 128:256], pp[2 + q][:, 128:256],
                        tok16[:, h:h + 1], MSKS[:], ALU.subtract, ALU.add),
                        waits=[t_bge[q], toks["mask_strict_ready"]],
                        inc="dve")
                for q in range(2):
                    t_exp[q] = P.act(lambda e, q=q: e.activation(
                        lgscrP[q][:], lgscrP[q][:], AF.Exp),
                        waits=[s_lam[q], s_gam[q]], inc="act")
                t_G = {}
                t_Z = {}
                t_qm = {}
                for q in range(2):
                    h = hs[q]
                    hb = (h % 2) * 64
                    t_G[q] = P.pe(lambda e, q=q, hb=hb: e.matmul(
                        pp[2 + q][:, 256:384], khat[hb:hb + 64, :],
                        khat[hb:hb + 64, :], start=True, stop=True),
                        waits=[s_lam[q], s_gam[q], t_kh[q]], inc="pe",
                        mmk=(hb, 64))
                    t_Z[q] = P.pe(lambda e, q=q, Kc=KcP[q], Qc=QcP[q]:
                                  e.matmul(
                        pp[2 + q][:, 384:512], Kc, Qc, start=True, stop=True),
                        waits=[], inc="pe", mmk=(hb, 64))
                    t_qm[q] = P.pe(lambda e, h=h, q=q, Qc=QcP[q]: e.matmul(
                        pp[4 + q][:, 256:320], Qc, S0b[hsl(h), csl(h)],
                        start=True, stop=True),
                        waits=[t_kh[q], rd_prev[q]], inc="pe",
                        mmk=(hb, 64))
                t_NT = {}
                t_lz = {}
                for q in range(2):
                    h = hs[q]
                    t_NT[q] = P.dve(lambda e, h=h, q=q:
                                    e.scalar_tensor_tensor(
                        NTp[q][:], pp[2 + q][:, 256:384], beta[:, h:h + 1],
                        lgscrP[q][:, 128:256], ALU.mult, ALU.mult),
                        waits=[t_Z[q], t_exp[q], t_beta], inc="dve")
                    t_lz[q] = P.dve(lambda e, h=h, q=q:
                                    e.scalar_tensor_tensor(
                        lamzP[q][:], pp[2 + q][:, 384:512],
                        sktok[:, h:h + 1], lgscrP[q][:, 0:128],
                        ALU.mult, ALU.mult),
                        waits=[t_Z[q], t_exp[q], t_sk], inc="dve")
                vdrive(c, 1)
                t_Nt = {}
                t_N = {}
                t_y = {}
                for q in range(2):
                    t_Nt[q] = P.pe(lambda e, q=q: e.transpose(
                        pp[0 + q][:, 256:384], NTp[q][:], IDF[:]),
                        waits=[t_NT[q]], inc="pe", mmk=(0, 128))
                for q in range(2):
                    t_N[q] = P.act(lambda e, q=q: e.activation(
                        nkpP[q][0][:], pp[0 + q][:, 256:384], AF.Copy),
                        waits=[t_Nt[q]], inc="act")
                    t_y[q] = P.dve(lambda e, q=q, h=hs[q]: e.tensor_copy(
                        YtP[q][0][:], Rt[:, h * 64:(h + 1) * 64]),
                        waits=[r4], inc="dve")
                ycur = [0, 0]
                e_ev = {0: [t_N[0]], 1: [t_N[1]]}
                for lvl in range(7):
                    ma = {}
                    for q in range(2):
                        Bt_l = NTp[q] if lvl == 0 else nksP[q][lvl % 2]
                        ma[q] = P.pe(lambda e, q=q, Bt_l=Bt_l,
                                     yc_=ycur[q]: e.matmul(
                            pp[4 + q][:, 64:128], Bt_l[:], YtP[q][yc_][:],
                            start=True, stop=True),
                            waits=[t_y[q], t_NT[q], t_qm[q]] + e_ev[q],
                            inc="pe", mmk=(0, 128))
                    if lvl < 6:
                        for q in range(2):
                            A_l = nkpP[q][lvl % 2] if lvl else nkpP[q][0]
                            Bt_l = NTp[q] if lvl == 0 else nksP[q][lvl % 2]
                            nxt = (lvl + 1) % 2
                            m1 = P.pe(lambda e, q=q, A_l=A_l, Bt_l=Bt_l:
                                      e.matmul(
                                pp[0 + q][:, 0:128], Bt_l[:], A_l[:],
                                start=True, stop=True),
                                waits=e_ev[q], inc="pe", mmk=(0, 128))
                            m2 = P.pe(lambda e, q=q, A_l=A_l, Bt_l=Bt_l:
                                      e.matmul(
                                pp[0 + q][:, 128:256], A_l[:], Bt_l[:],
                                start=True, stop=True),
                                waits=[], inc="pe", mmk=(0, 128))
                            e1_ = P.act(lambda e, q=q, nxt=nxt: e.activation(
                                nkpP[q][nxt][:], pp[0 + q][:, 0:128],
                                AF.Copy), waits=[m2], inc="act")
                            e2_ = P.act(lambda e, q=q, nxt=nxt: e.activation(
                                nksP[q][nxt][:], pp[0 + q][:, 128:256],
                                AF.Copy), waits=[e1_], inc="act")
                            e_ev[q] = [e1_, e2_]
                    for q in range(2):
                        if lvl == 0:
                            t_y[q] = P.dve(lambda e, q=q, yc_=ycur[q]:
                                           e.tensor_sub(
                                YtP[q][1 - yc_][:], YtP[q][yc_][:],
                                pp[4 + q][:, 64:128]),
                                waits=[ma[q]], inc="dve")
                        else:
                            t_y[q] = P.dve(lambda e, q=q, yc_=ycur[q]:
                                           e.tensor_add(
                                YtP[q][1 - yc_][:], pp[4 + q][:, 64:128],
                                YtP[q][yc_][:]),
                                waits=[ma[q]], inc="dve")
                        ycur[q] = 1 - ycur[q]
                vdrive(c, 1)
                t_w = {}
                for q in range(2):
                    h = hs[q]
                    t_w[q] = P.dve(lambda e, h=h, q=q, yc_=ycur[q]:
                                   e.tensor_scalar(
                        Wt_[:, h * 64:(h + 1) * 64], YtP[q][yc_][:],
                        beta[:, h:h + 1], None, ALU.mult),
                        waits=[t_y[q], t_beta], inc="dve")
                    if dbg is not None and h == 1 and ci == 0:
                        dbg_store("Yfin", YtP[q][ycur[q]][:], [t_y[q]])
                        dbg_store("NT0", NTp[q][:], [t_NT[q]])
                        dbg_store("W0", Wt_[:, 64:128], [t_w[q]])
                t_rm = {}
                for q in range(2):
                    h = hs[q]
                    t_rm[q] = P.pe(lambda e, h=h, q=q: e.matmul(
                        pp[4 + q][:, 320:384], lamzP[q][:],
                        Wt_[:, h * 64:(h + 1) * 64],
                        start=True, stop=True),
                        waits=[t_lz[q], t_w[q]], inc="pe", mmk=(0, 128))
                for q in range(2):
                    h = hs[q]
                    c1 = P.dve(lambda e, h=h, q=q: e.tensor_scalar(
                        cscr1P[q][:], pp[4 + q][:, 256:320], fA[:, h:h + 1],
                        None, ALU.mult), waits=[t_rm[q], t_fA], inc="dve")
                    c2 = P.dve(lambda e, q=q: e.tensor_add(
                        cscr2P[q][:], cscr1P[q][:], pp[4 + q][:, 320:384]),
                        waits=[c1], inc="dve")
                    c3 = P.dve(lambda e, h=h, q=q: e.tensor_scalar(
                        reads[:, h * 64:(h + 1) * 64], cscr2P[q][:],
                        tok16[:, 64 + h:64 + h + 1], None, ALU.mult),
                        waits=[c2], inc="dve")
                    rd_prev[q] = c3
                t_kst = {}
                for q in range(2):
                    h = hs[q]
                    t_kst[q] = P.dve(lambda e, h=h, q=q: e.tensor_scalar(
                        kstP[q][:], kT[:, h * 64:(h + 1) * 64],
                        fstate[:, h:h + 1], None, ALU.mult),
                        waits=[t_vk, t_fs, rd_prev[q]], inc="dve")
                t_sm = {}
                for q in range(2):
                    h = hs[q]
                    hb = (h % 2) * 64
                    t_sm[q] = P.pe(lambda e, h=h, q=q, hb=hb: e.matmul(
                        pp[4 + q][hb:hb + 64, 0:64], kstP[q][:],
                        Wt_[:, h * 64:(h + 1) * 64],
                        start=True, stop=True,
                        tile_position=((0, hb) if hb else None)),
                        waits=[t_kst[q], t_w[q], st_prev[q]],
                        inc="pe", mmk=(0, 128, hb))
                for q in range(2):
                    h = hs[q]
                    hb = (h % 2) * 64
                    t_su = P.dve(lambda e, h=h, q=q, hb=hb:
                                 e.scalar_tensor_tensor(
                        S0[hsl(h), csl(h)], S0[hsl(h), csl(h)],
                        acbc[hsl(h), h:h + 1], pp[4 + q][hb:hb + 64, 0:64],
                        ALU.mult, ALU.add),
                        waits=[t_sm[q], t_acbe], inc="dve")
                    t_sb = P.dve(lambda e, h=h: e.tensor_copy(
                        S0b[hsl(h), csl(h)], S0[hsl(h), csl(h)]),
                        waits=[t_su], inc="dve")
                    st_prev[q] = t_sb
                    hprevP[q] = [t_sb]
                vdrive(c, 1)
            vdrive(c, 99)
            t_rdall = [rd_prev[0], rd_prev[1]]
            t_stall = [st_prev[0], st_prev[1]]
            if ci == 0:
                dbg_store("Wt0", Wt_[:], t_stall)
                dbg_store("reads0", reads[:], t_rdall)
                dbg_store("S0a", S0[:], t_stall)

            t_msq = P.act(lambda e: e.activation(sqs[:], reads[:], AF.Square),
                          waits=[t_rdall], inc="act")
            t_mrd = P.dve(lambda e: e.tensor_reduce(
                small6[:].unsqueeze(2),
                sqs[:].rearrange("p (h j) -> p h j", j=64), AX.X, ALU.add),
                waits=[t_msq], inc="dve")
            t_mr2 = P.dve(lambda e: e.tensor_scalar(
                small6[:], small6[:], 1.0 / 64.0, None, ALU.mult),
                waits=[t_mrd], inc="dve")
            t_mr3 = P.act(lambda e: e.activation(
                small6[:], small6[:], AF.Sqrt, bias=MISC[:, 1:2]),
                waits=[t_mr2, toks["misc_ready"]], inc="act")
            t_mr4 = P.dve(lambda e: e.reciprocal(small6[:], small6[:]),
                          waits=[t_mr3], inc="dve")
            t_m = P.dve(lambda e: e.tensor_tensor(
                mtile[:].rearrange("p (h j) -> p h j", j=64),
                reads[:].rearrange("p (h j) -> p h j", j=64),
                small6[:].unsqueeze(2).broadcast_to((128, 16, 64)), ALU.mult),
                waits=[t_mr4, t_rdall], inc="dve")
            mt_toks = []
            for t8 in range(8):
                bk, war = rot_mt.alloc()
                view = bk[:, 0:512].bitcast(BF16)[:, 0:128]
                tt = P.pe(lambda e, t8=t8, view=view: e.transpose(
                    view, mtile[:, t8 * 128:(t8 + 1) * 128], IDB[:]),
                    waits=[t_m, war], inc="pe", mmk=(0, 128))
                tev = P.dve(lambda e, t8=t8, view=view: e.tensor_copy(
                    mT[:, t8 * 128:(t8 + 1) * 128], view),
                    waits=[tt, ev.get(("epi_mT", ci - 1))], inc="dve")
                rot_mt.set(tev)
                mt_toks.append(tev)
            ev[("epi_mT", ci)] = mt_toks[-1]
            if ci == 0:
                dbg_store("m0", mtile[:], [t_m, mt_toks[-1]])

            vxq = P.act(lambda e, c=c: e.activation(
                sqs[:, 0:512].rearrange("p (g t) -> p g t", t=128),
                vecbf[0][:].rearrange("p (g t) -> p g t", t=512)
                [:, :, c * 128:(c + 1) * 128], AF.Square),
                waits=[ev[("vecdone", ci)], t_msq, t_mrd, t_m], inc="act")
            vyq = P.act(lambda e, c=c: e.activation(
                sqs[:, 512:1024].rearrange("p (g t) -> p g t", t=128),
                vecbf[1][:].rearrange("p (g t) -> p g t", t=512)
                [:, :, c * 128:(c + 1) * 128], AF.Square),
                waits=[vxq], inc="act")
            vm1 = P.pe(lambda e: e.matmul(
                pp[5][0:1, 0:512], aps["sb_ones_col"][:, 0:1],
                sqs[:, 0:512], start=True, stop=False),
                waits=[vxq, vyq, toks["ones_col_ready"], t_stall], inc=None,
                mmk=(0, 128))
            vm2 = P.pe(lambda e: e.matmul(
                pp[5][0:1, 0:512], aps["sb_ones_col"][:, 0:1],
                sqs[:, 512:1024], start=False, stop=True), inc="pe",
                mmk=(0, 128))
            t_vm = P.tok("pe")
            vr1 = P.dve(lambda e: e.tensor_reduce(
                hm[0:1, 0:128].unsqueeze(2),
                pp[5][0:1, 0:512].rearrange("o (g t) -> o t g", g=4),
                AX.X, ALU.add), waits=[t_vm], inc="dve")
            vr2 = P.dve(lambda e: e.tensor_scalar(
                hm[0:1, 0:128], hm[0:1, 0:128], 1.0 / 1024.0, None, ALU.mult),
                waits=[vr1], inc="dve")
            vr3 = P.act(lambda e: e.activation(
                hm[0:1, 0:128], hm[0:1, 0:128], AF.Sqrt, bias=MISC[0:1, 1:2]),
                waits=[vr2], inc="act")
            vr4 = P.dve(lambda e: e.reciprocal(hm[0:1, 0:128],
                                               hm[0:1, 0:128]),
                        waits=[vr3], inc="dve")
            vr5 = P.pe(lambda e: e.transpose(pp[4][:, 0:1],
                                             hm[0:1, 0:128], IDF[0:1, 0:1]),
                       waits=[vr4, t_vm, t_stall], inc="pe", mmk=(0, 1))
            vr6 = P.dve(lambda e: e.tensor_copy(rstdv[:], pp[4][:, 0:1]),
                        waits=[vr5], inc="dve")

            mm_toks = {}
            for g in range(2):
                for part, bank0 in (("m", 0), ("v", 2)):
                    bank = bank0 + g
                    w0 = [mt_toks[-1], t_wo, ev.get(("vecdone", ci)),
                          ev.get(("epi_ev", ci - 1, part, g)),
                          ev.get(("ctrl_last", sup)), t_stall]
                    for kk in range(8):
                        if part == "m":
                            lhs = lambda e, kk=kk: mT[:, kk * 128:(kk + 1) * 128]
                            wok = kk
                            is_last = False
                        else:
                            sig = kk // 4
                            pg = kk % 4
                            lhs = (lambda e, sig=sig, pg=pg, c=c:
                                   vecbf[sig][:].rearrange(
                                       "p (g t) -> p g t", t=512)
                                   [:, pg, c * 128:(c + 1) * 128])
                            wok = 8 + kk
                            is_last = (kk == 7)
                        P.pe(lambda e, lhs=lhs, wok=wok, g=g, bank=bank,
                             kk=kk, is_last=is_last: e.matmul(
                            pp[bank][:, 0:512], lhs(e),
                            wo[:, wok * 1024 + g * 512:
                               wok * 1024 + (g + 1) * 512],
                            start=(kk == 0), stop=is_last),
                            waits=w0 if kk == 0 else [],
                            inc=("pe" if is_last else None), mmk=(0, 128))
                    if part == "m":
                        P.pe(lambda e, g=g, bank=bank: e.matmul(
                            pp[bank][:, 0:512], ONESRB[0:1, 0:128],
                            bout[0:1, g * 512:(g + 1) * 512], start=False,
                            stop=True), waits=[t_bo], inc="pe", mmk=(0, 1))
                    mm_toks[(part, g)] = P.tok("pe")

            tld_og = P.sp(lambda e, ci=ci: e.dma_start(
                ogst[:], ogbuf[ci * 128:(ci + 1) * 128, :]),
                waits=[ev[("ogstore", sup, 3)],
                       ev.get(("ogate", ci - 1))], stream="og")
            ev[("ogload", sup, c)] = tld_og
            t_xr = P.sp(lambda e, ci=ci: e.dma_start(
                sqs[:], x_in[ci * 128:(ci + 1) * 128, :]),
                waits=[t_vm, vyq], stream=f"xr{ci % 2}")
            st_evs = []
            o4prev = [None]
            for g in range(2):
                o1 = P.dve(lambda e, g=g: e.tensor_scalar(
                    ocomb[:, g * 512:(g + 1) * 512], pp[2 + g][:, 0:512],
                    rstdv[:, 0:1], None, ALU.mult),
                    waits=[mm_toks[("v", g)], vr6], inc="dve")
                o2 = P.dve(lambda e, g=g: e.tensor_add(
                    ocomb[:, g * 512:(g + 1) * 512],
                    ocomb[:, g * 512:(g + 1) * 512], pp[g][:, 0:512]),
                    waits=[o1, mm_toks[("m", g)]], inc="dve")
                ev[("epi_ev", ci, "v", g)] = o1
                ev[("epi_ev", ci, "m", g)] = o2
                o3 = P.act(lambda e, g=g: e.activation(
                    gate[:], ogst[:, g * 512:(g + 1) * 512], AF.Sigmoid),
                    waits=[tld_og, vyq, o4prev[0]], inc="act")
                o4 = P.dve(lambda e, g=g: e.tensor_mul(
                    ocomb[:, g * 512:(g + 1) * 512],
                    ocomb[:, g * 512:(g + 1) * 512], gate[:]),
                    waits=[o2, o3], inc="dve")
                o4prev[0] = o4
                o5 = P.dve(lambda e, g=g, ci=ci: e.tensor_add(
                    yc[ci % 2][:, g * 512:(g + 1) * 512],
                    ocomb[:, g * 512:(g + 1) * 512],
                    sqs[:, g * 512:(g + 1) * 512]),
                    waits=[o4, t_xr, ev.get(("storedma", ci - 2))], inc="dve")
                st_evs.append(o5)
            ev[("ogate", ci)] = o3
            t_store = P.sp(lambda e, ci=ci: e.dma_start(
                x_out[ci * 128:(ci + 1) * 128, :], yc[ci % 2][:]),
                waits=st_evs, stream=f"st{ci % 2}")
            ev[("storedma", ci)] = t_store
            ev[("store", ci)] = t_store
            ev[("dn_done", ci)] = st_evs[-1]
        ev[("join", sup)] = join(sup)

    fin = [P.tok("dma_st0"), P.tok("dma_st1")]
    fin = [t for t in fin if t[1] > 0]
    if fin:
        P.emit("sp", lambda e: e.nop(), waits=fin)
    return fin


def build_program(S_len, stages, io_kind="external", dbg_spec=None):
    nc = bass.Bass("TRN2", target_bir_lowering=False)
    aps = {}
    for name, shape in INPUT_SHAPES.items():
        aps[name] = nc.dram_tensor(name, list(shape), F32,
                                   kind="ExternalInput").ap()
    for name, shape in CONST_SHAPES.items():
        dt = BF16 if name.endswith("_b") else F32
        aps[name] = nc.dram_tensor(name, list(shape), dt,
                                   kind="ExternalInput").ap()
    aps["x0"] = nc.dram_tensor("x0", [S_len, D], F32, kind="ExternalInput").ap()
    aps["xout"] = nc.dram_tensor("xout", [S_len, D], F32,
                                 kind="ExternalOutput").ap()
    dbg = None
    if dbg_spec:
        dbg = {}
        for name, shape, dt in dbg_spec:
            dbg[name] = nc.dram_tensor("dbg_" + name, list(shape), dt,
                                       kind="ExternalOutput").ap()
    n_inter = len(stages) - 1
    inter = [nc.dram_tensor(f"xbuf{i}", [S_len, D], F32, kind="Internal").ap()
             for i in range(n_inter)]

    with contextlib.ExitStack() as st:
        P = Prog(nc)
        aps["psum"] = [st.enter_context(nc.psum_tensor(f"pp{i}", [128, 512], F32))
                       for i in range(8)]
        aps["arena"] = Arena(nc, st, 49500)
        toks = emit_prologue(P, st, aps)
        toks["start"] = [toks["consts_ready"]]
        bufs = ["x0"] + [f"xbuf{i}" for i in range(n_inter)] + ["xout"]
        buf_aps = [aps["x0"]] + inter + [aps["xout"]]
        aps["dbg"] = dbg
        for si, stage in enumerate(stages):
            aps["x_in"] = buf_aps[si]
            aps["x_out"] = buf_aps[si + 1]
            if stage == "l0_attn":
                fin = STAGE_EMITTERS[stage](P, st, aps, toks, S_len, dbg=dbg)
            else:
                fin = STAGE_EMITTERS[stage](P, st, aps, toks, S_len)
            toks["start"] = fin
        P.build(st)
    return nc


INPUT_SHAPES = {
    "L0_attn_W_ctrl": (1024, 6688), "L0_attn_b_ctrl": (6688,),
    "L0_attn_W_block": (512, 2, 2), "L0_attn_qconv_w": (1024, 4),
    "L0_attn_kconv_w": (1024, 4), "L0_attn_vconv_w": (1024, 4),
    "L0_attn_rms_m_w": (64,), "L0_attn_rms_v_w": (1024,),
    "L0_attn_W_out": (2048, 1024), "L0_attn_b_out": (1024,),
    "L0_ffn_router_w": (8, 1024), "L0_ffn_router_b": (8,),
    "L0_ffn_w1": (8, 1024, 4096), "L0_ffn_b1": (8, 4096),
    "L0_ffn_w2": (8, 4096, 1024), "L0_ffn_b2": (8, 1024),
    "L0_ffn_gamma": (1024,), "L0_ffn_beta": (1024,),
    "L1_attn_A": (1024,), "L1_attn_B": (1024, 1024),
    "L1_attn_W": (1024, 1024), "L1_attn_b": (1024,),
    "L1_ffn_kernel": (1024, 1024), "L1_ffn_bias": (1024,),
    "L1_ffn_a": (1024, 16), "L1_ffn_b": (16, 1024),
}

STAGE_EMITTERS = {
    "l0_attn": emit_l0_attn,
    "l1_ffn": emit_l1_ffn,
    "l1_attn": emit_l1_attn,
    "l0_ffn": emit_l0_ffn,
}


def kernel(**inputs):
    S_len = inputs["L0_attn_hidden"].shape[1]
    nc = build_program(S_len, ["l0_attn", "l0_ffn", "l1_attn", "l1_ffn"])
    consts = host_constants()
    in_maps = []
    for core in range(NCORES):
        m = {k: np.ascontiguousarray(inputs[k]) for k in INPUT_SHAPES}
        m.update(consts)
        m["x0"] = np.ascontiguousarray(inputs["L0_attn_hidden"][core])
        in_maps.append(m)
    res = run_bass_kernel_spmd(nc, in_maps, core_ids=list(range(NCORES)))
    out = np.stack([res.results[i]["xout"] for i in range(NCORES)], axis=0)
    return out.astype(np.float32)


# version 06
# speedup vs baseline: 1.2084x; 1.2084x over previous
import contextlib
import numpy as np
import concourse.bass as bass
import concourse.mybir as mybir
from concourse.bass_utils import run_bass_kernel_spmd

F32 = mybir.dt.float32
BF16 = mybir.dt.bfloat16
AF = mybir.ActivationFunctionType
ALU = mybir.AluOpType
AX = mybir.AxisListType

D = 1024
NCORES = 8

MAGIC = float(np.float32(1.5 * 2**23))
INV2PI = float(np.float32(1.0 / (2 * np.pi)))
TWOPI = float(np.float32(2 * np.pi))
HALFPI = float(np.float32(np.pi / 2))

ENGS = ("sp", "pe", "dve", "act", "pool")


class Prog:

    ROLL_CAP = 50000

    def __init__(self, nc):
        self.nc = nc
        self.steps = {e: [] for e in ENGS}
        self.counts = {}
        self.sem_names = []
        self.physmap = {}
        self.rollidx = {}

    def sem(self, name):
        if name not in self.counts:
            self.counts[name] = 0
            self.sem_names.append(name)
        return name

    def _phys(self, base):
        return self.physmap.get(base, base)

    def tok(self, base):
        p = self._phys(base)
        return (p, self.counts.get(p, 0))

    def emit(self, eng, fn, waits=(), inc=None, inc_n=1, drain=False,
             mmk=None):
        flat = []
        for w in waits:
            if w is None:
                continue
            if isinstance(w, list):
                flat.extend(x for x in w if x is not None)
            else:
                flat.append(w)
        waits = flat
        if inc is not None:
            phys = self._phys(inc)
            self.sem(phys)
            if self.counts[phys] + inc_n > self.ROLL_CAP:
                self.rollidx[inc] = self.rollidx.get(inc, 0) + 1
                phys = f"{inc}_r{self.rollidx[inc]}"
                self.physmap[inc] = phys
                self.sem(phys)
            self.counts[phys] += inc_n
            tok = (phys, self.counts[phys])
            inc = phys
        else:
            tok = None
        self.steps[eng].append((fn, waits, inc, inc_n if inc else 0, drain,
                                mmk))
        return tok

    def sp(self, fn, waits=(), stream="s0", drain=False):
        sem = "dma_" + stream
        self.sem(self._phys(sem))
        prev = self.tok(sem)
        prev = prev if prev[1] else None
        return self.emit("sp", fn, list(waits) + [prev], sem, 16, drain)

    def pooldma(self, fn, waits=(), stream="p0", drain=False):
        sem = "dma_" + stream
        self.sem(self._phys(sem))
        prev = self.tok(sem)
        prev = prev if prev[1] else None
        return self.emit("pool", fn, list(waits) + [prev], sem, 16, False)

    def pe(self, fn, waits=(), inc=None, drain=False, mmk=None):
        return self.emit("pe", fn, waits, inc, 1, drain, mmk)

    def dve(self, fn, waits=(), inc=None, drain=True):
        return self.emit("dve", fn, waits, inc, 1, drain)

    def act(self, fn, waits=(), inc=None, drain=True):
        return self.emit("act", fn, waits, inc, 1, drain)

    def pool(self, fn, waits=(), inc=None, inc_n=1, drain=True):
        return self.emit("pool", fn, waits, inc, inc_n, drain)

    def barrier(self, name="bar"):
        toks = []
        for e in ENGS:
            if self.steps[e]:
                toks.append(self.emit(e, lambda en: en.nop(), inc=self.sem(name + "_" + e)))
        for e in ENGS:
            if self.steps[e]:
                self.emit(e, lambda en: en.nop(), waits=[t for t in toks])
        return toks

    def build(self, st):
        nc = self.nc
        sems = {n: st.enter_context(nc.semaphore(n)) for n in self.sem_names}
        block = st.enter_context(nc.Block())

        pecfg_sem = st.enter_context(nc.semaphore("pecfg"))
        pecfg_n = [0]

        def _compat(a, b):
            if a is None or b is None:
                return True
            if a[1] == 1 or b[1] == 1:
                return True
            return a == b

        def mk_body(eng_name):
            def body(e):
                waited = {}
                last_mmk = [None]
                for fn, waits, inc, inc_n, drain, mmk in self.steps[eng_name]:
                    for sname, val in waits:
                        if waited.get(sname, 0) < val:
                            e.wait_ge(sems[sname], val)
                            waited[sname] = val
                    if (eng_name == "pe" and mmk is not None
                            and not _compat(last_mmk[0], mmk)):
                        pecfg_n[0] += 1
                        e.nop().then_inc(pecfg_sem, 1)
                        e.wait_ge(pecfg_sem, pecfg_n[0])
                    if mmk is not None:
                        last_mmk[0] = mmk
                    if drain:
                        e.drain()
                    inst = fn(e)
                    if inc is not None:
                        assert inst is not None, f"step for {inc} returned None"
                        inst.then_inc(sems[inc], inc_n)

            return body

        for eng_name, reg in (("sp", block.sync), ("pe", block.tensor),
                              ("dve", block.vector), ("act", block.scalar),
                              ("pool", block.gpsimd)):
            if self.steps[eng_name]:
                reg(mk_body(eng_name))


class BankRot:

    def __init__(self, pp, bank_tok, pool):
        self.pp = pp
        self.bank_tok = bank_tok
        self.pool = pool
        self.i = 0
        self.cur = None

    def alloc(self):
        b = self.pool[self.i % len(self.pool)]
        self.i += 1
        self.cur = b
        return self.pp[b], self.bank_tok[b]

    def set(self, tok):
        self.bank_tok[self.cur] = tok


class Arena:

    def __init__(self, nc, st, cols_f32):
        self.t = st.enter_context(nc.sbuf_tensor("arena", [128, cols_f32], F32))
        self.cols = cols_f32
        self.off = 0
        self.hi = 0

    def reset(self):
        self.hi = max(self.hi, self.off)
        self.off = 0

    def sb(self, name, shape, dtype=F32):
        P_ = shape[0]
        n_el = 1
        for d in shape[1:]:
            n_el *= d
        esz = 2 if dtype == BF16 else 4
        ncol = (n_el * esz + 31) // 32 * 8
        assert self.off + ncol <= self.cols, \
            f"arena overflow at {name}: {self.off + ncol} > {self.cols}"
        view = self.t[0:P_, self.off:self.off + ncol]
        self.off += ncol
        if dtype == BF16:
            view = view.bitcast(BF16)[:, 0:n_el]
        else:
            view = view[:, 0:n_el]
        if len(shape) > 2:
            view = view.rearrange(
                "p (" + " ".join(f"d{i}" for i in range(1, len(shape))) + ") -> p "
                + " ".join(f"d{i}" for i in range(1, len(shape))),
                **{f"d{i}": shape[i] for i in range(1, len(shape))})
        return view


def host_constants():
    import ml_dtypes

    c = {}
    c["c_ident_f"] = np.eye(128, dtype=np.float32)
    c["c_ident_b"] = np.eye(128, dtype=ml_dtypes.bfloat16)
    c["c_ones_row"] = np.ones((1, 512), dtype=np.float32)
    c["c_ones_row_b"] = np.ones((1, 512), dtype=ml_dtypes.bfloat16)
    tau = np.arange(128)[:, None]
    t = np.arange(128)[None, :]
    c["c_mask_incl"] = np.where(tau <= t, 0.0, -1e30).astype(np.float32)
    c["c_mask_strict"] = np.where(tau < t, 0.0, -1e30).astype(np.float32)
    sel = np.zeros((16, 16 * 128), dtype=np.float32)
    for h in range(16):
        sel[h, h * 128:(h + 1) * 128] = 1.0
    c["c_selh"] = sel
    blk = np.zeros((128, 8 * 16), dtype=np.float32)
    for i in range(8):
        for j2 in range(2):
            h = 2 * i + j2
            blk[j2 * 64:(j2 + 1) * 64, i * 16 + h] = 1.0
    c["c_blk16"] = blk
    misc = np.zeros((128, 8), dtype=np.float32)
    misc[:, 0] = 1e-5
    misc[:, 1] = 1e-6
    misc[:, 2] = 1e-12
    misc[:, 3] = 0.5
    misc[:, 4] = HALFPI
    c["c_misc"] = misc
    c["c_onesq"] = np.ones((128, 128), dtype=np.float32)
    c["c_ones_col"] = np.ones((128, 1), dtype=np.float32)
    return c


CONST_SHAPES = {k: v.shape for k, v in host_constants().items()}


def emit_prologue(P, st, aps):
    nc = P.nc
    sb = lambda n_, s_, d_=F32: st.enter_context(nc.sbuf_tensor(n_, s_, d_))
    toks = {}
    aps["sb_ident_f"] = sb("sb_ident_f", [128, 128])
    aps["sb_ident_b"] = sb("sb_ident_b", [128, 128], BF16)
    aps["sb_ones_row"] = sb("sb_ones_row", [1, 512])
    aps["sb_ones_row_b"] = sb("sb_ones_row_b", [1, 512], BF16)
    aps["sb_mask_incl"] = sb("sb_mask_incl", [128, 128])
    aps["sb_mask_strict"] = sb("sb_mask_strict", [128, 128])
    aps["sb_selh"] = sb("sb_selh", [16, 16 * 128])
    aps["sb_blk16"] = sb("sb_blk16", [128, 8 * 16])
    aps["sb_misc"] = sb("sb_misc", [128, 8])
    aps["sb_onesq"] = sb("sb_onesq", [128, 128])
    aps["sb_ones_col"] = sb("sb_ones_col", [128, 1])
    names = ["ident_f", "ident_b", "ones_row", "ones_row_b", "mask_incl",
             "mask_strict", "selh", "blk16", "misc", "onesq", "ones_col"]
    all_toks = []
    for n in names:
        tok = P.sp(lambda e, n=n: e.dma_start(aps["sb_" + n][:], aps["c_" + n]),
                   stream="cst")
        toks[n + "_ready"] = tok
        all_toks.append(tok)
    toks["consts_ready"] = all_toks[-1]
    return toks


def emit_ln_transpose(P, aps, xc, xln, xT, mcol, sqs, pbank, toks, tag,
                      war_xln=None, war_xT=None, war_psum=None, war_sqs=None,
                      t_x=None, eps=1e-5, transpose=True, xlnf=None,
                      xT_cols=None, rot=None):
    t_mean = P.dve(lambda e: e.tensor_reduce(mcol[:, 0:1], xc[:], AX.X, ALU.add),
                   waits=[t_x], inc="dve")
    t_sq = P.act(lambda e: e.activation(sqs[:], xc[:], AF.Square,
                                        accum_out=mcol[:, 1:2]),
                 waits=[t_x, war_sqs, t_mean], inc="act")
    t_st = P.dve(lambda e: e.tensor_scalar(mcol[:, 0:2], mcol[:, 0:2],
                                           1.0 / D, None, ALU.mult),
                 waits=[t_mean, t_sq], inc="dve")
    t_v1 = P.dve(lambda e: e.scalar_tensor_tensor(
        mcol[:, 1:2], mcol[:, 0:1], mcol[:, 0:1], mcol[:, 1:2],
        ALU.mult, ALU.subtract), waits=[t_st], inc="dve")
    eps_col = {1e-5: 0, 1e-6: 1, 1e-12: 2}[eps]
    t_v2 = P.act(lambda e: e.activation(mcol[:, 1:2], mcol[:, 1:2], AF.Sqrt,
                                        bias=aps["sb_misc"][:, eps_col:eps_col + 1],
                                        scale=-1.0),
                 waits=[t_v1, toks["misc_ready"]], inc="act")
    t_v3 = P.dve(lambda e: e.reciprocal(mcol[:, 1:2], mcol[:, 1:2]),
                 waits=[t_v2], inc="dve")
    if xlnf is not None:
        t_lnf = P.dve(lambda e: e.tensor_scalar(xlnf[:], xc[:], mcol[:, 0:1],
                                                mcol[:, 1:2], ALU.subtract,
                                                ALU.mult),
                      waits=[t_v3, war_xln], inc="dve")
        t_ln = P.dve(lambda e: e.tensor_copy(xln[:], xlnf[:]),
                     waits=[t_lnf], inc="dve")
    else:
        t_ln = P.dve(lambda e: e.tensor_scalar(xln[:], xc[:], mcol[:, 0:1],
                                               mcol[:, 1:2], ALU.subtract,
                                               ALU.mult),
                     waits=[t_v3, war_xln], inc="dve")
    if not transpose:
        return t_ln, None, []
    if xT_cols is None:
        xT_cols = lambda t: xT[:, t * 128:(t + 1) * 128]
    tr_toks = []
    if war_psum is None:
        war_psum = []
    elif not isinstance(war_psum, list):
        war_psum = [war_psum]
    t_prev_ev = war_psum
    for t in range(8):
        if rot is not None:
            bk, war = rot.alloc()
            view = bk[:, 0:512].bitcast(BF16)[:, 0:128]
        else:
            bk, war = None, None
            view = pbank[:, 0:256].bitcast(BF16)[:, 0:128]
        tt = P.pe(lambda e, t=t, view=view: e.transpose(
            view, xln[:, t * 128:(t + 1) * 128], aps["sb_ident_b"][:]),
            waits=[t_ln, toks["ident_b_ready"],
                   war_xT if t == 0 else None, war]
                  + (t_prev_ev if isinstance(t_prev_ev, list) else [t_prev_ev]
                     if rot is None else []),
            inc="pe", mmk=(0, 128))
        tev = P.dve(lambda e, t=t, view=view: e.tensor_copy(
            xT_cols(t), view), waits=[tt], inc="dve")
        if rot is not None:
            rot.set(tev)
        else:
            t_prev_ev = tev
        tr_toks.append(tev)
    return t_ln, tr_toks[-1], tr_toks


def emit_l1_ffn(P, st, aps, toks, S_len):
    nc = P.nc
    x_in = aps["x_in"]
    x_out = aps["x_out"]
    ar = aps["arena"]
    ar.reset()
    sb = lambda n_, s_, d_=F32: ar.sb(n_, s_, d_)
    pp = aps["psum"]

    NCH = S_len // 128

    wq = sb("f1_wq", [128, 8 * D], BF16)
    abf = sb("f1_a", [128, 8 * 16], BF16)
    bbf = sb("f1_b", [16, D], BF16)
    biasr = sb("f1_bias", [1, D], F32)
    stage = sb("f1_stage", [128, 2 * D], F32)
    sc = sb("f1_sc", [128, 2], F32)
    srow = sb("f1_srow", [1, 2], F32)
    tmp8 = sb("f1_tmp8", [128, 9], F32)
    xc = [sb(f"f1_xc{i}", [128, D], F32) for i in range(2)]
    sqs = sb("f1_sqs", [128, D], F32)
    xln = sb("f1_xln", [128, D], BF16)
    xT = [sb(f"f1_xT{i}", [128, D], BF16) for i in range(2)]
    xaT = sb("f1_xaT", [16, 128], BF16)
    mcol = sb("f1_mcol", [128, 2], F32)
    yc = [sb(f"f1_yc{i}", [128, D], F32) for i in range(2)]

    start = list(toks.get("start", []))

    t_bias = P.sp(lambda e: e.dma_start(biasr[:], aps["L1_ffn_bias"][None, :]),
                  waits=start, stream="cst")
    t_la = P.sp(lambda e: e.dma_start(
        stage[:, 0:128].rearrange("p (k r) -> p k r", r=16),
        aps["L1_ffn_a"].rearrange("(k p) r -> p k r", p=128)), waits=start,
        stream="cst")
    t_ca = P.dve(lambda e: e.tensor_copy(
        abf[:].rearrange("p (k r) -> p k r", r=16),
        stage[:, 0:128].rearrange("p (k r) -> p k r", r=16)),
        waits=[t_la], inc="dve")
    t_lb = P.sp(lambda e: e.dma_start(stage[0:16, 1024:1024 + D], aps["L1_ffn_b"]),
                waits=start, stream="cst")
    t_cb = P.dve(lambda e: e.tensor_copy(bbf[:], stage[0:16, 1024:1024 + D]),
                 waits=[t_lb], inc="dve")

    red_toks = []
    for k in range(8):
        w = [red_toks[k - 2]] if k >= 2 else ([t_ca, t_cb] + start)
        tk = P.sp(lambda e, k=k: e.dma_start(
            stage[:, (k % 2) * D:(k % 2 + 1) * D],
            aps["L1_ffn_kernel"][k * 128:(k + 1) * 128, :]), waits=w,
            stream=f"wt{k % 2}")
        tr = P.dve(lambda e, k=k: e.reduce_max(
            tmp8[:, k:k + 1], stage[:, (k % 2) * D:(k % 2 + 1) * D], AX.X,
            apply_absolute_value=True), waits=[tk], inc="dve")
        red_toks.append(tr)
    t_red = P.dve(lambda e: e.reduce_max(tmp8[:, 8:9], tmp8[:, 0:8], AX.X,
                                         apply_absolute_value=True),
                  waits=[red_toks[-1]], inc="dve")
    t_tr = P.pe(lambda e: e.transpose(pp[0][0:1, 0:128], tmp8[:, 8:9],
                                      aps["sb_ident_f"][:]),
                waits=[t_red, toks["ident_f_ready"]], inc="pe", mmk=(0, 128))
    t_mx = P.dve(lambda e: e.reduce_max(srow[0:1, 0:1], pp[0][0:1, 0:128], AX.X),
                 waits=[t_tr], inc="dve")
    t_sc = P.dve(lambda e: e.tensor_scalar(srow[0:1, 0:1], srow[0:1, 0:1],
                                           1.0 / 7.0, None, ALU.mult),
                 waits=[t_mx], inc="dve")
    t_rc = P.dve(lambda e: e.reciprocal(srow[0:1, 1:2], srow[0:1, 0:1]),
                 waits=[t_sc], inc="dve")
    t_bc = P.pe(lambda e: e.matmul(pp[0][:, 0:2], aps["sb_ones_row"][0:1, 0:128],
                                   srow[0:1, 0:2], start=True, stop=True),
                waits=[t_rc, toks["ones_row_ready"], t_mx], inc="pe", mmk=(0, 1))
    t_sc2 = P.dve(lambda e: e.tensor_copy(sc[:], pp[0][:, 0:2]),
                  waits=[t_bc], inc="dve")

    t_q_prev = [t_sc2]
    qt = []
    for k in range(8):
        sl = (k % 2) * D
        w = [qt[k - 2]] if k >= 2 else [t_red]
        tk = P.sp(lambda e, sl=sl, k=k: e.dma_start(
            stage[:, sl:sl + D], aps["L1_ffn_kernel"][k * 128:(k + 1) * 128, :]),
            waits=w + t_q_prev, stream=f"wt{k % 2}")
        t1 = P.dve(lambda e, sl=sl: e.tensor_scalar(
            stage[:, sl:sl + D], stage[:, sl:sl + D], sc[:, 1:2], MAGIC,
            ALU.mult, ALU.add), waits=[tk, t_sc2], inc="dve")
        t2 = P.dve(lambda e, sl=sl: e.tensor_scalar(
            stage[:, sl:sl + D], stage[:, sl:sl + D], MAGIC, None, ALU.subtract),
            waits=[t1], inc="dve")
        t3 = P.dve(lambda e, sl=sl: e.tensor_scalar(
            stage[:, sl:sl + D], stage[:, sl:sl + D], 7.0, -8.0, ALU.min,
            ALU.max), waits=[t2], inc="dve")
        t4 = P.dve(lambda e, sl=sl, k=k: e.tensor_scalar(
            wq[:, k * D:(k + 1) * D], stage[:, sl:sl + D], sc[:, 0:1], None,
            ALU.mult), waits=[t3], inc="dve")
        qt.append(t4)
    t_wq = qt[-1]

    ev = {}
    for ci in range(NCH):
        cb = ci % 2
        row = slice(ci * 128, (ci + 1) * 128)
        t_x = P.sp(lambda e, cb=cb, row=row: e.dma_start(
            xc[cb][:], x_in[row, :]),
            waits=[ev.get(("store", ci - 2))] + start, stream=f"x{cb}")
        t_ln, t_trl, _ = emit_ln_transpose(
            P, aps, xc[cb], xln, xT[cb], mcol, sqs, pp[6], toks, f"f1c{ci}",
            war_xln=ev.get(("mmlast", ci - 1)),
            war_xT=ev.get(("mmlast", ci - 2)),
            war_psum=ev.get(("trev", ci - 1)),
            war_sqs=None, t_x=t_x)
        ev[("trev", ci)] = t_trl

        for k in range(8):
            P.pe(lambda e, k=k, cb=cb: e.matmul(
                pp[7][0:16, 0:128], abf[:, k * 16:(k + 1) * 16],
                xT[cb][:, k * 128:(k + 1) * 128],
                start=(k == 0), stop=(k == 7)),
                waits=([t_trl, t_ca, ev.get(("xaT_ev", ci - 1))]
                       if k == 0 else []),
                inc=("pe" if k == 7 else None), mmk=(0, 128))
        t_xa = P.tok("pe")
        t_xae = P.dve(lambda e: e.tensor_copy(xaT[:], pp[7][0:16, 0:128]),
                      waits=[t_xa], inc="dve")
        ev[("xaT_ev", ci)] = t_xae

        mm_toks = []
        for g in range(2):
            for k in range(8):
                P.pe(lambda e, k=k, g=g, cb=cb: e.matmul(
                    pp[g][:, 0:512], xT[cb][:, k * 128:(k + 1) * 128],
                    wq[:, k * D + g * 512:k * D + (g + 1) * 512],
                    start=(k == 0), stop=False),
                    waits=([t_wq, t_trl, ev.get(("mm_ev", ci - 1, g))]
                           if k == 0 else []), mmk=(0, 128))
            P.pe(lambda e, g=g: e.matmul(
                pp[g][:, 0:512], aps["sb_ones_row"][0:1, 0:128],
                biasr[0:1, g * 512:(g + 1) * 512], start=False, stop=False),
                waits=[t_bias], mmk=(0, 1))
            tmm = P.pe(lambda e, g=g: e.matmul(
                pp[g][:, 0:512], xaT[0:16, 0:128],
                bbf[0:16, g * 512:(g + 1) * 512], start=False, stop=True),
                waits=[t_xae, t_cb], inc="pe", mmk=(0, 16))
            mm_toks.append(tmm)
        ev[("mmlast", ci)] = mm_toks[-1]

        evs = []
        for g in range(2):
            tev = P.dve(lambda e, g=g, cb=cb: e.tensor_add(
                yc[cb][:, g * 512:(g + 1) * 512], pp[g][:, 0:512],
                xc[cb][:, g * 512:(g + 1) * 512]),
                waits=[mm_toks[g], ev.get(("store", ci - 2))], inc="dve")
            ev[("mm_ev", ci, g)] = tev
            evs.append(tev)
        t_store = P.sp(lambda e, cb=cb, row=row: e.dma_start(
            x_out[row, :], yc[cb][:]), waits=evs, stream=f"st{cb}")
        ev[("store", ci)] = t_store

    fin = [P.tok("dma_st0"), P.tok("dma_st1")]
    fin = [t for t in fin if t[1] > 0]
    P.emit("sp", lambda e: e.nop(), waits=fin)
    return fin


def emit_l1_attn(P, st, aps, toks, S_len):
    nc = P.nc
    x_in = aps["x_in"]
    x_out = aps["x_out"]
    ar = aps["arena"]
    ar.reset()
    sb = lambda n_, s_, d_=F32: ar.sb(n_, s_, d_)
    pp = aps["psum"]
    NSUP = S_len // 512

    Bt = sb("a1_Bt", [128, 64 * 128], BF16)
    Wt = sb("a1_Wt", [128, 64 * 128], BF16)
    wband = [sb(f"a1_wband{i}", [128, 1024], F32) for i in range(2)]
    wbandb = sb("a1_wbandb", [128, 1024], BF16)
    acol = sb("a1_acol", [128, 8], F32)
    A_bc = sb("a1_Abc", [128, 8 * 512], F32)
    carry = sb("a1_carry", [128, 8], F32)
    brow = sb("a1_brow", [1, 1024], F32)
    xcs = [sb(f"a1_xcs{i}", [128, 4 * 1024], F32) for i in range(2)]
    xlnb = sb("a1_xlnb", [128, 1024], BF16)
    xT_s = [sb(f"a1_xT{i}", [128, 8 * 512], BF16) for i in range(2)]
    hT = sb("a1_hT", [128, 512], F32)
    hbf = [sb(f"a1_hbf{i}", [128, 8 * 512], BF16) for i in range(2)]
    mcol = sb("a1_mcol", [128, 2], F32)
    sqs = sb("a1_sqs", [128, 1024], F32)
    yc = [sb(f"a1_yc{i}", [128, 1024], F32) for i in range(2)]

    start = list(toks.get("start", []))

    a8r = sb("a1_a8r", [8, 128], F32)
    t_Al = P.sp(lambda e: e.dma_start(
        a8r[:], aps["L1_attn_A"].rearrange("(k p) -> k p", p=128)),
        waits=start, stream="cst")
    t_At = P.pe(lambda e: e.transpose(pp[5][:, 0:8], a8r[:],
                                      aps["sb_ident_f"][0:8, 0:8]),
                waits=[t_Al, toks["ident_f_ready"]], inc="pe", mmk=(0, 8))
    t_A = P.dve(lambda e: e.tensor_copy(acol[:], pp[5][:, 0:8]),
                waits=[t_At], inc="dve")
    abc_toks = []
    for i in range(8):
        tt = P.dve(lambda e, i=i: e.tensor_copy(
            A_bc[:, i * 512:(i + 1) * 512],
            acol[:, i:i + 1].broadcast_to((128, 512))), waits=[t_A], inc="dve")
        abc_toks.append(tt)
    t_abc = abc_toks[-1]
    t_cz = P.dve(lambda e: e.memset(carry[:], 0.0), waits=start, inc="dve")
    t_b = P.sp(lambda e: e.dma_start(brow[:], aps["L1_attn_b"][None, :]),
               waits=start, stream="cst")

    tr_done = []
    t_prev_ev = None
    for which, src, dstf in (
        ("B", aps["L1_attn_B"],
         lambda dt, nt: Bt[:, (dt * 8 + nt) * 128:(dt * 8 + nt) * 128 + 128]),
        ("W", aps["L1_attn_W"],
         lambda nt, ot: Wt[:, nt * 1024 + ot * 128: nt * 1024 + ot * 128 + 128]),
    ):
        for rt in range(8):
            t_ld = P.sp(lambda e, rt=rt, src=src: e.dma_start(
                wband[rt % 2][:], src[rt * 128:(rt + 1) * 128, :]),
                waits=[tr_done[-2]] if len(tr_done) >= 2 else start,
                stream=f"wt{rt % 2}")
            t_cast = P.dve(lambda e, rt=rt: e.tensor_copy(
                wbandb[:], wband[rt % 2][:]),
                waits=[t_ld, tr_done[-1] if tr_done else None], inc="dve")
            for ct in range(8):
                tt = P.pe(lambda e, ct=ct: e.transpose(
                    pp[6][:, 0:256].bitcast(BF16)[:, 0:128],
                    wbandb[:, ct * 128:(ct + 1) * 128], aps["sb_ident_b"][:]),
                    waits=[t_cast, toks["ident_b_ready"], t_prev_ev], inc="pe",
                    mmk=(0, 128))
                tev = P.dve(lambda e, ct=ct, rt=rt, dstf=dstf: e.tensor_copy(
                    dstf(ct, rt), pp[6][:, 0:256].bitcast(BF16)[:, 0:128]),
                    waits=[tt], inc="dve")
                t_prev_ev = tev
            tr_done.append(t_prev_ev)
    t_wt = tr_done[-1]

    ev = {}
    for sc in range(NSUP):
        par = sc % 2
        sub_ln = []
        for sub in range(4):
            ci = sc * 4 + sub
            cb = ci % 2
            t_x = P.sp(lambda e, par=par, sub=sub, ci=ci: e.dma_start(
                xcs[par][:, sub * 1024:(sub + 1) * 1024],
                x_in[ci * 128:(ci + 1) * 128, :]),
                waits=[ev.get(("sup_done", sc - 2))] + start, stream=f"x{cb}")
            t_ln, t_trl, _ = emit_ln_transpose(
                P, aps, xcs[par][:, sub * 1024:(sub + 1) * 1024], xlnb,
                None, mcol, sqs, pp[6], toks, f"a1c{ci}",
                war_xln=ev.get(("bmm", sc - 1)) if sub == 0 else None,
                war_xT=ev.get(("bmm", sc - 2)) if sub == 0 else None,
                war_psum=(t_wt if (sc == 0 and sub == 0) else
                          ev.get(("trev", sc, sub - 1)) or
                          ev.get(("trev", sc - 1, 3))),
                t_x=t_x,
                xT_cols=lambda t, par=par, sub=sub: xT_s[par][
                    :, t * 512 + sub * 128: t * 512 + sub * 128 + 128])
            ev[("trev", sc, sub)] = t_trl
            sub_ln.append((t_ln, t_trl))
        t_tr_all = sub_ln[-1][1]

        scan_toks = []
        t_bmm_last = None
        for i in range(8):
            bank = i % 2
            w0 = [t_tr_all, t_wt, ev.get(("scan", sc, i - 2)),
                  ev.get(("scan", sc - 1, 6 + bank))]
            for k in range(8):
                tok = P.pe(lambda e, i=i, k=k, par=par, bank=bank: e.matmul(
                    pp[bank][:, 0:512],
                    Bt[:, (k * 8 + i) * 128:(k * 8 + i) * 128 + 128],
                    xT_s[par][:, k * 512:(k + 1) * 512],
                    start=(k == 0), stop=(k == 7)),
                    waits=w0 if k == 0 else [],
                    inc=("pe" if k == 7 else None), mmk=(0, 128))
            t_bmm_last = P.tok("pe")
            t_scan = P.dve(lambda e, i=i, par=par, bank=bank: e.tensor_tensor_scan(
                hT[:], A_bc[:, i * 512:(i + 1) * 512], pp[bank][:, 0:512],
                carry[:, i:i + 1], ALU.mult, ALU.add),
                waits=[t_bmm_last, t_abc, t_cz,
                       ev.get(("hcast", sc - 1, i))], inc="dve")
            t_cy = P.dve(lambda e, i=i: e.tensor_copy(
                carry[:, i:i + 1], hT[:, 511:512]), waits=[t_scan], inc="dve")
            t_cast = P.dve(lambda e, i=i, par=par: e.tensor_copy(
                hbf[par][:, i * 512:(i + 1) * 512], hT[:]),
                waits=[t_cy, ev.get(("omm", sc - 2))], inc="dve")
            ev[("scan", sc, i)] = t_scan
            ev[("hcast", sc, i)] = t_cast
            scan_toks.append(t_cast)
        ev[("bmm", sc)] = t_bmm_last

        sub_evs = []
        for sub in range(4):
            ci = sc * 4 + sub
            cb = ci % 2
            mm_toks = []
            for g in range(2):
                bank = 2 + g
                w0 = [scan_toks[-1], ev.get(("omm_ev", sc, sub - 1, g)),
                      ev.get(("omm_ev", sc - 1, 3, g))]
                for nt in range(8):
                    P.pe(lambda e, nt=nt, g=g, par=par, sub=sub, bank=bank:
                         e.matmul(
                             pp[bank][:, 0:512],
                             hbf[par][:, nt * 512 + sub * 128:
                                      nt * 512 + sub * 128 + 128],
                             Wt[:, nt * 1024 + g * 512: nt * 1024 + (g + 1) * 512],
                             start=(nt == 0), stop=False),
                         waits=w0 if nt == 0 else [], mmk=(0, 128))
                tmm = P.pe(lambda e, g=g, bank=bank: e.matmul(
                    pp[bank][:, 0:512], aps["sb_ones_row"][0:1, 0:128],
                    brow[0:1, g * 512:(g + 1) * 512], start=False, stop=True),
                    waits=[t_b], inc="pe", mmk=(0, 1))
                mm_toks.append(tmm)
            ev[("omm", sc)] = mm_toks[-1]
            evs = []
            for g in range(2):
                tev = P.dve(lambda e, g=g, par=par, sub=sub, cb=cb: e.tensor_add(
                    yc[cb][:, g * 512:(g + 1) * 512], pp[2 + g][:, 0:512],
                    xcs[par][:, sub * 1024 + g * 512: sub * 1024 + (g + 1) * 512]),
                    waits=[mm_toks[g], ev.get(("store", ci - 2))], inc="dve")
                ev[("omm_ev", sc, sub, g)] = tev
                evs.append(tev)
            t_store = P.sp(lambda e, cb=cb, ci=ci: e.dma_start(
                x_out[ci * 128:(ci + 1) * 128, :], yc[cb][:]),
                waits=evs, stream=f"st{cb}")
            ev[("store", ci)] = t_store
            sub_evs.append(t_store)
        ev[("sup_done", sc)] = sub_evs[-1]

    fin = [P.tok("dma_st0"), P.tok("dma_st1")]
    fin = [t for t in fin if t[1] > 0]
    P.emit("sp", lambda e: e.nop(), waits=fin)
    return fin


def emit_l0_ffn(P, st, aps, toks, S_len):
    nc = P.nc
    x_in = aps["x_in"]
    x_out = aps["x_out"]
    ar = aps["arena"]
    ar.reset()
    sb = lambda n_, s_, d_=F32: ar.sb(n_, s_, d_)
    pp = aps["psum"]
    NCH = S_len // 128
    F = 4096

    w1sb = sb("f0_w1", [128, 8 * F], BF16)
    w2sb = sb("f0_w2", [128, 32 * 1024], BF16)
    wband = [sb(f"f0_wband{i}", [128, 1024], F32) for i in range(2)]
    b1col = sb("f0_b1col", [128, 32], F32)
    b2row = sb("f0_b2row", [1, 1024], F32)
    gb_bc = sb("f0_gbbc", [128, 2 * 1024], F32)
    xc = [sb(f"f0_xc{i}", [128, 1024], F32) for i in range(2)]
    xlnf = [sb(f"f0_xlnf{i}", [128, 1024], F32) for i in range(2)]
    xlnb = sb("f0_xlnb", [128, 1024], BF16)
    xT = [sb(f"f0_xT{i}", [128, 1024], BF16) for i in range(2)]
    hbf0 = sb("f0_hbf0", [128, 32 * 128], BF16)
    hbf = [hbf0, hbf0]
    x2 = sb("f0_x2", [128, 1024], F32)
    mcol = sb("f0_mcol", [128, 2], F32)
    mcol2 = sb("f0_mcol2", [128, 2], F32)
    yc = [sb(f"f0_yc{i}", [128, 1024], F32) for i in range(2)]
    grow = wband[0][0:1, 0:1024]
    grow2 = wband[1][0:1, 0:1024]

    start = list(toks.get("start", []))

    b1r = sb("f0_b1r", [32, 128], F32)
    t_b1l = P.sp(lambda e: e.dma_start(
        b1r[:], aps["L0_ffn_b1"][0:1, :].rearrange("o (f p) -> (o f) p", p=128)),
        waits=start, stream="cst")
    t_b1t = P.pe(lambda e: e.transpose(pp[5][:, 0:32], b1r[:],
                                       aps["sb_ident_f"][0:32, 0:32]),
                 waits=[t_b1l, toks["ident_f_ready"]], inc="pe", mmk=(0, 32))
    t_b1 = P.dve(lambda e: e.tensor_copy(b1col[:], pp[5][:, 0:32]),
                 waits=[t_b1t], inc="dve")
    t_b2 = P.sp(lambda e: e.dma_start(b2row[:], aps["L0_ffn_b2"][0:1, :]),
                waits=start, stream="cst")
    t_g = P.sp(lambda e: e.dma_start(grow[0:1, 0:1024], aps["L0_ffn_gamma"][None, :]),
               waits=start, stream="cst")
    t_be = P.sp(lambda e: e.dma_start(grow2[0:1, 0:1024], aps["L0_ffn_beta"][None, :]),
                waits=start, stream="cst")
    gb_toks = []
    for j in range(4):
        src = (grow[0:1, j * 512:(j + 1) * 512] if j < 2 else
               grow2[0:1, (j - 2) * 512:(j - 1) * 512])
        tmm = P.pe(lambda e, j=j, src=src: e.matmul(
            pp[7][:, 0:512], aps["sb_ones_row"][0:1, 0:128],
            src, start=True, stop=True),
            waits=[t_g, t_be, toks["ones_row_ready"],
                   gb_toks[-1] if gb_toks else None], inc="pe", mmk=(0, 1))
        tev = P.dve(lambda e, j=j: e.tensor_copy(
            gb_bc[:, j * 512:(j + 1) * 512], pp[7][:, 0:512]),
            waits=[tmm], inc="dve")
        gb_toks.append(tev)
    t_gb = gb_toks[-1]

    wt_toks = []
    for dt in range(8):
        for hh in range(4):
            i2 = dt * 4 + hh
            t_ld = P.sp(lambda e, dt=dt, hh=hh, i2=i2: e.dma_start(
                wband[i2 % 2][:],
                aps["L0_ffn_w1"][0:1, dt * 128:(dt + 1) * 128,
                                 hh * 1024:(hh + 1) * 1024].squeeze(0)),
                waits=[wt_toks[-2]] if len(wt_toks) >= 2 else (start + [t_gb]),
                stream=f"wt{i2 % 2}")
            t_cast = P.dve(lambda e, dt=dt, hh=hh, i2=i2: e.tensor_copy(
                w1sb[:, dt * F + hh * 1024: dt * F + (hh + 1) * 1024],
                wband[i2 % 2][:]),
                waits=[t_ld], inc="dve")
            wt_toks.append(t_cast)
    for ft1 in range(32):
        t_ld = P.sp(lambda e, ft1=ft1: e.dma_start(
            wband[ft1 % 2][:],
            aps["L0_ffn_w2"][0:1, ft1 * 128:(ft1 + 1) * 128, :].squeeze(0)),
            waits=[wt_toks[-2]] if len(wt_toks) >= 2 else (start + [t_gb]),
            stream=f"wt{ft1 % 2}")
        t_cast = P.dve(lambda e, ft1=ft1: e.tensor_copy(
            w2sb[:, ft1 * 1024:(ft1 + 1) * 1024],
            wband[ft1 % 2][:]),
            waits=[t_ld], inc="dve")
        wt_toks.append(t_cast)
    t_w = wt_toks[-1]

    ev = {}
    for ci in range(NCH):
        cb = ci % 2
        row = slice(ci * 128, (ci + 1) * 128)
        t_x = P.sp(lambda e, cb=cb, row=row: e.dma_start(
            xc[cb][:], x_in[row, :]),
            waits=[ev.get(("store", ci - 2))] + start, stream=f"x{cb}")
        t_ln, t_trl, _ = emit_ln_transpose(
            P, aps, xc[cb], xlnb, xT[cb], mcol, yc[cb], pp[6], toks, f"f0c{ci}",
            war_xln=ev.get(("h_mm", ci - 1)),
            war_xT=ev.get(("h_mm", ci - 2)),
            war_psum=([ev.get(("hev", ci - 1, 30)), ev.get(("hev", ci - 1, 31))]
                      if ci > 0 else [t_gb]),
            war_sqs=ev.get(("store", ci - 2)),
            t_x=t_x, xlnf=xlnf[cb])
        ev[("trev", ci)] = t_trl

        hev = []
        bankslot = []
        for ft in range(32):
            if ft < 24:
                bankslot.append((ft // 4, ft % 4))
            else:
                bankslot.append((6 + (ft - 24) // 4, (ft - 24) % 4))
        hmm_toks = []
        for ft in range(32):
            bank, slot = bankslot[ft]
            w0 = [t_trl, t_w, ev.get(("hev", ci - 1, ft))]
            for k in range(8):
                P.pe(lambda e, ft=ft, k=k, cb=cb, bank=bank, slot=slot: e.matmul(
                    pp[bank][:, slot * 128:(slot + 1) * 128],
                    w1sb[:, k * F + ft * 128: k * F + (ft + 1) * 128],
                    xT[cb][:, k * 128:(k + 1) * 128],
                    start=(k == 0), stop=(k == 7)),
                    waits=w0 if k == 0 else [],
                    inc=("pe" if k == 7 else None), mmk=(0, 128))
            hmm_toks.append(P.tok("pe"))
        bank_last = {}
        for ft in range(32):
            bank_last[bankslot[ft][0]] = hmm_toks[ft]
        for ft in range(32):
            bank, slot = bankslot[ft]
            w0 = [bank_last[bank], t_b1, ev.get(("w2mm", ci - 1))]
            if bank % 2 == 0:
                tev = P.dve(lambda e, ft=ft, cb=cb, bank=bank, slot=slot:
                            e.tensor_scalar(
                                hbf[cb][:, ft * 128:(ft + 1) * 128],
                                pp[bank][:, slot * 128:(slot + 1) * 128],
                                b1col[:, ft:ft + 1], 0.0, ALU.add, ALU.max),
                            waits=w0, inc="dve")
            else:
                tev = P.act(lambda e, ft=ft, cb=cb, bank=bank, slot=slot:
                            e.activation(
                                hbf[cb][:, ft * 128:(ft + 1) * 128],
                                pp[bank][:, slot * 128:(slot + 1) * 128],
                                AF.Relu, bias=b1col[:, ft:ft + 1]),
                            waits=w0, inc="act")
            ev[("hev", ci, ft)] = tev
            hev.append(tev)
        ev[("h_mm", ci)] = P.tok("pe")

        mm_toks = []
        for g in range(2):
            bank = g
            w0 = hev + [ev.get(("x2ev", ci - 1, g))]
            for ft in range(32):
                P.pe(lambda e, ft=ft, g=g, cb=cb, bank=bank: e.matmul(
                    pp[bank][:, 0:512],
                    hbf[cb][:, ft * 128:(ft + 1) * 128],
                    w2sb[:, ft * 1024 + g * 512: ft * 1024 + (g + 1) * 512],
                    start=(ft == 0), stop=False),
                    waits=w0 if ft == 0 else [], mmk=(0, 128))
            tmm = P.pe(lambda e, g=g, bank=bank: e.matmul(
                pp[bank][:, 0:512], aps["sb_ones_row"][0:1, 0:128],
                b2row[0:1, g * 512:(g + 1) * 512], start=False, stop=True),
                waits=[t_b2], inc="pe", mmk=(0, 1))
            mm_toks.append(tmm)
        ev[("w2mm", ci)] = mm_toks[-1]

        x2ev = []
        for g in range(2):
            tev = P.dve(lambda e, g=g, cb=cb: e.tensor_add(
                x2[:, g * 512:(g + 1) * 512], pp[g][:, 0:512],
                xlnf[cb][:, g * 512:(g + 1) * 512]),
                waits=[mm_toks[g]], inc="dve")
            ev[("x2ev", ci, g)] = tev
            x2ev.append(tev)
        t_mean = P.dve(lambda e: e.tensor_reduce(
            mcol2[:, 0:1], x2[:], AX.X, ALU.add), waits=x2ev, inc="dve")
        t_sq = P.act(lambda e, cb=cb: e.activation(yc[cb][:], x2[:], AF.Square,
                                                   accum_out=mcol2[:, 1:2]),
                     waits=x2ev + [t_mean], inc="act")
        t_st2 = P.dve(lambda e: e.tensor_scalar(
            mcol2[:, 0:2], mcol2[:, 0:2], 1.0 / D, None, ALU.mult),
            waits=[t_mean, t_sq], inc="dve")
        t_v1 = P.dve(lambda e: e.scalar_tensor_tensor(
            mcol2[:, 1:2], mcol2[:, 0:1], mcol2[:, 0:1], mcol2[:, 1:2],
            ALU.mult, ALU.subtract), waits=[t_st2], inc="dve")
        t_v2 = P.act(lambda e: e.activation(
            mcol2[:, 1:2], mcol2[:, 1:2], AF.Sqrt,
            bias=aps["sb_misc"][:, 0:1], scale=-1.0),
            waits=[t_v1, toks["misc_ready"]], inc="act")
        t_v3 = P.dve(lambda e: e.reciprocal(mcol2[:, 1:2], mcol2[:, 1:2]),
                     waits=[t_v2], inc="dve")
        t_n = P.dve(lambda e: e.tensor_scalar(
            x2[:], x2[:], mcol2[:, 0:1], mcol2[:, 1:2], ALU.subtract, ALU.mult),
            waits=[t_v3], inc="dve")
        t_gm = P.dve(lambda e: e.tensor_mul(x2[:], x2[:], gb_bc[:, 0:1024]),
                     waits=[t_n, t_gb], inc="dve")
        t_bt = P.dve(lambda e: e.tensor_add(x2[:], x2[:], gb_bc[:, 1024:2048]),
                     waits=[t_gm], inc="dve")
        t_y = P.dve(lambda e, cb=cb: e.tensor_add(yc[cb][:], x2[:], xc[cb][:]),
                    waits=[t_bt, ev.get(("store", ci - 2))], inc="dve")
        t_store = P.sp(lambda e, cb=cb, row=row: e.dma_start(
            x_out[row, :], yc[cb][:]), waits=[t_y], stream=f"st{cb}")
        ev[("store", ci)] = t_store

    fin = [P.tok("dma_st0"), P.tok("dma_st1")]
    fin = [t for t in fin if t[1] > 0]
    P.emit("sp", lambda e: e.nop(), waits=fin)
    return fin


def emit_l0_attn(P, st, aps, toks, S_len, dbg=None):
    import os
    L0CUT = int(os.environ.get("L0CUT", "99"))
    nc = P.nc
    x_in = aps["x_in"]
    x_out = aps["x_out"]
    ar = aps["arena"]
    ar.reset()
    sb = lambda n_, s_, d_=F32: ar.sb(n_, s_, d_)
    pp = aps["psum"]
    NSUP = S_len // 512
    NCTRL = 6688
    NG = 53

    wcbf = nc.dram_tensor("wcbf", [1024, NCTRL], BF16, kind="Internal").ap()
    ogbuf = nc.dram_tensor("ogbuf", [S_len, 1024], BF16, kind="Internal").ap()

    wband = [sb(f"a0_wband{i}", [128, 1024], F32) for i in range(2)]
    wo = sb("a0_wo", [128, 16 * 1024], BF16)
    wcp = [sb(f"a0_wcp{i}", [128, 8 * 128], BF16) for i in range(2)]
    bgates = sb("a0_bgates", [1, 3616], BF16)
    bout = sb("a0_bout", [1, 1024], BF16)
    rmm = sb("a0_rmm", [128, 1], F32)
    rmv = sb("a0_rmv", [128, 8], F32)
    wcv = sb("a0_wcv", [128, 3 * 32], F32)
    bq24r = sb("a0_bq24r", [24, 128], F32)
    bq24 = sb("a0_bq24", [128, 24], F32)
    sbias = sb("a0_sbias", [128, 24], F32)
    wbc = sb("a0_wbc", [128, 16], F32)
    bt4 = [sb(f"a0_bt{i}", [128, 4 * 128], BF16) for i in range(4)]
    win = [sb(f"a0_win{i}", [128, 8 * 515], BF16) for i in range(3)]
    xc = [sb(f"a0_xc{i}", [128, 1024], F32) for i in range(2)]
    xtreg = sb("a0_xtreg", [128, 2048], F32)
    xT4 = [xtreg[:, i * 512:(i + 1) * 512].bitcast(BF16)[:, 0:1024]
           for i in range(4)]
    xlnb = sb("a0_xlnb", [128, 1024], BF16)
    mcol = sb("a0_mcol", [128, 2], F32)
    sqs = sb("a0_sqs", [128, 1024], F32)
    upd4 = sb("a0_upd4", [128, 4 * 1024], BF16)
    cnd4 = sb("a0_cnd4", [128, 4 * 1024], BF16)
    ang4 = sb("a0_ang4", [128, 4 * 512], BF16)
    ogst = sb("a0_ogst", [128, 1024], BF16)
    abe = sb("a0_abe", [128, 4 * 32], F32)
    qkv256 = sb("a0_qkv256", [128, 256], BF16)
    rc = sb("a0_rc", [128, 1024], F32)
    trg = sb("a0_trg", [128, 1024], F32)
    wbandb = trg[:, 0:512].bitcast(BF16)[:, 0:1024]

    vin = [sb(f"a0_vin{i}", [128, 4 * 128], F32) for i in range(4)]
    plA = [sb(f"a0_plA{i}", [128, 4 * 128], F32) for i in range(6)]
    plB4 = sb("a0_plB4", [128, 4 * 128], F32)
    plB5 = sb("a0_plB5", [128, 4 * 128], F32)
    plB = vin + [plB4, plB5]
    vx0 = sb("a0_vx0", [128, 4 * 9], F32)
    vy0 = sb("a0_vy0", [128, 4 * 9], F32)
    vecbf = [sb(f"a0_vecbf{i}", [128, 4 * 512], BF16) for i in range(2)]
    cl16 = sb("a0_cl16", [16, 128], F32)
    lnahm = sb("a0_lnahm", [16, 128], F32)
    skhm = sb("a0_skhm", [16, 128], F32)
    khat = sb("a0_khat", [128, 128], F32)
    clm16 = sb("a0_clm16", [16, 128], F32)
    hm = sb("a0_hm", [128, 128], F32)
    tok16 = sb("a0_tok16", [128, 128], F32)
    lna = sb("a0_lna", [128, 16], F32)
    beta = sb("a0_beta", [128, 16], F32)
    sktok = sb("a0_sktok", [128, 16], F32)
    fA = sb("a0_fA", [128, 16], F32)
    fash = sb("a0_fash", [128, 16], F32)
    fstate = sb("a0_fstate", [128, 16], F32)
    small6 = sb("a0_small6", [128, 16], F32)
    NTp = [sb(f"a0_NT{i}", [128, 128], F32) for i in range(2)]
    nkpP = [[sb(f"a0_nkp{p_}_{i}", [128, 128], F32) for i in range(2)]
            for p_ in range(2)]
    nksP = [[sb(f"a0_nks{p_}_{i}", [128, 128], F32) for i in range(2)]
            for p_ in range(2)]
    lamzP = [sb(f"a0_lamz{i}", [128, 128], BF16) for i in range(2)]
    vTg = sb("a0_vTg", [128, 512], F32)
    vT = vTg[:, 0:512].bitcast(BF16)[:, 0:1024]
    gate = vTg[:, 0:512]
    kTm = sb("a0_kTm", [128, 512], F32)
    kT = kTm[:, 0:512].bitcast(BF16)[:, 0:1024]
    Rt = xtreg[:, 0:1024]
    YtP = [[sb(f"a0_Yt{p_}_{i}", [128, 64], F32) for i in range(2)]
           for p_ in range(2)]
    Wt_ = sb("a0_Wt", [128, 16 * 64], BF16)
    kstP = [sb(f"a0_kst{i}", [128, 64], BF16) for i in range(2)]
    acbc = sb("a0_acbc", [128, 16], F32)
    acrow = sb("a0_acrow", [1, 16], F32)
    S0 = sb("a0_S0", [128, 8 * 64], F32)
    S0b = sb("a0_S0b", [128, 8 * 64], BF16)

    mT = xtreg[:, 1024:1536].bitcast(BF16)[:, 0:1024]
    rstdv = sb("a0_rstdv", [128, 1], F32)
    ocomb = sb("a0_ocomb", [128, 1024], F32)
    mtile = kTm[:, 0:512].bitcast(BF16)[:, 0:1024]
    acc = [ocomb[:, 0:512], ocomb[:, 512:1024]]
    yc = [rc, trg]

    reads = wband[0]
    rscr = wband[1]
    lgscrP = [rscr[:, 0:256], rscr[:, 256:512]]
    cscr1P = [rscr[:, 512:576], rscr[:, 576:640]]
    cscr2P = [rscr[:, 640:704], rscr[:, 704:768]]
    start = list(toks.get("start", []))
    ev = {}
    IDF = aps["sb_ident_f"]
    IDB = aps["sb_ident_b"]
    ONESR = aps["sb_ones_row"]
    ONESRB = aps["sb_ones_row_b"]
    ONESQ = aps["sb_onesq"]
    MISC = aps["sb_misc"]
    MSKI = aps["sb_mask_incl"]
    MSKS = aps["sb_mask_strict"]
    SELH = aps["sb_selh"]
    BLK16 = aps["sb_blk16"]

    def join(tag):
        return [P.tok(s_) for s_ in ("pe", "dve", "act", "pool")
                if P.tok(s_)[1] > 0]

    pk_toks = []
    for k in range(8):
        for cc in range(7):
            w_ = min(1024, NCTRL - cc * 1024)
            i2 = k * 7 + cc
            tl = P.sp(lambda e, k=k, cc=cc, w_=w_, i2=i2: e.dma_start(
                wband[i2 % 2][:, 0:w_],
                aps["L0_attn_W_ctrl"][k * 128:(k + 1) * 128,
                                      cc * 1024:cc * 1024 + w_]),
                waits=[pk_toks[-2][0]] if len(pk_toks) >= 2 else start,
                stream=f"wt{i2 % 2}")
            tc = P.dve(lambda e, w_=w_, i2=i2: e.tensor_copy(
                wbandb[:, 0:w_], wband[i2 % 2][:, 0:w_]),
                waits=[tl, pk_toks[-1][1] if pk_toks else None], inc="dve")
            ts_ = P.sp(lambda e, k=k, cc=cc, w_=w_: e.dma_start(
                wcbf[k * 128:(k + 1) * 128, cc * 1024:cc * 1024 + w_],
                wbandb[:, 0:w_]), waits=[tc], stream="wcst")
            pk_toks.append((tc, ts_))
    t_wc_done = pk_toks[-1][1]
    t_wc_done2 = pk_toks[-1][1]

    t_rmm0 = P.sp(lambda e: e.dma_start(
        rmm[0:64, 0:1], aps["L0_attn_rms_m_w"].unsqueeze(1)),
        waits=start, stream="cst")
    t_rmm = P.sp(lambda e: e.dma_start(
        rmm[64:128, 0:1], aps["L0_attn_rms_m_w"].unsqueeze(1)),
        waits=start, stream="cst")
    t_rmv = P.sp(lambda e: e.dma_start(
        rmv[:].rearrange("p (s g) -> p s g", s=2),
        aps["L0_attn_rms_v_w"].rearrange("(g p s) -> p s g", p=128, s=2),
        allow_slow_non_contiguous=True), waits=start, stream="cst")

    wo_toks = []
    for mt_ in range(8):
        tl = P.sp(lambda e, mt_=mt_: e.dma_start(
            wband[mt_ % 2][:], aps["L0_attn_W_out"][mt_ * 128:(mt_ + 1) * 128, :]),
            waits=[wo_toks[-2]] if len(wo_toks) >= 2 else [t_wc_done, t_rmm],
            stream=f"wt{mt_ % 2}")
        tc = P.dve(lambda e, mt_=mt_: e.tensor_scalar(
            wo[:, mt_ * 1024:(mt_ + 1) * 1024], wband[mt_ % 2][:],
            rmm[:, 0:1], None, ALU.mult), waits=[tl], inc="dve")
        wo_toks.append(tc)
    for tv in range(8):
        sig = tv // 4
        pg = tv % 4
        tl = P.sp(lambda e, sig=sig, pg=pg, tv=tv: e.dma_start(
            wband[tv % 2][:],
            aps["L0_attn_W_out"][1024 + pg * 256 + sig:
                                 1024 + (pg + 1) * 256:2, :]),
            waits=[wo_toks[-2]], stream=f"wt{tv % 2}")
        tc = P.dve(lambda e, tv=tv: e.tensor_scalar(
            wo[:, (8 + tv) * 1024:(9 + tv) * 1024], wband[tv % 2][:],
            rmv[:, tv:tv + 1], None, ALU.mult), waits=[tl, t_rmv], inc="dve")
        wo_toks.append(tc)
    t_wo = wo_toks[-1]

    t_bg1 = P.pooldma(lambda e: e.dma_start(bgates[0:1, 0:2560],
                                            aps["L0_attn_b_ctrl"][None, 0:2560]),
                      waits=start, stream="pcst")
    t_bg = P.pooldma(lambda e: e.dma_start(bgates[0:1, 2560:3616],
                                           aps["L0_attn_b_ctrl"][None, 5632:6688]),
                     waits=start, stream="pcst")
    t_bo = P.pooldma(lambda e: e.dma_start(bout[:], aps["L0_attn_b_out"][None, :]),
                     waits=start, stream="pcst")
    t_bqr = P.sp(lambda e: e.dma_start(
        bq24r[:], aps["L0_attn_b_ctrl"][2560:5632].rearrange("(j p) -> j p", p=128)),
        waits=start, stream="cst")
    t_bqt = P.pe(lambda e: e.transpose(pp[5][:, 0:24], bq24r[:],
                                       IDF[0:24, 0:24]),
                 waits=[t_bqr, toks["ident_f_ready"]], inc="pe", mmk=(0, 24))
    t_bq = P.dve(lambda e: e.tensor_copy(bq24[:], pp[5][:, 0:24]),
                 waits=[t_bqt], inc="dve")

    cv_toks = []
    for cj, nm in enumerate(["qconv_w", "kconv_w", "vconv_w"]):
        tl = P.sp(lambda e, cj=cj, nm=nm: e.dma_start(
            wcv[:, cj * 32:(cj + 1) * 32].rearrange("p (c t) -> p c t", t=4),
            aps["L0_attn_" + nm].rearrange("(c p) t -> p c t", p=128)),
            waits=start, stream="cst")
        tsum = P.dve(lambda e, cj=cj: e.tensor_reduce(
            sqs[:, cj * 8:(cj + 1) * 8].unsqueeze(2),
            wcv[:, cj * 32:(cj + 1) * 32].rearrange("p (c t) -> p c t", t=4),
            AX.X, ALU.add), waits=[tl], inc="dve")
        cv_toks.append(tsum)
    t_sbias = P.dve(lambda e: e.tensor_mul(sbias[:], bq24[:], sqs[:, 0:24]),
                    waits=cv_toks + [t_bq], inc="dve")
    t_cv = cv_toks[-1]

    t_wb = P.sp(lambda e: e.dma_start(
        wbc[:].rearrange("p (g i j) -> p g i j", i=2, j=2),
        aps["L0_attn_W_block"].rearrange("(g p) i j -> p g i j", p=128)),
        waits=start, stream="cst")
    bt_toks = []
    for ii in range(2):
        for jj in range(2):
            idx = ii * 2 + jj
            tt = P.dve(lambda e, ii=ii, jj=jj, idx=idx: e.tensor_copy(
                bt4[idx][:].rearrange("p (g t) -> p g t", t=128),
                wbc[:].rearrange("p (g i j) -> p g i j", i=2, j=2)
                [:, :, ii, jj].unsqueeze(2).broadcast_to((128, 4, 128))),
                waits=[t_wb], inc="dve")
            bt_toks.append(tt)
    t_bt = bt_toks[-1]

    t_s0 = P.dve(lambda e: e.memset(S0[:], 0.0), waits=start, inc="dve")
    t_hm0 = P.dve(lambda e: e.memset(hm[:], 0.0), waits=start, inc="dve")
    t_s0b = P.dve(lambda e: e.memset(S0b[:], 0.0), waits=start, inc="dve")
    hsl = lambda h: slice((h % 2) * 64, (h % 2) * 64 + 64)
    csl = lambda h: slice((h // 2) * 64, (h // 2) * 64 + 64)
    t_vc = P.dve(lambda e: e.memset(vx0[:], 0.0), waits=start, inc="dve")
    t_vc2 = P.dve(lambda e: e.memset(vy0[:], 0.0), waits=start, inc="dve")
    tail_toks = []
    for cj in range(3):
        tt = P.dve(lambda e, cj=cj: e.tensor_scalar(
            win[cj][:].rearrange("p (c t) -> p c t", t=515)[:, :, 0:3],
            bq24[:, cj * 8:(cj + 1) * 8].unsqueeze(2).broadcast_to((128, 8, 3)),
            -1.0, None, ALU.mult), waits=[t_bq] + start, inc="dve")
        tail_toks.append(tt)
    t_tails = tail_toks[-1]

    pp6_last = [toks["consts_ready"]]
    bank_tok = {b_: toks["consts_ready"] for b_ in (2, 3, 4, 5, 6, 7)}
    rot_a1 = BankRot(pp, bank_tok, [6, 7])
    rot_qkv = BankRot(pp, bank_tok, [6, 7, 2, 3])
    rot_vg = BankRot(pp, bank_tok, [4, 5])
    rot_vk = BankRot(pp, bank_tok, [6, 7])
    rot_mt = BankRot(pp, bank_tok, [6, 7])

    def dbg_store(name, buf_ap, waitl):
        if dbg is None or name not in dbg:
            return
        tok = P.sp(lambda e: e.dma_start(dbg[name], buf_ap), waits=waitl,
                   stream="dbgs")
        ev[("dbg_last",)] = tok

    cuts = [0, 1024, 1536, 2560, 3584, 4608, 5632, 5648, 5664, 6688]
    names = ["ug", "ang", "cand", "q", "k", "v", "a", "be", "og"]

    for sup in range(NSUP):
        prev_join = ev.get(("join", sup - 1), [])
        ln_toks = []
        for c in range(4):
            ci = sup * 4 + c
            t_x = P.sp(lambda e, c=c, ci=ci: e.dma_start(
                xc[c % 2][:], x_in[ci * 128:(ci + 1) * 128, :]),
                waits=[ln_toks[c - 2] if c >= 2 else None] + start
                      + ([prev_join] if c < 2 else []), stream=f"x{ci % 2}")
            t_ln, t_trl, _ = emit_ln_transpose(
                P, aps, xc[c % 2][:], xlnb, xT4[c][:], mcol, sqs, pp[6], toks,
                f"a0c{ci}",
                war_xln=prev_join if c == 0 else None,
                war_xT=prev_join if c == 0 else None,
                war_sqs=prev_join if c == 0 else None,
                t_x=t_x, rot=rot_a1)
            ln_toks.append(t_trl)
        t_lnall = ln_toks[-1]

        sched = ([(ng, c) for ng in range(44) for c in range(4)]
                 + [(ng, c) for c in range(4) for ng in range(44, NG)])
        piece_tok = {}
        load_i = [0]
        gev = {}
        seg_ev = {}
        pend_tr = []
        qkv_i = [0]
        qkv_tr = {}
        gidx = [0]
        last_piece = [None]
        for ng, c in sched:
            col0 = ng * 128
            wn = min(128, NCTRL - col0)
            pkey = (ng, c if ng >= 44 else -1)
            if last_piece[0] != pkey:
                li = load_i[0]
                tld_a = P.sp(lambda e, col0=col0, wn=wn, li=li: e.dma_start(
                    wcp[li % 2][:].rearrange("p (k n) -> p k n", n=128)
                    [:, 0:4, 0:wn],
                    wcbf[0:512, col0:col0 + wn]
                    .rearrange("(k p) n -> p k n", p=128)),
                    waits=[gev.get(gidx[0] - 2), t_wc_done, t_wc_done2],
                    stream=f"wc{(li % 2) * 2}")
                tld_b = P.sp(lambda e, col0=col0, wn=wn, li=li: e.dma_start(
                    wcp[li % 2][:].rearrange("p (k n) -> p k n", n=128)
                    [:, 4:8, 0:wn],
                    wcbf[512:1024, col0:col0 + wn]
                    .rearrange("(k p) n -> p k n", p=128)),
                    waits=[gev.get(gidx[0] - 2), t_wc_done, t_wc_done2],
                    stream=f"wc{(li % 2) * 2 + 1}")
                piece_tok[pkey] = ([tld_a, tld_b], li % 2)
                load_i[0] += 1
                last_piece[0] = pkey
            tld, pbuf = piece_tok[pkey]
            g_ = gidx[0]
            gidx[0] += 1
            bank = g_ % 2
            segs = []
            for si in range(9):
                lo = max(col0, cuts[si])
                hi = min(col0 + wn, cuts[si + 1])
                if lo < hi:
                    segs.append((names[si], lo, hi))
            has_bias = any(n_ in ("ug", "ang", "cand", "a", "be", "og")
                           for n_, _, _ in segs)
            w0 = [tld, t_lnall, gev.get(g_ - 2),
                  prev_join if ng == 0 else None]
            for k in range(8):
                last_mm = (k == 7) and not has_bias
                P.pe(lambda e, k=k, bank=bank, wn=wn, pbuf=pbuf, c=c,
                     has_bias=has_bias: e.matmul(
                    pp[bank][:, 0:wn],
                    xT4[c][:, k * 128:(k + 1) * 128],
                    wcp[pbuf][:].rearrange("p (k n) -> p k n", n=128)[:, k, 0:wn],
                    start=(k == 0), stop=(k == 7 and not has_bias)),
                    waits=w0 if k == 0 else [],
                    inc=("pe" if last_mm else None), mmk=(0, 128))
            if has_bias:
                bias_sl = []
                for n_, lo, hi in segs:
                    if n_ in ("q", "k", "v"):
                        continue
                    boff = lo if lo < 2560 else (2560 + lo - 5632)
                    bias_sl.append((lo - col0, boff, hi - lo))
                for bi, (po, boff, bw) in enumerate(bias_sl):
                    P.pe(lambda e, bank=bank, po=po, boff=boff, bw=bw,
                         last=(bi == len(bias_sl) - 1): e.matmul(
                        pp[bank][:, po:po + bw], ONESRB[0:1, 0:128],
                        bgates[0:1, boff:boff + bw], start=False, stop=last),
                        waits=[t_bg, t_bg1],
                        inc=("pe" if bi == len(bias_sl) - 1 else None),
                        mmk=(0, 1))
            tmm = P.tok("pe")
            flush, pend_tr = pend_tr, []
            for fn_ in flush:
                fn_()
            evs = []
            for n_, lo, hi in segs:
                po = lo - col0
                ww = hi - lo
                if n_ == "ug":
                    tevv = P.act(lambda e, c=c, lo=lo, po=po, ww=ww,
                                 bank=bank: e.activation(
                        upd4[:, c * 1024 + lo: c * 1024 + lo + ww],
                        pp[bank][:, po:po + ww], AF.Sigmoid),
                        waits=[tmm, prev_join if ng == 0 else None],
                        inc="act")
                elif n_ == "cand":
                    tevv = P.act(lambda e, c=c, lo=lo, po=po, ww=ww,
                                 bank=bank: e.activation(
                        cnd4[:, c * 1024 + lo - 1536:
                             c * 1024 + lo - 1536 + ww],
                        pp[bank][:, po:po + ww], AF.Silu),
                        waits=[tmm, prev_join if ng == 12 else None],
                        inc="act")
                elif n_ == "ang":
                    tevv = P.dve(lambda e, c=c, lo=lo, po=po, ww=ww,
                                 bank=bank: e.tensor_copy(
                        ang4[:, c * 512 + lo - 1024:
                             c * 512 + lo - 1024 + ww],
                        pp[bank][:, po:po + ww]),
                        waits=[tmm, prev_join if ng == 8 else None],
                        inc="dve")
                elif n_ in ("q", "k", "v"):
                    qi = qkv_i[0]
                    qkv_i[0] += 1
                    qoff = (qi % 2) * 128
                    tevv = P.dve(lambda e, po=po, ww=ww, bank=bank, qoff=qoff:
                                 e.tensor_copy(
                        qkv256[:, qoff:qoff + ww], pp[bank][:, po:po + ww]),
                        waits=[tmm, qkv_tr.get(qi - 2)], inc="dve")
                    cglob = (lo - 2560) // 128
                    cj = cglob // 8
                    c8 = cglob % 8

                    def emit_tr(tevv=tevv, cj=cj, c8=c8, c=c, ng=ng,
                                qi=qi, qoff=qoff):
                        bk, war = rot_qkv.alloc()
                        view = bk[:, 0:512].bitcast(BF16)[:, 0:128]
                        ttr = P.pe(lambda e, view=view, qoff=qoff:
                                   e.transpose(
                            view, qkv256[:, qoff:qoff + 128], IDB[:]),
                            waits=[tevv, war], inc="pe", mmk=(0, 128))
                        tte = P.dve(lambda e, cj=cj, c8=c8, c=c, view=view:
                                    e.tensor_copy(
                            win[cj][:].rearrange("p (ch t) -> p ch t", t=515)
                            [:, c8, 3 + c * 128: 3 + c * 128 + 128], view),
                            waits=[ttr, (t_tails if sup == 0 else None),
                                   prev_join if ng == 20 else None,
                                   ev.get(("dbg_last",))], inc="dve")
                        rot_qkv.set(tte)
                        qkv_tr[qi] = ttr
                        ev[("qkvtr_prev",)] = tte

                    pend_tr.append(emit_tr)
                elif n_ in ("a", "be"):
                    dst0 = c * 32 + (0 if n_ == "a" else 16)
                    tevv = P.dve(lambda e, dst0=dst0, po=po, ww=ww,
                                 bank=bank: e.tensor_copy(
                        abe[:, dst0:dst0 + ww], pp[bank][:, po:po + ww]),
                        waits=[tmm, prev_join if ng == 44 else None],
                        inc="dve")
                else:
                    tevv = P.dve(lambda e, lo=lo, po=po, ww=ww,
                                 bank=bank: e.tensor_copy(
                        ogst[:, lo - 5664: lo - 5664 + ww],
                        pp[bank][:, po:po + ww]),
                        waits=[tmm, prev_join if (ng == 44 and c == 0) else None,
                               ev.get(("ogstore", sup, c - 1)),
                               ev.get(("ogload", sup - 1, 3))],
                        inc="dve")
                    if lo + ww == 6688:
                        tso = P.sp(lambda e, c=c, sup=sup: e.dma_start(
                            ogbuf[(sup * 4 + c) * 128:
                                  (sup * 4 + c + 1) * 128, :], ogst[:]),
                            waits=[tevv], stream="og")
                        ev[("ogstore", sup, c)] = tso
                evs.append(tevv)
            gev[g_] = evs[-1]
            for n_, lo, hi in segs:
                seg_ev[(n_, c)] = evs[-1]
        for fn_ in pend_tr:
            fn_()
        pend_tr = []
        ev[("ctrl_last", sup)] = P.tok("pe")

        def vec_gen(c):
            ci = sup * 4 + c
            wv_prev = ev.get(("vecdone", ci - 1))
            w0v = [ev.get(("dn_done", ci - 1)), ev.get(("ctrl_last", sup)),
                   ev.get(("storedma", ci - 2)),
                   ev.get(("storedma", ci - 1)), wv_prev]
            vscr = vin[0][:]
            vsin = ocomb
            sq4s = trg[:, 0:512]
            cscr = trg[:, 512:528]
            t_rc = P.dve(lambda e, c=c: e.tensor_mul(
                rc[:], upd4[:, c * 1024:(c + 1) * 1024],
                cnd4[:, c * 1024:(c + 1) * 1024]),
                waits=[seg_ev[("ug", c)], seg_ev[("cand", c)]] + w0v,
                inc="dve")
            t_rc = P.dve(lambda e, c=c: e.tensor_sub(
                rc[:], cnd4[:, c * 1024:(c + 1) * 1024], rc[:]),
                waits=[t_rc], inc="dve")
            yield
            t_t1 = P.dve(lambda e, c=c: e.tensor_scalar(
                trg[:, 0:512], ang4[:, c * 512:(c + 1) * 512], INV2PI, MAGIC,
                ALU.mult, ALU.add),
                waits=[seg_ev[("ang", c)]] + w0v,
                inc="dve")
            t_t1 = P.dve(lambda e: e.tensor_scalar(
                trg[:, 0:512], trg[:, 0:512], MAGIC, TWOPI, ALU.subtract,
                ALU.mult), waits=[t_t1], inc="dve")
            t_t1 = P.dve(lambda e, c=c: e.scalar_tensor_tensor(
                trg[:, 0:512], trg[:, 0:512], -1.0,
                ang4[:, c * 512:(c + 1) * 512], ALU.mult, ALU.add),
                waits=[t_t1], inc="dve")
            t_t2 = P.dve(lambda e, c=c: e.tensor_scalar(
                trg[:, 512:1024], ang4[:, c * 512:(c + 1) * 512], 1.0, HALFPI,
                ALU.mult, ALU.add), waits=[seg_ev[("ang", c)]] + w0v,
                inc="dve")
            t_t2 = P.dve(lambda e, vscr=vscr: e.tensor_scalar(
                vscr[:, 0:512], trg[:, 512:1024], INV2PI, MAGIC, ALU.mult,
                ALU.add), waits=[t_t2, t_rc], inc="dve")
            t_t2 = P.dve(lambda e, vscr=vscr: e.tensor_scalar(
                vscr[:, 0:512], vscr[:, 0:512], MAGIC, TWOPI, ALU.subtract,
                ALU.mult), waits=[t_t2], inc="dve")
            t_t2 = P.dve(lambda e, vscr=vscr: e.tensor_sub(
                trg[:, 512:1024], trg[:, 512:1024], vscr[:, 0:512]),
                waits=[t_t2], inc="dve")
            t_sin = P.act(lambda e, vsin=vsin: e.activation(
                vsin[:, 0:512], trg[:, 0:512], AF.Sin),
                waits=[t_t1, wv_prev], inc="act")
            t_cos = P.act(lambda e, vsin=vsin: e.activation(
                vsin[:, 512:1024], trg[:, 512:1024], AF.Sin),
                waits=[t_t2], inc="act")
            yield
            vin_toks = []
            for pg in range(4):
                for sig in range(2):
                    bk, war = rot_vg.alloc()
                    view = bk[:, 0:512].bitcast(BF16)[:, 0:128]
                    tt = P.pe(lambda e, c=c, pg=pg, sig=sig, view=view:
                              e.transpose(
                        view, upd4[:, c * 1024 + pg * 256 + sig:
                                   c * 1024 + (pg + 1) * 256:2], IDB[:]),
                        waits=[war, seg_ev[("ug", c)]], inc="pe",
                        mmk=(0, 128))
                    tev = P.dve(lambda e, pg=pg, sig=sig, view=view:
                                e.tensor_copy(
                        vin[sig][:, pg * 128:(pg + 1) * 128], view),
                        waits=[tt, wv_prev], inc="dve")
                    rot_vg.set(tev)
                    vin_toks.append(tev)
                for sig in range(2):
                    bk, war = rot_vg.alloc()
                    tt = P.pe(lambda e, pg=pg, sig=sig, bk=bk: e.transpose(
                        bk[:, 0:128],
                        rc[:, pg * 256 + sig:(pg + 1) * 256:2], IDF[:]),
                        waits=[war, t_rc], inc="pe", mmk=(0, 128))
                    tev = P.dve(lambda e, pg=pg, sig=sig, bk=bk: e.tensor_copy(
                        plA[4 + sig][:, pg * 128:(pg + 1) * 128],
                        bk[:, 0:128]), waits=[tt, wv_prev], inc="dve")
                    rot_vg.set(tev)
                    vin_toks.append(tev)
                for sig, src in ((3, 0), (2, 512)):
                    bk, war = rot_vg.alloc()
                    tt = P.pe(lambda e, pg=pg, src=src, bk=bk, vsin=vsin:
                              e.transpose(
                        bk[:, 0:128],
                        vsin[:, src + pg * 128: src + (pg + 1) * 128],
                        IDF[:]),
                        waits=[war, t_sin, t_cos], inc="pe",
                        mmk=(0, 128))
                    tev = P.dve(lambda e, pg=pg, sig=sig, bk=bk: e.tensor_copy(
                        vin[sig][:, pg * 128:(pg + 1) * 128], bk[:, 0:128]),
                        waits=[tt, wv_prev], inc="dve")
                    rot_vg.set(tev)
                    vin_toks.append(tev)
                yield
            t_vin = vin_toks[-1]

            bt_w = [t_bt, t_vin]
            pb = []
            for dst, g_, (tr1, bb1), (tr2, bb2), sgn in (
                    (0, 0, (2, 0), (3, 1), -1.0),
                    (1, 0, (2, 2), (3, 3), -1.0),
                    (2, 1, (3, 0), (2, 1), 1.0),
                    (3, 1, (3, 2), (2, 3), 1.0)):
                ta = P.dve(lambda e, tr1=tr1, bb1=bb1: e.tensor_mul(
                    plB4[:], vin[tr1][:], bt4[bb1][:]),
                    waits=bt_w + (pb[-1:] if pb else []), inc="dve")
                tb = P.dve(lambda e, tr2=tr2, bb2=bb2: e.tensor_mul(
                    plB5[:], vin[tr2][:], bt4[bb2][:]), waits=[ta], inc="dve")
                tc_ = P.dve(lambda e, sgn=sgn: e.tensor_tensor(
                    plB4[:], plB4[:], plB5[:],
                    ALU.add if sgn > 0 else ALU.subtract),
                    waits=[ta, tb], inc="dve")
                td = P.dve(lambda e, dst=dst, g_=g_: e.tensor_mul(
                    plA[dst][:], plB4[:], vin[g_][:]), waits=[tc_], inc="dve")
                pb.append(td)
                if dst == 1:
                    yield
            t_planes = pb[-1]
            yield

            def v4(t_):
                return t_[:].rearrange("p (g sub t) -> p g sub t", g=4, t=16)

            sq4 = sq4s.rearrange("p (g sub t) -> p g sub t", g=4, t=16)
            cur, nxt = plA, plB
            t_lvl = t_planes
            for s_ in (1, 2, 4, 8):
                lw = [t_lvl]
                last = None
                for pl in range(6):
                    last = P.dve(lambda e, pl=pl, s_=s_, cur=cur, nxt=nxt:
                                 e.tensor_copy(
                        v4(nxt[pl])[:, :, :, 0:s_],
                        v4(cur[pl])[:, :, :, 0:s_]), waits=lw, inc="dve")
                for d_, a_, b_, c2, d2 in ((0, 0, 0, 1, 2), (1, 0, 1, 1, 3),
                                           (2, 2, 0, 3, 2), (3, 2, 1, 3, 3)):
                    t1 = P.dve(lambda e, a_=a_, b_=b_, s_=s_, cur=cur:
                               e.tensor_mul(
                        sq4[:, :, :, s_:16],
                        v4(cur[a_])[:, :, :, s_:16],
                        v4(cur[b_])[:, :, :, 0:16 - s_]), waits=lw + [last],
                        inc="dve")
                    t2 = P.dve(lambda e, d_=d_, c2=c2, d2=d2, s_=s_, cur=cur,
                               nxt=nxt: e.tensor_mul(
                        v4(nxt[d_])[:, :, :, s_:16],
                        v4(cur[c2])[:, :, :, s_:16],
                        v4(cur[d2])[:, :, :, 0:16 - s_]), waits=lw, inc="dve")
                    last = P.dve(lambda e, d_=d_, s_=s_, nxt=nxt: e.tensor_add(
                        v4(nxt[d_])[:, :, :, s_:16],
                        v4(nxt[d_])[:, :, :, s_:16],
                        sq4[:, :, :, s_:16]), waits=[t1, t2], inc="dve")
                for d_, a_, b_ in ((4, 0, 1), (5, 2, 3)):
                    t1 = P.dve(lambda e, a_=a_, s_=s_, cur=cur: e.tensor_mul(
                        sq4[:, :, :, s_:16],
                        v4(cur[a_])[:, :, :, s_:16],
                        v4(cur[4])[:, :, :, 0:16 - s_]), waits=lw + [last],
                        inc="dve")
                    t2 = P.dve(lambda e, d_=d_, b_=b_, s_=s_, cur=cur, nxt=nxt:
                               e.tensor_mul(
                        v4(nxt[d_])[:, :, :, s_:16],
                        v4(cur[b_])[:, :, :, s_:16],
                        v4(cur[5])[:, :, :, 0:16 - s_]), waits=lw, inc="dve")
                    t3 = P.dve(lambda e, d_=d_, s_=s_, nxt=nxt: e.tensor_add(
                        v4(nxt[d_])[:, :, :, s_:16],
                        v4(nxt[d_])[:, :, :, s_:16],
                        sq4[:, :, :, s_:16]), waits=[t1, t2], inc="dve")
                    last = P.dve(lambda e, d_=d_, s_=s_, cur=cur, nxt=nxt:
                                 e.tensor_add(
                        v4(nxt[d_])[:, :, :, s_:16],
                        v4(nxt[d_])[:, :, :, s_:16],
                        v4(cur[d_])[:, :, :, s_:16]), waits=[t3], inc="dve")
                t_lvl = last
                cur, nxt = nxt, cur
                yield
            fin = cur
            oth = nxt

            vv = lambda t_: t_[:].rearrange("p (g t) -> p g t", t=128)
            vx9 = vx0[:].rearrange("p (g n) -> p g n", n=9)
            vy9 = vy0[:].rearrange("p (g n) -> p g n", n=9)
            csc = lambda a, b: cscr[:, a:b].rearrange("p (g o) -> p g o", o=1)
            t_car = t_lvl
            for s8 in range(8):
                tcol = s8 * 16 + 15
                e1 = P.dve(lambda e, s8=s8, tcol=tcol, fin=fin: e.tensor_mul(
                    csc(0, 4),
                    vv(fin[0])[:, :, tcol:tcol + 1], vx9[:, :, s8:s8 + 1]),
                    waits=[t_car], inc="dve")
                e2 = P.dve(lambda e, s8=s8, tcol=tcol, fin=fin: e.tensor_mul(
                    csc(4, 8),
                    vv(fin[1])[:, :, tcol:tcol + 1], vy9[:, :, s8:s8 + 1]),
                    waits=[e1], inc="dve")
                e3 = P.dve(lambda e: e.tensor_add(cscr[:, 0:4], cscr[:, 0:4],
                                                  cscr[:, 4:8]),
                           waits=[e1, e2], inc="dve")
                f1 = P.dve(lambda e, s8=s8, tcol=tcol, fin=fin: e.tensor_mul(
                    csc(8, 12),
                    vv(fin[2])[:, :, tcol:tcol + 1], vx9[:, :, s8:s8 + 1]),
                    waits=[e3], inc="dve")
                f2 = P.dve(lambda e, s8=s8, tcol=tcol, fin=fin: e.tensor_mul(
                    csc(12, 16),
                    vv(fin[3])[:, :, tcol:tcol + 1], vy9[:, :, s8:s8 + 1]),
                    waits=[f1], inc="dve")
                f3 = P.dve(lambda e: e.tensor_add(cscr[:, 8:12], cscr[:, 8:12],
                                                  cscr[:, 12:16]),
                           waits=[f1, f2], inc="dve")
                e4 = P.dve(lambda e, s8=s8, tcol=tcol, fin=fin: e.tensor_add(
                    vx9[:, :, s8 + 1:s8 + 2],
                    csc(0, 4),
                    vv(fin[4])[:, :, tcol:tcol + 1]), waits=[e3, f3],
                    inc="dve")
                f4 = P.dve(lambda e, s8=s8, tcol=tcol, fin=fin: e.tensor_add(
                    vy9[:, :, s8 + 1:s8 + 2],
                    csc(8, 12),
                    vv(fin[5])[:, :, tcol:tcol + 1]), waits=[f3, e4],
                    inc="dve")
                t_car = f4
                if s8 % 3 == 2:
                    yield

            v4f = v4
            vx0v = vx9[:, :, 0:8].unsqueeze(3).broadcast_to((128, 4, 8, 16))
            vy0v = vy9[:, :, 0:8].unsqueeze(3).broadcast_to((128, 4, 8, 16))
            a1 = P.dve(lambda e, fin=fin: e.tensor_mul(
                v4f(oth[2]), v4f(fin[0]), vx0v), waits=[t_car], inc="dve")
            a2 = P.dve(lambda e, fin=fin: e.tensor_mul(
                v4f(oth[3]), v4f(fin[1]), vy0v), waits=[a1], inc="dve")
            a3 = P.dve(lambda e: e.tensor_add(v4f(oth[2]), v4f(oth[2]),
                                              v4f(oth[3])),
                       waits=[a1, a2], inc="dve")
            a4 = P.dve(lambda e, fin=fin: e.tensor_add(
                v4f(oth[0]), v4f(oth[2]), v4f(fin[4])), waits=[a3], inc="dve")
            yield
            b1 = P.dve(lambda e, fin=fin: e.tensor_mul(
                v4f(oth[4]), v4f(fin[2]), vx0v), waits=[a4], inc="dve")
            b2 = P.dve(lambda e, fin=fin: e.tensor_mul(
                v4f(oth[5]), v4f(fin[3]), vy0v), waits=[b1], inc="dve")
            b3 = P.dve(lambda e: e.tensor_add(v4f(oth[4]), v4f(oth[4]),
                                              v4f(oth[5])),
                       waits=[b1, b2], inc="dve")
            b4 = P.dve(lambda e, fin=fin: e.tensor_add(
                v4f(oth[1]), v4f(oth[4]), v4f(fin[5])), waits=[b3], inc="dve")
            cx = P.dve(lambda e, c=c: e.tensor_copy(
                vecbf[0][:].rearrange("p (g t) -> p g t", t=512)
                [:, :, c * 128:(c + 1) * 128], vv(oth[0])),
                waits=[a4, prev_join if c == 0 else None], inc="dve")
            cy = P.dve(lambda e, c=c: e.tensor_copy(
                vecbf[1][:].rearrange("p (g t) -> p g t", t=512)
                [:, :, c * 128:(c + 1) * 128], vv(oth[1])),
                waits=[b4, prev_join if c == 0 else None], inc="dve")
            cc1 = P.dve(lambda e: e.tensor_copy(vx9[:, :, 0:1], vx9[:, :, 8:9]),
                        waits=[cx, b4], inc="dve")
            cc2 = P.dve(lambda e: e.tensor_copy(vy9[:, :, 0:1], vy9[:, :, 8:9]),
                        waits=[cy, cc1], inc="dve")
            ev[("vecdone", ci)] = cc2
            if ci == 3:
                dbg_store("vecx", vecbf[0][:], [cc2])
                dbg_store("vecy", vecbf[1][:], [cc2])

        vec_gens = {c: vec_gen(c) for c in range(4 if L0CUT >= 2 else 0)}

        def vdrive(c, n):
            g = vec_gens.get(c)
            if g is None:
                return
            for _ in range(n):
                try:
                    next(g)
                except StopIteration:
                    vec_gens[c] = None
                    return

        if L0CUT < 4:
            for c in range(4 if L0CUT >= 2 else 0):
                vdrive(c, 99)
        if L0CUT < 3:
            for c in range(4):
                ci = sup * 4 + c
                tst = P.sp(lambda e, ci=ci, c=c: e.dma_start(
                    x_out[ci * 128:(ci + 1) * 128, :], xc[c % 2][:]),
                    waits=[P.tok("pe"), P.tok("dve"),
                           P.tok("act")],
                    stream=f"st{ci % 2}")
                ev[("store", ci)] = tst
                ev[("storedma", ci)] = tst
            ev[("join", sup)] = join(sup)
            continue
        wconv = [ev[("qkvtr_prev",)], t_cv, t_sbias,
                 ] + ev.get(("convsilu", sup - 1), [])
        silu_toks = []
        for cj in range(3):
            wv = win[cj][:].rearrange("p (ch t) -> p ch t", t=515)
            for c8 in range(8):
                ab = c8 % 2
                tprev = silu_toks[-2] if len(silu_toks) >= 2 else None
                t_ = None
                for j in range(4):
                    if j == 0:
                        t_ = P.dve(lambda e, cj=cj, c8=c8, ab=ab, wv=wv:
                                   e.tensor_scalar(
                            acc[ab][:], wv[:, c8, 0:512],
                            wcv[:, cj * 32 + c8 * 4: cj * 32 + c8 * 4 + 1],
                            None, ALU.mult), waits=wconv + [tprev], inc="dve")
                    else:
                        t_ = P.dve(lambda e, cj=cj, c8=c8, ab=ab, j=j, wv=wv:
                                   e.scalar_tensor_tensor(
                            acc[ab][:], wv[:, c8, j:j + 512],
                            wcv[:, cj * 32 + c8 * 4 + j:
                                 cj * 32 + c8 * 4 + j + 1],
                            acc[ab][:], ALU.mult, ALU.add), waits=[t_],
                            inc="dve")
                t_tl = P.dve(lambda e, cj=cj, c8=c8, wv=wv: e.tensor_copy(
                    wv[:, c8, 0:3], wv[:, c8, 512:515]), waits=[t_], inc="dve")
                tsl = P.act(lambda e, cj=cj, c8=c8, ab=ab, wv=wv: e.activation(
                    wv[:, c8, 3:515], acc[ab][:], AF.Silu,
                    bias=sbias[:, cj * 8 + c8: cj * 8 + c8 + 1]),
                    waits=[t_, t_tl], inc="act")
                silu_toks.append(tsl)
        t_conv = silu_toks[-1]
        ev[("convsilu", sup)] = silu_toks[-2:]
        if sup == 0:
            dbg_store("win_q", win[0][:], [t_conv])
            dbg_store("win_k", win[1][:], [t_conv])
            dbg_store("upd", upd4[:], [t_conv])
            dbg_store("ang", ang4[:], [t_conv])

        if L0CUT < 4:
            for c in range(4):
                ci = sup * 4 + c
                tst = P.sp(lambda e, ci=ci, c=c: e.dma_start(
                    x_out[ci * 128:(ci + 1) * 128, :], xc[c % 2][:]),
                    waits=[P.tok("pe"), P.tok("dve"),
                           P.tok("act")],
                    stream=f"st{ci % 2}")
                ev[("store", ci)] = tst
                ev[("storedma", ci)] = tst
            ev[("join", sup)] = join(sup)
            continue
        for c in range(4):
            ci = sup * 4 + c
            qwv = win[0][:].rearrange("p (ch t) -> p ch t", t=515)
            kwv = win[1][:].rearrange("p (ch t) -> p ch t", t=515)
            vwv = win[2][:].rearrange("p (ch t) -> p ch t", t=515)
            tc0 = 3 + c * 128
            dprev = ev.get(("dn_done", ci - 1))
            wD = [t_conv, dprev]

            def cutstore(tokl):
                tst = P.sp(lambda e, ci=ci, c=c: e.dma_start(
                    x_out[ci * 128:(ci + 1) * 128, :], xc[c % 2][:]),
                    waits=tokl + [P.tok("pe"),
                                  P.tok("act"),
                                  P.tok("dve")],
                    stream=f"st{ci % 2}")
                ev[("store", ci)] = tst
                ev[("storedma", ci)] = tst
                ev[("dn_done", ci)] = P.tok("dve")

            l1 = P.dve(lambda e, c=c: e.tensor_scalar(
                small6[:], abe[:, c * 32:c * 32 + 16], 0.0, None, ALU.min),
                waits=wD, inc="dve")
            l2 = P.act(lambda e, c=c: e.activation(
                lna[:], abe[:, c * 32:c * 32 + 16], AF.Abs), waits=wD,
                inc="act")
            l3 = P.act(lambda e: e.activation(lna[:], lna[:], AF.Exp,
                                              scale=-1.0), waits=[l2],
                       inc="act")
            l4 = P.act(lambda e: e.activation(lna[:], lna[:], AF.Ln, bias=1.0),
                       waits=[l3], inc="act")
            l5 = P.dve(lambda e: e.tensor_sub(lna[:], small6[:], lna[:]),
                       waits=[l1, l4], inc="dve")
            t_beta = P.act(lambda e, c=c: e.activation(
                beta[:], abe[:, c * 32 + 16:c * 32 + 32], AF.Sigmoid),
                waits=wD, inc="act")

            t_lt = P.pe(lambda e: e.transpose(pp[2][0:16, 0:128], lna[:],
                                              IDF[:]),
                        waits=[l5, dprev], inc="pe", mmk=(0, 128))
            t_le = P.dve(lambda e: e.tensor_copy(lnahm[:],
                                                 pp[2][0:16, 0:128]),
                         waits=[t_lt, t_hm0], inc="dve")
            t_cl = P.dve(lambda e: e.tensor_tensor_scan(
                cl16[:], ONESQ[0:16, :], lnahm[:], 0.0, ALU.mult,
                ALU.add), waits=[t_le, toks["onesq_ready"]], inc="dve")
            t_clm = P.dve(lambda e: e.tensor_sub(clm16[:], cl16[:],
                                                 lnahm[:]),
                          waits=[t_cl], inc="dve")
            if L0CUT == 40:
                cutstore([t_clm, t_beta])
                continue

            t_qs = P.dve(lambda e, c=c, qwv=qwv, tc0=tc0: e.tensor_mul(
                sqs[:].rearrange("p (ch t) -> p ch t", t=128),
                qwv[:, :, tc0:tc0 + 128], qwv[:, :, tc0:tc0 + 128]),
                waits=wD, inc="dve")
            for i8 in range(8):
                P.pe(lambda e, i8=i8: e.matmul(
                    pp[4][0:16, 0:128], BLK16[:, i8 * 16:(i8 + 1) * 16],
                    sqs[:, i8 * 128:(i8 + 1) * 128],
                    start=(i8 == 0), stop=(i8 == 7)),
                    waits=([t_qs, toks["blk16_ready"], dprev]
                           if i8 == 0 else []),
                    inc=("pe" if i8 == 7 else None), mmk=(0, 128))
            t_qsm = P.tok("pe")
            h3 = P.dve(lambda e: e.tensor_scalar(
                hm[64:80, :], pp[4][0:16, 0:128], 1e-12, None, ALU.add),
                waits=[t_qsm], inc="dve")
            h3b = P.dve(lambda e: e.reciprocal(hm[64:80, :], hm[64:80, :]),
                        waits=[h3], inc="dve")
            h3c = P.act(lambda e: e.activation(hm[64:80, :], hm[64:80, :],
                                               AF.Sqrt), waits=[h3b],
                        inc="act")
            t_ks = P.dve(lambda e, c=c, kwv=kwv, tc0=tc0: e.tensor_mul(
                sqs[:].rearrange("p (ch t) -> p ch t", t=128),
                kwv[:, :, tc0:tc0 + 128], kwv[:, :, tc0:tc0 + 128]),
                waits=[h3], inc="dve")
            for i8 in range(8):
                P.pe(lambda e, i8=i8: e.matmul(
                    pp[4][32:48, 0:128], BLK16[:, i8 * 16:(i8 + 1) * 16],
                    sqs[:, i8 * 128:(i8 + 1) * 128],
                    start=(i8 == 0), stop=(i8 == 7)),
                    waits=([t_ks] if i8 == 0 else []),
                    inc=("pe" if i8 == 7 else None), mmk=(0, 128))
            t_ksm = P.tok("pe")
            h2 = P.dve(lambda e: e.tensor_scalar(
                hm[32:48, :], pp[4][32:48, 0:128], 1e-12, None, ALU.add),
                waits=[t_ksm], inc="dve")
            h2b = P.dve(lambda e: e.reciprocal(hm[32:48, :], hm[32:48, :]),
                        waits=[h2], inc="dve")
            if L0CUT == 41:
                cutstore([h2b])
                continue
            t_skhm = P.act(lambda e: e.activation(skhm[:], hm[32:48, :],
                                                  AF.Sqrt), waits=[h2b],
                           inc="act")
            h1 = P.dve(lambda e: e.tensor_copy(hm[0:16, :], cl16[:]),
                       waits=[t_cl, h2b, h3c], inc="dve")
            t_ht = P.pe(lambda e: e.transpose(pp[2][:, 0:128], hm[:], IDF[:]),
                        waits=[h1, t_le], inc="pe", mmk=(0, 128))
            t_he = P.dve(lambda e: e.tensor_copy(tok16[:], pp[2][:, 0:128]),
                         waits=[t_ht], inc="dve")
            t_sk = P.act(lambda e: e.activation(sktok[:], tok16[:, 32:48],
                                                AF.Sqrt), waits=[t_he],
                         inc="act")
            if L0CUT == 42:
                cutstore([t_sk])
                continue
            t_fA = P.act(lambda e: e.activation(fA[:], tok16[:, 0:16], AF.Exp),
                         waits=[t_he], inc="act")
            t_f1 = P.dve(lambda e: e.tensor_sub(fash[:], tok16[:, 0:16],
                                                lna[:]),
                         waits=[t_he, l5], inc="dve")
            t_fash = P.act(lambda e: e.activation(fash[:], fash[:], AF.Exp),
                           waits=[t_f1], inc="act")
            t_f2 = P.dve(lambda e: e.tensor_mul(fash[:], fash[:], sktok[:]),
                         waits=[t_fash, t_sk], inc="dve")
            t_c127 = P.pe(lambda e: e.transpose(pp[2][0:1, 0:16],
                                                cl16[:, 127:128],
                                                IDF[0:16, 0:16]),
                          waits=[t_cl, t_he], inc="pe", mmk=(0, 16))
            t_c127e = P.dve(lambda e: e.tensor_copy(acrow[:],
                                                    pp[2][0:1, 0:16]),
                            waits=[t_c127], inc="dve")
            t_acm = P.pe(lambda e: e.matmul(
                pp[2][:, 0:16], ONESR[0:1, 0:128], acrow[:], start=True,
                stop=True), waits=[t_c127e], inc="pe", mmk=(0, 1))
            t_fs1 = P.dve(lambda e: e.tensor_sub(fstate[:], pp[2][:, 0:16],
                                                 tok16[:, 0:16]),
                          waits=[t_acm], inc="dve")
            t_fs2 = P.act(lambda e: e.activation(fstate[:], fstate[:], AF.Exp),
                          waits=[t_fs1], inc="act")
            t_fs = P.dve(lambda e: e.tensor_mul(fstate[:], fstate[:],
                                                sktok[:]),
                         waits=[t_fs2, t_sk], inc="dve")
            t_ace = P.act(lambda e: e.activation(acrow[:], acrow[:], AF.Exp),
                          waits=[t_fs1], inc="act")
            t_acb = P.pe(lambda e: e.matmul(
                pp[2][:, 16:32], ONESR[0:1, 0:128], acrow[:], start=True,
                stop=True), waits=[t_ace, t_fs1], inc="pe", mmk=(0, 1))
            t_acbe = P.dve(lambda e: e.tensor_copy(acbc[:],
                                                   pp[2][:, 16:32]),
                           waits=[t_acb], inc="dve")

            if L0CUT == 43:
                cutstore([t_acbe, t_fs])
                continue
            vk_toks = []
            for wvv, dst in ((vwv, vT), (kwv, kT)):
                for c8 in range(8):
                    bk, war = rot_vk.alloc()
                    view = bk[:, 0:512].bitcast(BF16)[:, 0:128]
                    tt = P.pe(lambda e, wvv=wvv, c8=c8, tc0=tc0, view=view:
                              e.transpose(
                        view, wvv[:, c8, tc0:tc0 + 128], IDB[:]),
                        waits=[war, t_conv], inc="pe", mmk=(0, 128))
                    tev = P.dve(lambda e, dst=dst, c8=c8, view=view:
                                e.tensor_copy(
                        dst[:, c8 * 128:(c8 + 1) * 128], view),
                        waits=[tt, dprev, ev.get(("dbg_last",))], inc="dve")
                    rot_vk.set(tev)
                    vk_toks.append(tev)
            t_vk = vk_toks[-1]

            if L0CUT == 44:
                cutstore([t_vk])
                continue
            wave_toks = []
            for wv in range(2):
                for pi, hh in enumerate([0, 2, 4, 6, 1, 3, 5, 7]):
                    h = wv * 8 + hh
                    ht = h // 2
                    hb = (h % 2) * 64
                    slot = h % 8
                    P.pe(lambda e, h=h, ht=ht, hb=hb, slot=slot,
                         kwv=kwv, tc0=tc0: e.matmul(
                        pp[7][:, slot * 64:(slot + 1) * 64],
                        kwv[hb:hb + 64, ht, tc0:tc0 + 128],
                        S0b[hsl(h), csl(h)], start=True, stop=True),
                        waits=([t_conv, t_s0b, dprev, t_vk,
                                ev.get(("ctrl_last", sup)),
                                ev.get(("dbg_last",))]
                               if (wv == 0 and pi == 0)
                               else ([wave_toks[0]] if pi == 0 else [])),
                        inc=("pe" if pi == 7 else None), mmk=(hb, 64))
                t_pw = P.tok("pe")
                rw = P.dve(lambda e, wv=wv: e.tensor_tensor(
                    rscr[:, wv * 512:(wv + 1) * 512]
                    .rearrange("p (h j) -> p h j", j=64),
                    pp[7][:, 0:512].rearrange("p (h j) -> p h j", j=64),
                    fash[:, wv * 8:(wv + 1) * 8].unsqueeze(2)
                    .broadcast_to((128, 8, 64)),
                    ALU.mult), waits=[t_pw, t_f2], inc="dve")
                wave_toks.append(rw)
            r4 = P.dve(lambda e: e.tensor_sub(Rt[:], vT[:], rscr[:]),
                       waits=wave_toks + [t_vk], inc="dve")
            bank_tok[7] = r4
            if ci == 0:
                dbg_store("Rt0", Rt[:], [r4])
                dbg_store("cl0", cl16[:], [t_cl, r4])
                dbg_store("tok160", tok16[:], [t_he, r4])

            if L0CUT < 5:
                tst = P.sp(lambda e, ci=ci, c=c: e.dma_start(
                    x_out[ci * 128:(ci + 1) * 128, :], xc[c % 2][:]),
                    waits=[r4, P.tok("pe"),
                           P.tok("act")],
                    stream=f"st{ci % 2}")
                ev[("store", ci)] = tst
                ev[("storedma", ci)] = tst
                ev[("dn_done", ci)] = r4
                continue
            vdrive(c, 4)
            rd_prev = [None, None]
            st_prev = [None, None]
            hprevP = [[r4], [r4]]
            for pr in range(8):
                hs = [2 * pr, 2 * pr + 1]
                QcP = [qwv[(h % 2) * 64:(h % 2) * 64 + 64, h // 2,
                           tc0:tc0 + 128] for h in hs]
                KcP = [kwv[(h % 2) * 64:(h % 2) * 64 + 64, h // 2,
                           tc0:tc0 + 128] for h in hs]
                t_bl = {}
                t_bge = {}
                t_bsk2 = {}
                for q in range(2):
                    h = hs[q]
                    t_bl[q] = P.pe(lambda e, h=h, q=q: e.matmul(
                        pp[2 + q][:, 0:128], SELH[:, h * 128:(h + 1) * 128],
                        cl16[:], start=True, stop=True),
                        waits=[t_cl, toks["selh_ready"], t_acbe,
                               ev.get(("dbg_last",))] + hprevP[q],
                        inc="pe", mmk=(0, 16))
                    t_bge[q] = P.pe(lambda e, h=h, q=q: e.matmul(
                        pp[2 + q][:, 128:256], SELH[:, h * 128:(h + 1) * 128],
                        clm16[:], start=True, stop=True), waits=[t_clm],
                        inc="pe", mmk=(0, 16))
                    t_bsk2[q] = P.pe(lambda e, h=h, q=q: e.matmul(
                        pp[4 + q][:, 128:256], SELH[:, h * 128:(h + 1) * 128],
                        skhm[:], start=True, stop=True),
                        waits=[t_skhm], inc="pe", mmk=(0, 16))
                t_kh = {}
                s_lam = {}
                s_gam = {}
                t_exp = {}
                for q in range(2):
                    h = hs[q]
                    hb = (h % 2) * 64
                    t_kh[q] = P.dve(lambda e, q=q, hb=hb, Kc=KcP[q]:
                                    e.tensor_tensor(
                        khat[hb:hb + 64, :], Kc, pp[4 + q][hb:hb + 64, 128:256],
                        ALU.mult), waits=[t_bsk2[q]], inc="dve")
                    s_lam[q] = P.dve(lambda e, h=h, q=q:
                                     e.scalar_tensor_tensor(
                        lgscrP[q][:, 0:128], pp[2 + q][:, 0:128],
                        tok16[:, h:h + 1], MSKI[:], ALU.subtract, ALU.add),
                        waits=[t_bl[q], t_he, toks["mask_incl_ready"]],
                        inc="dve")
                    s_gam[q] = P.dve(lambda e, h=h, q=q:
                                     e.scalar_tensor_tensor(
                        lgscrP[q][:, 128:256], pp[2 + q][:, 128:256],
                        tok16[:, h:h + 1], MSKS[:], ALU.subtract, ALU.add),
                        waits=[t_bge[q], toks["mask_strict_ready"]],
                        inc="dve")
                for q in range(2):
                    t_exp[q] = P.act(lambda e, q=q: e.activation(
                        lgscrP[q][:], lgscrP[q][:], AF.Exp),
                        waits=[s_lam[q], s_gam[q]], inc="act")
                t_G = {}
                t_Z = {}
                t_qm = {}
                for q in range(2):
                    h = hs[q]
                    hb = (h % 2) * 64
                    t_G[q] = P.pe(lambda e, q=q, hb=hb: e.matmul(
                        pp[2 + q][:, 256:384], khat[hb:hb + 64, :],
                        khat[hb:hb + 64, :], start=True, stop=True),
                        waits=[s_lam[q], s_gam[q], t_kh[q]], inc="pe",
                        mmk=(hb, 64))
                    t_Z[q] = P.pe(lambda e, q=q, Kc=KcP[q], Qc=QcP[q]:
                                  e.matmul(
                        pp[2 + q][:, 384:512], Kc, Qc, start=True, stop=True),
                        waits=[], inc="pe", mmk=(hb, 64))
                    t_qm[q] = P.pe(lambda e, h=h, q=q, Qc=QcP[q]: e.matmul(
                        pp[4 + q][:, 256:320], Qc, S0b[hsl(h), csl(h)],
                        start=True, stop=True),
                        waits=[t_kh[q], rd_prev[q]], inc="pe",
                        mmk=(hb, 64))
                t_NT = {}
                t_lz = {}
                for q in range(2):
                    h = hs[q]
                    t_NT[q] = P.dve(lambda e, h=h, q=q:
                                    e.scalar_tensor_tensor(
                        NTp[q][:], pp[2 + q][:, 256:384], beta[:, h:h + 1],
                        lgscrP[q][:, 128:256], ALU.mult, ALU.mult),
                        waits=[t_Z[q], t_exp[q], t_beta], inc="dve")
                    t_lz[q] = P.dve(lambda e, h=h, q=q:
                                    e.scalar_tensor_tensor(
                        lamzP[q][:], pp[2 + q][:, 384:512],
                        sktok[:, h:h + 1], lgscrP[q][:, 0:128],
                        ALU.mult, ALU.mult),
                        waits=[t_Z[q], t_exp[q], t_sk], inc="dve")
                vdrive(c, 1)
                t_Nt = {}
                t_N = {}
                t_y = {}
                for q in range(2):
                    t_Nt[q] = P.pe(lambda e, q=q: e.transpose(
                        pp[0 + q][:, 256:384], NTp[q][:], IDF[:]),
                        waits=[t_NT[q]], inc="pe", mmk=(0, 128))
                for q in range(2):
                    t_N[q] = P.act(lambda e, q=q: e.activation(
                        nkpP[q][0][:], pp[0 + q][:, 256:384], AF.Copy),
                        waits=[t_Nt[q]], inc="act")
                    t_y[q] = P.dve(lambda e, q=q, h=hs[q]: e.tensor_copy(
                        YtP[q][0][:], Rt[:, h * 64:(h + 1) * 64]),
                        waits=[r4], inc="dve")
                ycur = [0, 0]
                e_ev = {0: [t_N[0]], 1: [t_N[1]]}
                for lvl in range(7):
                    ma = {}
                    for q in range(2):
                        Bt_l = NTp[q] if lvl == 0 else nksP[q][lvl % 2]
                        ma[q] = P.pe(lambda e, q=q, Bt_l=Bt_l,
                                     yc_=ycur[q]: e.matmul(
                            pp[4 + q][:, 64:128], Bt_l[:], YtP[q][yc_][:],
                            start=True, stop=True),
                            waits=[t_y[q], t_NT[q], t_qm[q]] + e_ev[q],
                            inc="pe", mmk=(0, 128))
                    if lvl < 6:
                        for q in range(2):
                            A_l = nkpP[q][lvl % 2] if lvl else nkpP[q][0]
                            Bt_l = NTp[q] if lvl == 0 else nksP[q][lvl % 2]
                            nxt = (lvl + 1) % 2
                            m1 = P.pe(lambda e, q=q, A_l=A_l, Bt_l=Bt_l:
                                      e.matmul(
                                pp[0 + q][:, 0:128], Bt_l[:], A_l[:],
                                start=True, stop=True),
                                waits=e_ev[q], inc="pe", mmk=(0, 128))
                            m2 = P.pe(lambda e, q=q, A_l=A_l, Bt_l=Bt_l:
                                      e.matmul(
                                pp[0 + q][:, 128:256], A_l[:], Bt_l[:],
                                start=True, stop=True),
                                waits=[], inc="pe", mmk=(0, 128))
                            e1_ = P.act(lambda e, q=q, nxt=nxt: e.activation(
                                nkpP[q][nxt][:], pp[0 + q][:, 0:128],
                                AF.Copy), waits=[m2], inc="act")
                            e2_ = P.act(lambda e, q=q, nxt=nxt: e.activation(
                                nksP[q][nxt][:], pp[0 + q][:, 128:256],
                                AF.Copy), waits=[e1_], inc="act")
                            e_ev[q] = [e1_, e2_]
                    for q in range(2):
                        if lvl == 0:
                            t_y[q] = P.dve(lambda e, q=q, yc_=ycur[q]:
                                           e.tensor_sub(
                                YtP[q][1 - yc_][:], YtP[q][yc_][:],
                                pp[4 + q][:, 64:128]),
                                waits=[ma[q]], inc="dve")
                        else:
                            t_y[q] = P.dve(lambda e, q=q, yc_=ycur[q]:
                                           e.tensor_add(
                                YtP[q][1 - yc_][:], pp[4 + q][:, 64:128],
                                YtP[q][yc_][:]),
                                waits=[ma[q]], inc="dve")
                        ycur[q] = 1 - ycur[q]
                vdrive(c, 1)
                t_w = {}
                for q in range(2):
                    h = hs[q]
                    t_w[q] = P.dve(lambda e, h=h, q=q, yc_=ycur[q]:
                                   e.tensor_scalar(
                        Wt_[:, h * 64:(h + 1) * 64], YtP[q][yc_][:],
                        beta[:, h:h + 1], None, ALU.mult),
                        waits=[t_y[q], t_beta], inc="dve")
                    if dbg is not None and h == 1 and ci == 0:
                        dbg_store("Yfin", YtP[q][ycur[q]][:], [t_y[q]])
                        dbg_store("NT0", NTp[q][:], [t_NT[q]])
                        dbg_store("W0", Wt_[:, 64:128], [t_w[q]])
                t_rm = {}
                for q in range(2):
                    h = hs[q]
                    t_rm[q] = P.pe(lambda e, h=h, q=q: e.matmul(
                        pp[4 + q][:, 320:384], lamzP[q][:],
                        Wt_[:, h * 64:(h + 1) * 64],
                        start=True, stop=True),
                        waits=[t_lz[q], t_w[q]], inc="pe", mmk=(0, 128))
                for q in range(2):
                    h = hs[q]
                    c1 = P.dve(lambda e, h=h, q=q: e.tensor_scalar(
                        cscr1P[q][:], pp[4 + q][:, 256:320], fA[:, h:h + 1],
                        None, ALU.mult), waits=[t_rm[q], t_fA], inc="dve")
                    c2 = P.dve(lambda e, q=q: e.tensor_add(
                        cscr2P[q][:], cscr1P[q][:], pp[4 + q][:, 320:384]),
                        waits=[c1], inc="dve")
                    c3 = P.dve(lambda e, h=h, q=q: e.tensor_scalar(
                        reads[:, h * 64:(h + 1) * 64], cscr2P[q][:],
                        tok16[:, 64 + h:64 + h + 1], None, ALU.mult),
                        waits=[c2], inc="dve")
                    rd_prev[q] = c3
                t_kst = {}
                for q in range(2):
                    h = hs[q]
                    t_kst[q] = P.dve(lambda e, h=h, q=q: e.tensor_scalar(
                        kstP[q][:], kT[:, h * 64:(h + 1) * 64],
                        fstate[:, h:h + 1], None, ALU.mult),
                        waits=[t_vk, t_fs, rd_prev[q]], inc="dve")
                t_sm = {}
                for q in range(2):
                    h = hs[q]
                    hb = (h % 2) * 64
                    t_sm[q] = P.pe(lambda e, h=h, q=q, hb=hb: e.matmul(
                        pp[4 + q][hb:hb + 64, 0:64], kstP[q][:],
                        Wt_[:, h * 64:(h + 1) * 64],
                        start=True, stop=True,
                        tile_position=((0, hb) if hb else None)),
                        waits=[t_kst[q], t_w[q], st_prev[q]],
                        inc="pe", mmk=(0, 128, hb))
                for q in range(2):
                    h = hs[q]
                    hb = (h % 2) * 64
                    t_su = P.dve(lambda e, h=h, q=q, hb=hb:
                                 e.scalar_tensor_tensor(
                        S0[hsl(h), csl(h)], S0[hsl(h), csl(h)],
                        acbc[hsl(h), h:h + 1], pp[4 + q][hb:hb + 64, 0:64],
                        ALU.mult, ALU.add),
                        waits=[t_sm[q], t_acbe], inc="dve")
                    t_sb = P.dve(lambda e, h=h: e.tensor_copy(
                        S0b[hsl(h), csl(h)], S0[hsl(h), csl(h)]),
                        waits=[t_su], inc="dve")
                    st_prev[q] = t_sb
                    hprevP[q] = [t_sb]
                vdrive(c, 1)
            vdrive(c, 99)
            t_rdall = [rd_prev[0], rd_prev[1]]
            t_stall = [st_prev[0], st_prev[1]]
            if ci == 0:
                dbg_store("Wt0", Wt_[:], t_stall)
                dbg_store("reads0", reads[:], t_rdall)
                dbg_store("S0a", S0[:], t_stall)

            t_msq = P.act(lambda e: e.activation(sqs[:], reads[:], AF.Square),
                          waits=[t_rdall], inc="act")
            t_mrd = P.dve(lambda e: e.tensor_reduce(
                small6[:].unsqueeze(2),
                sqs[:].rearrange("p (h j) -> p h j", j=64), AX.X, ALU.add),
                waits=[t_msq], inc="dve")
            t_mr2 = P.dve(lambda e: e.tensor_scalar(
                small6[:], small6[:], 1.0 / 64.0, None, ALU.mult),
                waits=[t_mrd], inc="dve")
            t_mr3 = P.act(lambda e: e.activation(
                small6[:], small6[:], AF.Sqrt, bias=MISC[:, 1:2]),
                waits=[t_mr2, toks["misc_ready"]], inc="act")
            t_mr4 = P.dve(lambda e: e.reciprocal(small6[:], small6[:]),
                          waits=[t_mr3], inc="dve")
            t_m = P.dve(lambda e: e.tensor_tensor(
                mtile[:].rearrange("p (h j) -> p h j", j=64),
                reads[:].rearrange("p (h j) -> p h j", j=64),
                small6[:].unsqueeze(2).broadcast_to((128, 16, 64)), ALU.mult),
                waits=[t_mr4, t_rdall], inc="dve")
            mt_toks = []
            for t8 in range(8):
                bk, war = rot_mt.alloc()
                view = bk[:, 0:512].bitcast(BF16)[:, 0:128]
                tt = P.pe(lambda e, t8=t8, view=view: e.transpose(
                    view, mtile[:, t8 * 128:(t8 + 1) * 128], IDB[:]),
                    waits=[t_m, war], inc="pe", mmk=(0, 128))
                tev = P.dve(lambda e, t8=t8, view=view: e.tensor_copy(
                    mT[:, t8 * 128:(t8 + 1) * 128], view),
                    waits=[tt, ev.get(("epi_mT", ci - 1))], inc="dve")
                rot_mt.set(tev)
                mt_toks.append(tev)
            ev[("epi_mT", ci)] = mt_toks[-1]
            if ci == 0:
                dbg_store("m0", mtile[:], [t_m, mt_toks[-1]])

            vxq = P.act(lambda e, c=c: e.activation(
                sqs[:, 0:512].rearrange("p (g t) -> p g t", t=128),
                vecbf[0][:].rearrange("p (g t) -> p g t", t=512)
                [:, :, c * 128:(c + 1) * 128], AF.Square),
                waits=[ev[("vecdone", ci)], t_msq, t_mrd, t_m], inc="act")
            vyq = P.act(lambda e, c=c: e.activation(
                sqs[:, 512:1024].rearrange("p (g t) -> p g t", t=128),
                vecbf[1][:].rearrange("p (g t) -> p g t", t=512)
                [:, :, c * 128:(c + 1) * 128], AF.Square),
                waits=[vxq], inc="act")
            vm1 = P.pe(lambda e: e.matmul(
                pp[5][0:1, 0:512], aps["sb_ones_col"][:, 0:1],
                sqs[:, 0:512], start=True, stop=False),
                waits=[vxq, vyq, toks["ones_col_ready"], t_stall], inc=None,
                mmk=(0, 128))
            vm2 = P.pe(lambda e: e.matmul(
                pp[5][0:1, 0:512], aps["sb_ones_col"][:, 0:1],
                sqs[:, 512:1024], start=False, stop=True), inc="pe",
                mmk=(0, 128))
            t_vm = P.tok("pe")
            vr1 = P.dve(lambda e: e.tensor_reduce(
                hm[0:1, 0:128].unsqueeze(2),
                pp[5][0:1, 0:512].rearrange("o (g t) -> o t g", g=4),
                AX.X, ALU.add), waits=[t_vm], inc="dve")
            vr2 = P.dve(lambda e: e.tensor_scalar(
                hm[0:1, 0:128], hm[0:1, 0:128], 1.0 / 1024.0, None, ALU.mult),
                waits=[vr1], inc="dve")
            vr3 = P.act(lambda e: e.activation(
                hm[0:1, 0:128], hm[0:1, 0:128], AF.Sqrt, bias=MISC[0:1, 1:2]),
                waits=[vr2], inc="act")
            vr4 = P.dve(lambda e: e.reciprocal(hm[0:1, 0:128],
                                               hm[0:1, 0:128]),
                        waits=[vr3], inc="dve")
            vr5 = P.pe(lambda e: e.transpose(pp[4][:, 0:1],
                                             hm[0:1, 0:128], IDF[0:1, 0:1]),
                       waits=[vr4, t_vm, t_stall], inc="pe", mmk=(0, 1))
            vr6 = P.dve(lambda e: e.tensor_copy(rstdv[:], pp[4][:, 0:1]),
                        waits=[vr5], inc="dve")

            mm_toks = {}
            for g in range(2):
                for part, bank0 in (("m", 0), ("v", 2)):
                    bank = bank0 + g
                    w0 = [mt_toks[-1], t_wo, ev.get(("vecdone", ci)),
                          ev.get(("epi_ev", ci - 1, part, g)),
                          ev.get(("ctrl_last", sup)), t_stall]
                    for kk in range(8):
                        if part == "m":
                            lhs = lambda e, kk=kk: mT[:, kk * 128:(kk + 1) * 128]
                            wok = kk
                            is_last = False
                        else:
                            sig = kk // 4
                            pg = kk % 4
                            lhs = (lambda e, sig=sig, pg=pg, c=c:
                                   vecbf[sig][:].rearrange(
                                       "p (g t) -> p g t", t=512)
                                   [:, pg, c * 128:(c + 1) * 128])
                            wok = 8 + kk
                            is_last = (kk == 7)
                        P.pe(lambda e, lhs=lhs, wok=wok, g=g, bank=bank,
                             kk=kk, is_last=is_last: e.matmul(
                            pp[bank][:, 0:512], lhs(e),
                            wo[:, wok * 1024 + g * 512:
                               wok * 1024 + (g + 1) * 512],
                            start=(kk == 0), stop=is_last),
                            waits=w0 if kk == 0 else [],
                            inc=("pe" if is_last else None), mmk=(0, 128))
                    if part == "m":
                        P.pe(lambda e, g=g, bank=bank: e.matmul(
                            pp[bank][:, 0:512], ONESRB[0:1, 0:128],
                            bout[0:1, g * 512:(g + 1) * 512], start=False,
                            stop=True), waits=[t_bo], inc="pe", mmk=(0, 1))
                    mm_toks[(part, g)] = P.tok("pe")

            tld_og = P.sp(lambda e, ci=ci: e.dma_start(
                ogst[:], ogbuf[ci * 128:(ci + 1) * 128, :]),
                waits=[ev[("ogstore", sup, 3)],
                       ev.get(("ogate", ci - 1))], stream="og")
            ev[("ogload", sup, c)] = tld_og
            t_xr = P.sp(lambda e, ci=ci: e.dma_start(
                sqs[:], x_in[ci * 128:(ci + 1) * 128, :]),
                waits=[t_vm, vyq], stream=f"xr{ci % 2}")
            st_evs = []
            o4prev = [None]
            for g in range(2):
                o1 = P.dve(lambda e, g=g: e.tensor_scalar(
                    ocomb[:, g * 512:(g + 1) * 512], pp[2 + g][:, 0:512],
                    rstdv[:, 0:1], None, ALU.mult),
                    waits=[mm_toks[("v", g)], vr6], inc="dve")
                o2 = P.dve(lambda e, g=g: e.tensor_add(
                    ocomb[:, g * 512:(g + 1) * 512],
                    ocomb[:, g * 512:(g + 1) * 512], pp[g][:, 0:512]),
                    waits=[o1, mm_toks[("m", g)]], inc="dve")
                ev[("epi_ev", ci, "v", g)] = o1
                ev[("epi_ev", ci, "m", g)] = o2
                o3 = P.act(lambda e, g=g: e.activation(
                    gate[:], ogst[:, g * 512:(g + 1) * 512], AF.Sigmoid),
                    waits=[tld_og, vyq, o4prev[0]], inc="act")
                o4 = P.dve(lambda e, g=g: e.tensor_mul(
                    ocomb[:, g * 512:(g + 1) * 512],
                    ocomb[:, g * 512:(g + 1) * 512], gate[:]),
                    waits=[o2, o3], inc="dve")
                o4prev[0] = o4
                o5 = P.dve(lambda e, g=g, ci=ci: e.tensor_add(
                    yc[ci % 2][:, g * 512:(g + 1) * 512],
                    ocomb[:, g * 512:(g + 1) * 512],
                    sqs[:, g * 512:(g + 1) * 512]),
                    waits=[o4, t_xr, ev.get(("storedma", ci - 2))], inc="dve")
                st_evs.append(o5)
            ev[("ogate", ci)] = o3
            t_store = P.sp(lambda e, ci=ci: e.dma_start(
                x_out[ci * 128:(ci + 1) * 128, :], yc[ci % 2][:]),
                waits=st_evs, stream=f"st{ci % 2}")
            ev[("storedma", ci)] = t_store
            ev[("store", ci)] = t_store
            ev[("dn_done", ci)] = st_evs[-1]
        ev[("join", sup)] = join(sup)

    fin = [P.tok("dma_st0"), P.tok("dma_st1")]
    fin = [t for t in fin if t[1] > 0]
    if fin:
        P.emit("sp", lambda e: e.nop(), waits=fin)
    return fin


def build_program(S_len, stages, io_kind="external", dbg_spec=None):
    nc = bass.Bass("TRN2", target_bir_lowering=False)
    aps = {}
    for name, shape in INPUT_SHAPES.items():
        aps[name] = nc.dram_tensor(name, list(shape), F32,
                                   kind="ExternalInput").ap()
    for name, shape in CONST_SHAPES.items():
        dt = BF16 if name.endswith("_b") else F32
        aps[name] = nc.dram_tensor(name, list(shape), dt,
                                   kind="ExternalInput").ap()
    aps["x0"] = nc.dram_tensor("x0", [S_len, D], F32, kind="ExternalInput").ap()
    aps["xout"] = nc.dram_tensor("xout", [S_len, D], F32,
                                 kind="ExternalOutput").ap()
    dbg = None
    if dbg_spec:
        dbg = {}
        for name, shape, dt in dbg_spec:
            dbg[name] = nc.dram_tensor("dbg_" + name, list(shape), dt,
                                       kind="ExternalOutput").ap()
    n_inter = len(stages) - 1
    inter = [nc.dram_tensor(f"xbuf{i}", [S_len, D], F32, kind="Internal").ap()
             for i in range(n_inter)]

    with contextlib.ExitStack() as st:
        P = Prog(nc)
        aps["psum"] = [st.enter_context(nc.psum_tensor(f"pp{i}", [128, 512], F32))
                       for i in range(8)]
        aps["arena"] = Arena(nc, st, 49500)
        toks = emit_prologue(P, st, aps)
        toks["start"] = [toks["consts_ready"]]
        bufs = ["x0"] + [f"xbuf{i}" for i in range(n_inter)] + ["xout"]
        buf_aps = [aps["x0"]] + inter + [aps["xout"]]
        aps["dbg"] = dbg
        for si, stage in enumerate(stages):
            aps["x_in"] = buf_aps[si]
            aps["x_out"] = buf_aps[si + 1]
            if stage == "l0_attn":
                fin = STAGE_EMITTERS[stage](P, st, aps, toks, S_len, dbg=dbg)
            else:
                fin = STAGE_EMITTERS[stage](P, st, aps, toks, S_len)
            toks["start"] = fin
        P.build(st)
    return nc


INPUT_SHAPES = {
    "L0_attn_W_ctrl": (1024, 6688), "L0_attn_b_ctrl": (6688,),
    "L0_attn_W_block": (512, 2, 2), "L0_attn_qconv_w": (1024, 4),
    "L0_attn_kconv_w": (1024, 4), "L0_attn_vconv_w": (1024, 4),
    "L0_attn_rms_m_w": (64,), "L0_attn_rms_v_w": (1024,),
    "L0_attn_W_out": (2048, 1024), "L0_attn_b_out": (1024,),
    "L0_ffn_router_w": (8, 1024), "L0_ffn_router_b": (8,),
    "L0_ffn_w1": (8, 1024, 4096), "L0_ffn_b1": (8, 4096),
    "L0_ffn_w2": (8, 4096, 1024), "L0_ffn_b2": (8, 1024),
    "L0_ffn_gamma": (1024,), "L0_ffn_beta": (1024,),
    "L1_attn_A": (1024,), "L1_attn_B": (1024, 1024),
    "L1_attn_W": (1024, 1024), "L1_attn_b": (1024,),
    "L1_ffn_kernel": (1024, 1024), "L1_ffn_bias": (1024,),
    "L1_ffn_a": (1024, 16), "L1_ffn_b": (16, 1024),
}

STAGE_EMITTERS = {
    "l0_attn": emit_l0_attn,
    "l1_ffn": emit_l1_ffn,
    "l1_attn": emit_l1_attn,
    "l0_ffn": emit_l0_ffn,
}


def kernel(**inputs):
    S_len = inputs["L0_attn_hidden"].shape[1]
    nc = build_program(S_len, ["l0_attn", "l0_ffn", "l1_attn", "l1_ffn"])
    consts = host_constants()
    in_maps = []
    for core in range(NCORES):
        m = {k: np.ascontiguousarray(inputs[k]) for k in INPUT_SHAPES}
        m.update(consts)
        m["x0"] = np.ascontiguousarray(inputs["L0_attn_hidden"][core])
        in_maps.append(m)
    res = run_bass_kernel_spmd(nc, in_maps, core_ids=list(range(NCORES)))
    out = np.stack([res.results[i]["xout"] for i in range(NCORES)], axis=0)
    return out.astype(np.float32)


# version 08 (final)
# speedup vs baseline: 1.3301x; 1.1007x over previous
import contextlib
import numpy as np
import concourse.bass as bass
import concourse.mybir as mybir
from concourse.bass_utils import run_bass_kernel_spmd

F32 = mybir.dt.float32
BF16 = mybir.dt.bfloat16
AF = mybir.ActivationFunctionType
ALU = mybir.AluOpType
AX = mybir.AxisListType

D = 1024
NCORES = 8

MAGIC = float(np.float32(1.5 * 2**23))
INV2PI = float(np.float32(1.0 / (2 * np.pi)))
TWOPI = float(np.float32(2 * np.pi))
HALFPI = float(np.float32(np.pi / 2))

ENGS = ("sp", "pe", "dve", "act", "pool")


class Prog:

    ROLL_CAP = 50000

    def __init__(self, nc):
        self.nc = nc
        self.steps = {e: [] for e in ENGS}
        self.counts = {}
        self.sem_names = []
        self.physmap = {}
        self.rollidx = {}

    def sem(self, name):
        if name not in self.counts:
            self.counts[name] = 0
            self.sem_names.append(name)
        return name

    def _phys(self, base):
        return self.physmap.get(base, base)

    def tok(self, base):
        p = self._phys(base)
        return (p, self.counts.get(p, 0))

    def emit(self, eng, fn, waits=(), inc=None, inc_n=1, drain=False,
             mmk=None):
        flat = []
        for w in waits:
            if w is None:
                continue
            if isinstance(w, list):
                flat.extend(x for x in w if x is not None)
            else:
                flat.append(w)
        waits = flat
        if inc is not None:
            phys = self._phys(inc)
            self.sem(phys)
            if self.counts[phys] + inc_n > self.ROLL_CAP:
                self.rollidx[inc] = self.rollidx.get(inc, 0) + 1
                phys = f"{inc}_r{self.rollidx[inc]}"
                self.physmap[inc] = phys
                self.sem(phys)
            self.counts[phys] += inc_n
            tok = (phys, self.counts[phys])
            inc = phys
        else:
            tok = None
        self.steps[eng].append((fn, waits, inc, inc_n if inc else 0, drain,
                                mmk))
        return tok

    def sp(self, fn, waits=(), stream="s0", drain=False):
        sem = "dma_" + stream
        self.sem(self._phys(sem))
        prev = self.tok(sem)
        prev = prev if prev[1] else None
        return self.emit("sp", fn, list(waits) + [prev], sem, 16, drain)

    def pooldma(self, fn, waits=(), stream="p0", drain=False):
        sem = "dma_" + stream
        self.sem(self._phys(sem))
        prev = self.tok(sem)
        prev = prev if prev[1] else None
        return self.emit("pool", fn, list(waits) + [prev], sem, 16, False)

    def pe(self, fn, waits=(), inc=None, drain=False, mmk=None):
        return self.emit("pe", fn, waits, inc, 1, drain, mmk)

    def dve(self, fn, waits=(), inc=None, drain=False):
        return self.emit("dve", fn, waits, inc, 1, drain)

    def act(self, fn, waits=(), inc=None, drain=False):
        return self.emit("act", fn, waits, inc, 1, drain)

    def pool(self, fn, waits=(), inc=None, inc_n=1, drain=True):
        return self.emit("pool", fn, waits, inc, inc_n, drain)

    def barrier(self, name="bar"):
        toks = []
        for e in ENGS:
            if self.steps[e]:
                toks.append(self.emit(e, lambda en: en.nop(), inc=self.sem(name + "_" + e)))
        for e in ENGS:
            if self.steps[e]:
                self.emit(e, lambda en: en.nop(), waits=[t for t in toks])
        return toks

    def build(self, st):
        nc = self.nc
        sems = {n: st.enter_context(nc.semaphore(n)) for n in self.sem_names}
        block = st.enter_context(nc.Block())

        pecfg_sem = st.enter_context(nc.semaphore("pecfg"))
        pecfg_n = [0]

        def _compat(a, b):
            if a is None or b is None:
                return True
            if a[1] == 1 or b[1] == 1:
                return True
            return a == b

        def mk_body(eng_name):
            def body(e):
                waited = {}
                last_mmk = [None]
                for fn, waits, inc, inc_n, drain, mmk in self.steps[eng_name]:
                    for sname, val in waits:
                        if waited.get(sname, 0) < val:
                            e.wait_ge(sems[sname], val)
                            waited[sname] = val
                    if (eng_name == "pe" and mmk is not None
                            and not _compat(last_mmk[0], mmk)):
                        pecfg_n[0] += 1
                        e.nop().then_inc(pecfg_sem, 1)
                        e.wait_ge(pecfg_sem, pecfg_n[0])
                    if mmk is not None:
                        last_mmk[0] = mmk
                    if drain:
                        e.drain()
                    inst = fn(e)
                    if inc is not None:
                        assert inst is not None, f"step for {inc} returned None"
                        inst.then_inc(sems[inc], inc_n)

            return body

        for eng_name, reg in (("sp", block.sync), ("pe", block.tensor),
                              ("dve", block.vector), ("act", block.scalar),
                              ("pool", block.gpsimd)):
            if self.steps[eng_name]:
                reg(mk_body(eng_name))


class BankRot:

    def __init__(self, pp, bank_tok, pool):
        self.pp = pp
        self.bank_tok = bank_tok
        self.pool = pool
        self.i = 0
        self.cur = None

    def alloc(self):
        b = self.pool[self.i % len(self.pool)]
        self.i += 1
        self.cur = b
        return self.pp[b], self.bank_tok[b]

    def set(self, tok):
        self.bank_tok[self.cur] = tok


class Arena:

    def __init__(self, nc, st, cols_f32):
        self.t = st.enter_context(nc.sbuf_tensor("arena", [128, cols_f32], F32))
        self.cols = cols_f32
        self.off = 0
        self.hi = 0

    def reset(self):
        self.hi = max(self.hi, self.off)
        self.off = 0

    def sb(self, name, shape, dtype=F32):
        P_ = shape[0]
        n_el = 1
        for d in shape[1:]:
            n_el *= d
        esz = 2 if dtype == BF16 else 4
        ncol = (n_el * esz + 31) // 32 * 8
        assert self.off + ncol <= self.cols, \
            f"arena overflow at {name}: {self.off + ncol} > {self.cols}"
        view = self.t[0:P_, self.off:self.off + ncol]
        self.off += ncol
        if dtype == BF16:
            view = view.bitcast(BF16)[:, 0:n_el]
        else:
            view = view[:, 0:n_el]
        if len(shape) > 2:
            view = view.rearrange(
                "p (" + " ".join(f"d{i}" for i in range(1, len(shape))) + ") -> p "
                + " ".join(f"d{i}" for i in range(1, len(shape))),
                **{f"d{i}": shape[i] for i in range(1, len(shape))})
        return view


def host_constants():
    import ml_dtypes

    c = {}
    c["c_ident_f"] = np.eye(128, dtype=np.float32)
    c["c_ident_b"] = np.eye(128, dtype=ml_dtypes.bfloat16)
    c["c_ones_row"] = np.ones((1, 512), dtype=np.float32)
    c["c_ones_row_b"] = np.ones((1, 512), dtype=ml_dtypes.bfloat16)
    tau = np.arange(128)[:, None]
    t = np.arange(128)[None, :]
    c["c_mask_incl"] = np.where(tau <= t, 0.0, -1e30).astype(np.float32)
    c["c_mask_strict"] = np.where(tau < t, 0.0, -1e30).astype(np.float32)
    sel = np.zeros((16, 16 * 128), dtype=np.float32)
    for h in range(16):
        sel[h, h * 128:(h + 1) * 128] = 1.0
    c["c_selh"] = sel
    blk = np.zeros((128, 8 * 16), dtype=np.float32)
    for i in range(8):
        for j2 in range(2):
            h = 2 * i + j2
            blk[j2 * 64:(j2 + 1) * 64, i * 16 + h] = 1.0
    c["c_blk16"] = blk
    misc = np.zeros((128, 8), dtype=np.float32)
    misc[:, 0] = 1e-5
    misc[:, 1] = 1e-6
    misc[:, 2] = 1e-12
    misc[:, 3] = 0.5
    misc[:, 4] = HALFPI
    c["c_misc"] = misc
    c["c_onesq"] = np.ones((128, 128), dtype=np.float32)
    c["c_ones_col"] = np.ones((128, 1), dtype=np.float32)
    return c


CONST_SHAPES = {k: v.shape for k, v in host_constants().items()}


def emit_prologue(P, st, aps):
    nc = P.nc
    sb = lambda n_, s_, d_=F32: st.enter_context(nc.sbuf_tensor(n_, s_, d_))
    toks = {}
    aps["sb_ident_f"] = sb("sb_ident_f", [128, 128])
    aps["sb_ident_b"] = sb("sb_ident_b", [128, 128], BF16)
    aps["sb_ones_row"] = sb("sb_ones_row", [1, 512])
    aps["sb_ones_row_b"] = sb("sb_ones_row_b", [1, 512], BF16)
    aps["sb_mask_incl"] = sb("sb_mask_incl", [128, 128])
    aps["sb_mask_strict"] = sb("sb_mask_strict", [128, 128])
    aps["sb_selh"] = sb("sb_selh", [16, 16 * 128])
    aps["sb_blk16"] = sb("sb_blk16", [128, 8 * 16])
    aps["sb_misc"] = sb("sb_misc", [128, 8])
    aps["sb_onesq"] = sb("sb_onesq", [128, 128])
    aps["sb_ones_col"] = sb("sb_ones_col", [128, 1])
    names = ["ident_f", "ident_b", "ones_row", "ones_row_b", "mask_incl",
             "mask_strict", "selh", "blk16", "misc", "onesq", "ones_col"]
    all_toks = []
    for n in names:
        tok = P.sp(lambda e, n=n: e.dma_start(aps["sb_" + n][:], aps["c_" + n]),
                   stream="cst")
        toks[n + "_ready"] = tok
        all_toks.append(tok)
    toks["consts_ready"] = all_toks[-1]
    return toks


def emit_ln_transpose(P, aps, xc, xln, xT, mcol, sqs, pbank, toks, tag,
                      war_xln=None, war_xT=None, war_psum=None, war_sqs=None,
                      t_x=None, eps=1e-5, transpose=True, xlnf=None,
                      xT_cols=None, rot=None):
    t_mean = P.dve(lambda e: e.tensor_reduce(mcol[:, 0:1], xc[:], AX.X, ALU.add),
                   waits=[t_x], inc="dve")
    t_sq = P.act(lambda e: e.activation(sqs[:], xc[:], AF.Square,
                                        accum_out=mcol[:, 1:2]),
                 waits=[t_x, war_sqs, t_mean], inc="act")
    t_st = P.dve(lambda e: e.tensor_scalar(mcol[:, 0:2], mcol[:, 0:2],
                                           1.0 / D, None, ALU.mult),
                 waits=[t_mean, t_sq], inc="dve")
    t_v1 = P.dve(lambda e: e.scalar_tensor_tensor(
        mcol[:, 1:2], mcol[:, 0:1], mcol[:, 0:1], mcol[:, 1:2],
        ALU.mult, ALU.subtract), waits=[t_st], inc="dve")
    eps_col = {1e-5: 0, 1e-6: 1, 1e-12: 2}[eps]
    t_v2 = P.act(lambda e: e.activation(mcol[:, 1:2], mcol[:, 1:2], AF.Sqrt,
                                        bias=aps["sb_misc"][:, eps_col:eps_col + 1],
                                        scale=-1.0),
                 waits=[t_v1, toks["misc_ready"]], inc="act")
    t_v3 = P.dve(lambda e: e.reciprocal(mcol[:, 1:2], mcol[:, 1:2]),
                 waits=[t_v2], inc="dve")
    if xlnf is not None:
        t_lnf = P.dve(lambda e: e.tensor_scalar(xlnf[:], xc[:], mcol[:, 0:1],
                                                mcol[:, 1:2], ALU.subtract,
                                                ALU.mult),
                      waits=[t_v3, war_xln], inc="dve")
        t_ln = P.dve(lambda e: e.tensor_copy(xln[:], xlnf[:]),
                     waits=[t_lnf], inc="dve")
    else:
        t_ln = P.dve(lambda e: e.tensor_scalar(xln[:], xc[:], mcol[:, 0:1],
                                               mcol[:, 1:2], ALU.subtract,
                                               ALU.mult),
                     waits=[t_v3, war_xln], inc="dve")
    if not transpose:
        return t_ln, None, []
    if xT_cols is None:
        xT_cols = lambda t: xT[:, t * 128:(t + 1) * 128]
    tr_toks = []
    if war_psum is None:
        war_psum = []
    elif not isinstance(war_psum, list):
        war_psum = [war_psum]
    t_prev_ev = war_psum
    for t in range(8):
        if rot is not None:
            bk, war = rot.alloc()
            view = bk[:, 0:512].bitcast(BF16)[:, 0:128]
        else:
            bk, war = None, None
            view = pbank[:, 0:256].bitcast(BF16)[:, 0:128]
        tt = P.pe(lambda e, t=t, view=view: e.transpose(
            view, xln[:, t * 128:(t + 1) * 128], aps["sb_ident_b"][:]),
            waits=[t_ln, toks["ident_b_ready"],
                   war_xT if t == 0 else None, war]
                  + (t_prev_ev if isinstance(t_prev_ev, list) else [t_prev_ev]
                     if rot is None else []),
            inc="pe", mmk=(0, 128))
        tev = P.dve(lambda e, t=t, view=view: e.tensor_copy(
            xT_cols(t), view), waits=[tt], inc="dve")
        if rot is not None:
            rot.set(tev)
        else:
            t_prev_ev = tev
        tr_toks.append(tev)
    return t_ln, tr_toks[-1], tr_toks


def emit_l1_ffn(P, st, aps, toks, S_len):
    nc = P.nc
    x_in = aps["x_in"]
    x_out = aps["x_out"]
    ar = aps["arena"]
    ar.reset()
    sb = lambda n_, s_, d_=F32: ar.sb(n_, s_, d_)
    pp = aps["psum"]

    NCH = S_len // 128

    wq = sb("f1_wq", [128, 8 * D], BF16)
    abf = sb("f1_a", [128, 8 * 16], BF16)
    bbf = sb("f1_b", [16, D], BF16)
    biasr = sb("f1_bias", [1, D], F32)
    stage = sb("f1_stage", [128, 2 * D], F32)
    sc = sb("f1_sc", [128, 2], F32)
    srow = sb("f1_srow", [1, 2], F32)
    tmp8 = sb("f1_tmp8", [128, 9], F32)
    xc = [sb(f"f1_xc{i}", [128, D], F32) for i in range(2)]
    sqs = sb("f1_sqs", [128, D], F32)
    xln = sb("f1_xln", [128, D], BF16)
    xT = [sb(f"f1_xT{i}", [128, D], BF16) for i in range(2)]
    xaT = sb("f1_xaT", [16, 128], BF16)
    mcol = sb("f1_mcol", [128, 2], F32)
    yc = [sb(f"f1_yc{i}", [128, D], F32) for i in range(2)]

    start = list(toks.get("start", []))

    t_bias = P.sp(lambda e: e.dma_start(biasr[:], aps["L1_ffn_bias"][None, :]),
                  waits=start, stream="cst")
    t_la = P.sp(lambda e: e.dma_start(
        stage[:, 0:128].rearrange("p (k r) -> p k r", r=16),
        aps["L1_ffn_a"].rearrange("(k p) r -> p k r", p=128)), waits=start,
        stream="cst")
    t_ca = P.dve(lambda e: e.tensor_copy(
        abf[:].rearrange("p (k r) -> p k r", r=16),
        stage[:, 0:128].rearrange("p (k r) -> p k r", r=16)),
        waits=[t_la], inc="dve")
    t_lb = P.sp(lambda e: e.dma_start(stage[0:16, 1024:1024 + D], aps["L1_ffn_b"]),
                waits=start, stream="cst")
    t_cb = P.dve(lambda e: e.tensor_copy(bbf[:], stage[0:16, 1024:1024 + D]),
                 waits=[t_lb], inc="dve")

    red_toks = []
    for k in range(8):
        w = [red_toks[k - 2]] if k >= 2 else ([t_ca, t_cb] + start)
        tk = P.sp(lambda e, k=k: e.dma_start(
            stage[:, (k % 2) * D:(k % 2 + 1) * D],
            aps["L1_ffn_kernel"][k * 128:(k + 1) * 128, :]), waits=w,
            stream=f"wt{k % 2}")
        tr = P.dve(lambda e, k=k: e.reduce_max(
            tmp8[:, k:k + 1], stage[:, (k % 2) * D:(k % 2 + 1) * D], AX.X,
            apply_absolute_value=True), waits=[tk], inc="dve")
        red_toks.append(tr)
    t_red = P.dve(lambda e: e.reduce_max(tmp8[:, 8:9], tmp8[:, 0:8], AX.X,
                                         apply_absolute_value=True),
                  waits=[red_toks[-1]], inc="dve")
    t_tr = P.pe(lambda e: e.transpose(pp[0][0:1, 0:128], tmp8[:, 8:9],
                                      aps["sb_ident_f"][:]),
                waits=[t_red, toks["ident_f_ready"]], inc="pe", mmk=(0, 128))
    t_mx = P.dve(lambda e: e.reduce_max(srow[0:1, 0:1], pp[0][0:1, 0:128], AX.X),
                 waits=[t_tr], inc="dve")
    t_sc = P.dve(lambda e: e.tensor_scalar(srow[0:1, 0:1], srow[0:1, 0:1],
                                           1.0 / 7.0, None, ALU.mult),
                 waits=[t_mx], inc="dve")
    t_rc = P.dve(lambda e: e.reciprocal(srow[0:1, 1:2], srow[0:1, 0:1]),
                 waits=[t_sc], inc="dve")
    t_bc = P.pe(lambda e: e.matmul(pp[0][:, 0:2], aps["sb_ones_row"][0:1, 0:128],
                                   srow[0:1, 0:2], start=True, stop=True),
                waits=[t_rc, toks["ones_row_ready"], t_mx], inc="pe", mmk=(0, 1))
    t_sc2 = P.dve(lambda e: e.tensor_copy(sc[:], pp[0][:, 0:2]),
                  waits=[t_bc], inc="dve")

    t_q_prev = [t_sc2]
    qt = []
    for k in range(8):
        sl = (k % 2) * D
        w = [qt[k - 2]] if k >= 2 else [t_red]
        tk = P.sp(lambda e, sl=sl, k=k: e.dma_start(
            stage[:, sl:sl + D], aps["L1_ffn_kernel"][k * 128:(k + 1) * 128, :]),
            waits=w + t_q_prev, stream=f"wt{k % 2}")
        t1 = P.dve(lambda e, sl=sl: e.tensor_scalar(
            stage[:, sl:sl + D], stage[:, sl:sl + D], sc[:, 1:2], MAGIC,
            ALU.mult, ALU.add), waits=[tk, t_sc2], inc="dve")
        t2 = P.dve(lambda e, sl=sl: e.tensor_scalar(
            stage[:, sl:sl + D], stage[:, sl:sl + D], MAGIC, None, ALU.subtract),
            waits=[t1], inc="dve")
        t3 = P.dve(lambda e, sl=sl: e.tensor_scalar(
            stage[:, sl:sl + D], stage[:, sl:sl + D], 7.0, -8.0, ALU.min,
            ALU.max), waits=[t2], inc="dve")
        t4 = P.dve(lambda e, sl=sl, k=k: e.tensor_scalar(
            wq[:, k * D:(k + 1) * D], stage[:, sl:sl + D], sc[:, 0:1], None,
            ALU.mult), waits=[t3], inc="dve")
        qt.append(t4)
    t_wq = qt[-1]

    ev = {}
    for ci in range(NCH):
        cb = ci % 2
        row = slice(ci * 128, (ci + 1) * 128)
        t_x = P.sp(lambda e, cb=cb, row=row: e.dma_start(
            xc[cb][:], x_in[row, :]),
            waits=[ev.get(("store", ci - 2))] + start, stream=f"x{cb}")
        t_ln, t_trl, _ = emit_ln_transpose(
            P, aps, xc[cb], xln, xT[cb], mcol, sqs, pp[6], toks, f"f1c{ci}",
            war_xln=ev.get(("mmlast", ci - 1)),
            war_xT=ev.get(("mmlast", ci - 2)),
            war_psum=ev.get(("trev", ci - 1)),
            war_sqs=None, t_x=t_x)
        ev[("trev", ci)] = t_trl

        for k in range(8):
            P.pe(lambda e, k=k, cb=cb: e.matmul(
                pp[7][0:16, 0:128], abf[:, k * 16:(k + 1) * 16],
                xT[cb][:, k * 128:(k + 1) * 128],
                start=(k == 0), stop=(k == 7)),
                waits=([t_trl, t_ca, ev.get(("xaT_ev", ci - 1))]
                       if k == 0 else []),
                inc=("pe" if k == 7 else None), mmk=(0, 128))
        t_xa = P.tok("pe")
        t_xae = P.dve(lambda e: e.tensor_copy(xaT[:], pp[7][0:16, 0:128]),
                      waits=[t_xa], inc="dve")
        ev[("xaT_ev", ci)] = t_xae

        mm_toks = []
        for g in range(2):
            for k in range(8):
                P.pe(lambda e, k=k, g=g, cb=cb: e.matmul(
                    pp[g][:, 0:512], xT[cb][:, k * 128:(k + 1) * 128],
                    wq[:, k * D + g * 512:k * D + (g + 1) * 512],
                    start=(k == 0), stop=False),
                    waits=([t_wq, t_trl, ev.get(("mm_ev", ci - 1, g))]
                           if k == 0 else []), mmk=(0, 128))
            P.pe(lambda e, g=g: e.matmul(
                pp[g][:, 0:512], aps["sb_ones_row"][0:1, 0:128],
                biasr[0:1, g * 512:(g + 1) * 512], start=False, stop=False),
                waits=[t_bias], mmk=(0, 1))
            tmm = P.pe(lambda e, g=g: e.matmul(
                pp[g][:, 0:512], xaT[0:16, 0:128],
                bbf[0:16, g * 512:(g + 1) * 512], start=False, stop=True),
                waits=[t_xae, t_cb], inc="pe", mmk=(0, 16))
            mm_toks.append(tmm)
        ev[("mmlast", ci)] = mm_toks[-1]

        evs = []
        for g in range(2):
            tev = P.dve(lambda e, g=g, cb=cb: e.tensor_add(
                yc[cb][:, g * 512:(g + 1) * 512], pp[g][:, 0:512],
                xc[cb][:, g * 512:(g + 1) * 512]),
                waits=[mm_toks[g], ev.get(("store", ci - 2))], inc="dve")
            ev[("mm_ev", ci, g)] = tev
            evs.append(tev)
        t_store = P.sp(lambda e, cb=cb, row=row: e.dma_start(
            x_out[row, :], yc[cb][:]), waits=evs, stream=f"st{cb}")
        ev[("store", ci)] = t_store

    fin = [P.tok("dma_st0"), P.tok("dma_st1")]
    fin = [t for t in fin if t[1] > 0]
    P.emit("sp", lambda e: e.nop(), waits=fin)
    return fin


def emit_l1_attn(P, st, aps, toks, S_len):
    nc = P.nc
    x_in = aps["x_in"]
    x_out = aps["x_out"]
    ar = aps["arena"]
    ar.reset()
    sb = lambda n_, s_, d_=F32: ar.sb(n_, s_, d_)
    pp = aps["psum"]
    NSUP = S_len // 512

    Bt = sb("a1_Bt", [128, 64 * 128], BF16)
    Wt = sb("a1_Wt", [128, 64 * 128], BF16)
    wband = [sb(f"a1_wband{i}", [128, 1024], F32) for i in range(2)]
    wbandb = sb("a1_wbandb", [128, 1024], BF16)
    acol = sb("a1_acol", [128, 8], F32)
    A_bc = sb("a1_Abc", [128, 8 * 512], F32)
    carry = sb("a1_carry", [128, 8], F32)
    brow = sb("a1_brow", [1, 1024], F32)
    xcs = [sb(f"a1_xcs{i}", [128, 4 * 1024], F32) for i in range(2)]
    xlnb = sb("a1_xlnb", [128, 1024], BF16)
    xT_s = [sb(f"a1_xT{i}", [128, 8 * 512], BF16) for i in range(2)]
    hT = sb("a1_hT", [128, 512], F32)
    hbf = [sb(f"a1_hbf{i}", [128, 8 * 512], BF16) for i in range(2)]
    mcol = sb("a1_mcol", [128, 2], F32)
    sqs = sb("a1_sqs", [128, 1024], F32)
    yc = [sb(f"a1_yc{i}", [128, 1024], F32) for i in range(2)]

    start = list(toks.get("start", []))

    a8r = sb("a1_a8r", [8, 128], F32)
    t_Al = P.sp(lambda e: e.dma_start(
        a8r[:], aps["L1_attn_A"].rearrange("(k p) -> k p", p=128)),
        waits=start, stream="cst")
    t_At = P.pe(lambda e: e.transpose(pp[5][:, 0:8], a8r[:],
                                      aps["sb_ident_f"][0:8, 0:8]),
                waits=[t_Al, toks["ident_f_ready"]], inc="pe", mmk=(0, 8))
    t_A = P.dve(lambda e: e.tensor_copy(acol[:], pp[5][:, 0:8]),
                waits=[t_At], inc="dve")
    abc_toks = []
    for i in range(8):
        tt = P.dve(lambda e, i=i: e.tensor_copy(
            A_bc[:, i * 512:(i + 1) * 512],
            acol[:, i:i + 1].broadcast_to((128, 512))), waits=[t_A], inc="dve")
        abc_toks.append(tt)
    t_abc = abc_toks[-1]
    t_cz = P.dve(lambda e: e.memset(carry[:], 0.0), waits=start, inc="dve")
    t_b = P.sp(lambda e: e.dma_start(brow[:], aps["L1_attn_b"][None, :]),
               waits=start, stream="cst")

    tr_done = []
    t_prev_ev = None
    for which, src, dstf in (
        ("B", aps["L1_attn_B"],
         lambda dt, nt: Bt[:, (dt * 8 + nt) * 128:(dt * 8 + nt) * 128 + 128]),
        ("W", aps["L1_attn_W"],
         lambda nt, ot: Wt[:, nt * 1024 + ot * 128: nt * 1024 + ot * 128 + 128]),
    ):
        for rt in range(8):
            t_ld = P.sp(lambda e, rt=rt, src=src: e.dma_start(
                wband[rt % 2][:], src[rt * 128:(rt + 1) * 128, :]),
                waits=[tr_done[-2]] if len(tr_done) >= 2 else start,
                stream=f"wt{rt % 2}")
            t_cast = P.dve(lambda e, rt=rt: e.tensor_copy(
                wbandb[:], wband[rt % 2][:]),
                waits=[t_ld, tr_done[-1] if tr_done else None], inc="dve")
            for ct in range(8):
                tt = P.pe(lambda e, ct=ct: e.transpose(
                    pp[6][:, 0:256].bitcast(BF16)[:, 0:128],
                    wbandb[:, ct * 128:(ct + 1) * 128], aps["sb_ident_b"][:]),
                    waits=[t_cast, toks["ident_b_ready"], t_prev_ev], inc="pe",
                    mmk=(0, 128))
                tev = P.dve(lambda e, ct=ct, rt=rt, dstf=dstf: e.tensor_copy(
                    dstf(ct, rt), pp[6][:, 0:256].bitcast(BF16)[:, 0:128]),
                    waits=[tt], inc="dve")
                t_prev_ev = tev
            tr_done.append(t_prev_ev)
    t_wt = tr_done[-1]

    ev = {}
    for sc in range(NSUP):
        par = sc % 2
        sub_ln = []
        for sub in range(4):
            ci = sc * 4 + sub
            cb = ci % 2
            t_x = P.sp(lambda e, par=par, sub=sub, ci=ci: e.dma_start(
                xcs[par][:, sub * 1024:(sub + 1) * 1024],
                x_in[ci * 128:(ci + 1) * 128, :]),
                waits=[ev.get(("sup_done", sc - 2))] + start, stream=f"x{cb}")
            t_ln, t_trl, _ = emit_ln_transpose(
                P, aps, xcs[par][:, sub * 1024:(sub + 1) * 1024], xlnb,
                None, mcol, sqs, pp[6], toks, f"a1c{ci}",
                war_xln=ev.get(("bmm", sc - 1)) if sub == 0 else None,
                war_xT=ev.get(("bmm", sc - 2)) if sub == 0 else None,
                war_psum=(t_wt if (sc == 0 and sub == 0) else
                          ev.get(("trev", sc, sub - 1)) or
                          ev.get(("trev", sc - 1, 3))),
                t_x=t_x,
                xT_cols=lambda t, par=par, sub=sub: xT_s[par][
                    :, t * 512 + sub * 128: t * 512 + sub * 128 + 128])
            ev[("trev", sc, sub)] = t_trl
            sub_ln.append((t_ln, t_trl))
        t_tr_all = sub_ln[-1][1]

        scan_toks = []
        t_bmm_last = None
        for i in range(8):
            bank = i % 2
            w0 = [t_tr_all, t_wt, ev.get(("scan", sc, i - 2)),
                  ev.get(("scan", sc - 1, 6 + bank))]
            for k in range(8):
                tok = P.pe(lambda e, i=i, k=k, par=par, bank=bank: e.matmul(
                    pp[bank][:, 0:512],
                    Bt[:, (k * 8 + i) * 128:(k * 8 + i) * 128 + 128],
                    xT_s[par][:, k * 512:(k + 1) * 512],
                    start=(k == 0), stop=(k == 7)),
                    waits=w0 if k == 0 else [],
                    inc=("pe" if k == 7 else None), mmk=(0, 128))
            t_bmm_last = P.tok("pe")
            t_scan = P.dve(lambda e, i=i, par=par, bank=bank: e.tensor_tensor_scan(
                hT[:], A_bc[:, i * 512:(i + 1) * 512], pp[bank][:, 0:512],
                carry[:, i:i + 1], ALU.mult, ALU.add),
                waits=[t_bmm_last, t_abc, t_cz,
                       (scan_toks[-1] if scan_toks
                        else ev.get(("hcast", sc - 1, 7))),
                       ev.get(("hcast", sc - 1, i))], inc="dve")
            t_cy = P.dve(lambda e, i=i: e.tensor_copy(
                carry[:, i:i + 1], hT[:, 511:512]), waits=[t_scan], inc="dve")
            t_cast = P.dve(lambda e, i=i, par=par: e.tensor_copy(
                hbf[par][:, i * 512:(i + 1) * 512], hT[:]),
                waits=[t_cy, ev.get(("omm", sc - 2))], inc="dve")
            ev[("scan", sc, i)] = t_scan
            ev[("hcast", sc, i)] = t_cast
            scan_toks.append(t_cast)
        ev[("bmm", sc)] = t_bmm_last

        sub_evs = []
        for sub in range(4):
            ci = sc * 4 + sub
            cb = ci % 2
            mm_toks = []
            for g in range(2):
                bank = 2 + g
                w0 = [scan_toks[-1], ev.get(("omm_ev", sc, sub - 1, g)),
                      ev.get(("omm_ev", sc - 1, 3, g))]
                for nt in range(8):
                    P.pe(lambda e, nt=nt, g=g, par=par, sub=sub, bank=bank:
                         e.matmul(
                             pp[bank][:, 0:512],
                             hbf[par][:, nt * 512 + sub * 128:
                                      nt * 512 + sub * 128 + 128],
                             Wt[:, nt * 1024 + g * 512: nt * 1024 + (g + 1) * 512],
                             start=(nt == 0), stop=False),
                         waits=w0 if nt == 0 else [], mmk=(0, 128))
                tmm = P.pe(lambda e, g=g, bank=bank: e.matmul(
                    pp[bank][:, 0:512], aps["sb_ones_row"][0:1, 0:128],
                    brow[0:1, g * 512:(g + 1) * 512], start=False, stop=True),
                    waits=[t_b], inc="pe", mmk=(0, 1))
                mm_toks.append(tmm)
            ev[("omm", sc)] = mm_toks[-1]
            evs = []
            for g in range(2):
                tev = P.dve(lambda e, g=g, par=par, sub=sub, cb=cb: e.tensor_add(
                    yc[cb][:, g * 512:(g + 1) * 512], pp[2 + g][:, 0:512],
                    xcs[par][:, sub * 1024 + g * 512: sub * 1024 + (g + 1) * 512]),
                    waits=[mm_toks[g], ev.get(("store", ci - 2))], inc="dve")
                ev[("omm_ev", sc, sub, g)] = tev
                evs.append(tev)
            t_store = P.sp(lambda e, cb=cb, ci=ci: e.dma_start(
                x_out[ci * 128:(ci + 1) * 128, :], yc[cb][:]),
                waits=evs, stream=f"st{cb}")
            ev[("store", ci)] = t_store
            sub_evs.append(t_store)
        ev[("sup_done", sc)] = sub_evs[-1]

    fin = [P.tok("dma_st0"), P.tok("dma_st1")]
    fin = [t for t in fin if t[1] > 0]
    P.emit("sp", lambda e: e.nop(), waits=fin)
    return fin


def emit_l0_ffn(P, st, aps, toks, S_len):
    nc = P.nc
    x_in = aps["x_in"]
    x_out = aps["x_out"]
    ar = aps["arena"]
    ar.reset()
    sb = lambda n_, s_, d_=F32: ar.sb(n_, s_, d_)
    pp = aps["psum"]
    NCH = S_len // 128
    F = 4096

    w1sb = sb("f0_w1", [128, 8 * F], BF16)
    w2sb = sb("f0_w2", [128, 32 * 1024], BF16)
    wband = [sb(f"f0_wband{i}", [128, 1024], F32) for i in range(2)]
    b1col = sb("f0_b1col", [128, 32], F32)
    b2row = sb("f0_b2row", [1, 1024], F32)
    gb_bc = sb("f0_gbbc", [128, 2 * 1024], F32)
    xc = [sb(f"f0_xc{i}", [128, 1024], F32) for i in range(2)]
    xlnf = [sb(f"f0_xlnf{i}", [128, 1024], F32) for i in range(2)]
    xlnb = sb("f0_xlnb", [128, 1024], BF16)
    xT = [sb(f"f0_xT{i}", [128, 1024], BF16) for i in range(2)]
    hbf0 = sb("f0_hbf0", [128, 32 * 128], BF16)
    hbf = [hbf0, hbf0]
    x2 = sb("f0_x2", [128, 1024], F32)
    mcol = sb("f0_mcol", [128, 2], F32)
    mcol2 = sb("f0_mcol2", [128, 2], F32)
    yc = [sb(f"f0_yc{i}", [128, 1024], F32) for i in range(2)]
    grow = wband[0][0:1, 0:1024]
    grow2 = wband[1][0:1, 0:1024]

    start = list(toks.get("start", []))

    b1r = sb("f0_b1r", [32, 128], F32)
    t_b1l = P.sp(lambda e: e.dma_start(
        b1r[:], aps["L0_ffn_b1"][0:1, :].rearrange("o (f p) -> (o f) p", p=128)),
        waits=start, stream="cst")
    t_b1t = P.pe(lambda e: e.transpose(pp[5][:, 0:32], b1r[:],
                                       aps["sb_ident_f"][0:32, 0:32]),
                 waits=[t_b1l, toks["ident_f_ready"]], inc="pe", mmk=(0, 32))
    t_b1 = P.dve(lambda e: e.tensor_copy(b1col[:], pp[5][:, 0:32]),
                 waits=[t_b1t], inc="dve")
    t_b2 = P.sp(lambda e: e.dma_start(b2row[:], aps["L0_ffn_b2"][0:1, :]),
                waits=start, stream="cst")
    t_g = P.sp(lambda e: e.dma_start(grow[0:1, 0:1024], aps["L0_ffn_gamma"][None, :]),
               waits=start, stream="cst")
    t_be = P.sp(lambda e: e.dma_start(grow2[0:1, 0:1024], aps["L0_ffn_beta"][None, :]),
                waits=start, stream="cst")
    gb_toks = []
    for j in range(4):
        src = (grow[0:1, j * 512:(j + 1) * 512] if j < 2 else
               grow2[0:1, (j - 2) * 512:(j - 1) * 512])
        tmm = P.pe(lambda e, j=j, src=src: e.matmul(
            pp[7][:, 0:512], aps["sb_ones_row"][0:1, 0:128],
            src, start=True, stop=True),
            waits=[t_g, t_be, toks["ones_row_ready"],
                   gb_toks[-1] if gb_toks else None], inc="pe", mmk=(0, 1))
        tev = P.dve(lambda e, j=j: e.tensor_copy(
            gb_bc[:, j * 512:(j + 1) * 512], pp[7][:, 0:512]),
            waits=[tmm], inc="dve")
        gb_toks.append(tev)
    t_gb = gb_toks[-1]

    wt_toks = []
    for dt in range(8):
        for hh in range(4):
            i2 = dt * 4 + hh
            t_ld = P.sp(lambda e, dt=dt, hh=hh, i2=i2: e.dma_start(
                wband[i2 % 2][:],
                aps["L0_ffn_w1"][0:1, dt * 128:(dt + 1) * 128,
                                 hh * 1024:(hh + 1) * 1024].squeeze(0)),
                waits=[wt_toks[-2]] if len(wt_toks) >= 2 else (start + [t_gb]),
                stream=f"wt{i2 % 2}")
            t_cast = P.dve(lambda e, dt=dt, hh=hh, i2=i2: e.tensor_copy(
                w1sb[:, dt * F + hh * 1024: dt * F + (hh + 1) * 1024],
                wband[i2 % 2][:]),
                waits=[t_ld], inc="dve")
            wt_toks.append(t_cast)
    for ft1 in range(32):
        t_ld = P.sp(lambda e, ft1=ft1: e.dma_start(
            wband[ft1 % 2][:],
            aps["L0_ffn_w2"][0:1, ft1 * 128:(ft1 + 1) * 128, :].squeeze(0)),
            waits=[wt_toks[-2]] if len(wt_toks) >= 2 else (start + [t_gb]),
            stream=f"wt{ft1 % 2}")
        t_cast = P.dve(lambda e, ft1=ft1: e.tensor_copy(
            w2sb[:, ft1 * 1024:(ft1 + 1) * 1024],
            wband[ft1 % 2][:]),
            waits=[t_ld], inc="dve")
        wt_toks.append(t_cast)
    t_w = wt_toks[-1]

    ev = {}
    for ci in range(NCH):
        cb = ci % 2
        row = slice(ci * 128, (ci + 1) * 128)
        t_x = P.sp(lambda e, cb=cb, row=row: e.dma_start(
            xc[cb][:], x_in[row, :]),
            waits=[ev.get(("store", ci - 2))] + start, stream=f"x{cb}")
        t_ln, t_trl, _ = emit_ln_transpose(
            P, aps, xc[cb], xlnb, xT[cb], mcol, yc[cb], pp[6], toks, f"f0c{ci}",
            war_xln=ev.get(("h_mm", ci - 1)),
            war_xT=ev.get(("h_mm", ci - 2)),
            war_psum=([ev.get(("hev", ci - 1, 30)), ev.get(("hev", ci - 1, 31))]
                      if ci > 0 else [t_gb]),
            war_sqs=ev.get(("store", ci - 2)),
            t_x=t_x, xlnf=xlnf[cb])
        ev[("trev", ci)] = t_trl

        hev = []
        bankslot = []
        for ft in range(32):
            if ft < 24:
                bankslot.append((ft // 4, ft % 4))
            else:
                bankslot.append((6 + (ft - 24) // 4, (ft - 24) % 4))
        hmm_toks = []
        for ft in range(32):
            bank, slot = bankslot[ft]
            w0 = [t_trl, t_w, ev.get(("hev", ci - 1, ft))]
            for k in range(8):
                P.pe(lambda e, ft=ft, k=k, cb=cb, bank=bank, slot=slot: e.matmul(
                    pp[bank][:, slot * 128:(slot + 1) * 128],
                    w1sb[:, k * F + ft * 128: k * F + (ft + 1) * 128],
                    xT[cb][:, k * 128:(k + 1) * 128],
                    start=(k == 0), stop=(k == 7)),
                    waits=w0 if k == 0 else [],
                    inc=("pe" if k == 7 else None), mmk=(0, 128))
            hmm_toks.append(P.tok("pe"))
        bank_last = {}
        for ft in range(32):
            bank_last[bankslot[ft][0]] = hmm_toks[ft]
        for ft in range(32):
            bank, slot = bankslot[ft]
            w0 = [bank_last[bank], t_b1, ev.get(("w2mm", ci - 1))]
            if bank % 2 == 0:
                tev = P.dve(lambda e, ft=ft, cb=cb, bank=bank, slot=slot:
                            e.tensor_scalar(
                                hbf[cb][:, ft * 128:(ft + 1) * 128],
                                pp[bank][:, slot * 128:(slot + 1) * 128],
                                b1col[:, ft:ft + 1], 0.0, ALU.add, ALU.max),
                            waits=w0, inc="dve")
            else:
                tev = P.act(lambda e, ft=ft, cb=cb, bank=bank, slot=slot:
                            e.activation(
                                hbf[cb][:, ft * 128:(ft + 1) * 128],
                                pp[bank][:, slot * 128:(slot + 1) * 128],
                                AF.Relu, bias=b1col[:, ft:ft + 1]),
                            waits=w0, inc="act")
            ev[("hev", ci, ft)] = tev
            hev.append(tev)
        ev[("h_mm", ci)] = P.tok("pe")

        mm_toks = []
        for g in range(2):
            bank = g
            w0 = hev + [ev.get(("x2ev", ci - 1, g))]
            for ft in range(32):
                P.pe(lambda e, ft=ft, g=g, cb=cb, bank=bank: e.matmul(
                    pp[bank][:, 0:512],
                    hbf[cb][:, ft * 128:(ft + 1) * 128],
                    w2sb[:, ft * 1024 + g * 512: ft * 1024 + (g + 1) * 512],
                    start=(ft == 0), stop=False),
                    waits=w0 if ft == 0 else [], mmk=(0, 128))
            tmm = P.pe(lambda e, g=g, bank=bank: e.matmul(
                pp[bank][:, 0:512], aps["sb_ones_row"][0:1, 0:128],
                b2row[0:1, g * 512:(g + 1) * 512], start=False, stop=True),
                waits=[t_b2], inc="pe", mmk=(0, 1))
            mm_toks.append(tmm)
        ev[("w2mm", ci)] = mm_toks[-1]

        x2ev = []
        for g in range(2):
            tev = P.dve(lambda e, g=g, cb=cb: e.tensor_add(
                x2[:, g * 512:(g + 1) * 512], pp[g][:, 0:512],
                xlnf[cb][:, g * 512:(g + 1) * 512]),
                waits=[mm_toks[g]], inc="dve")
            ev[("x2ev", ci, g)] = tev
            x2ev.append(tev)
        t_mean = P.dve(lambda e: e.tensor_reduce(
            mcol2[:, 0:1], x2[:], AX.X, ALU.add), waits=x2ev, inc="dve")
        t_sq = P.act(lambda e, cb=cb: e.activation(yc[cb][:], x2[:], AF.Square,
                                                   accum_out=mcol2[:, 1:2]),
                     waits=x2ev + [t_mean], inc="act")
        t_st2 = P.dve(lambda e: e.tensor_scalar(
            mcol2[:, 0:2], mcol2[:, 0:2], 1.0 / D, None, ALU.mult),
            waits=[t_mean, t_sq], inc="dve")
        t_v1 = P.dve(lambda e: e.scalar_tensor_tensor(
            mcol2[:, 1:2], mcol2[:, 0:1], mcol2[:, 0:1], mcol2[:, 1:2],
            ALU.mult, ALU.subtract), waits=[t_st2], inc="dve")
        t_v2 = P.act(lambda e: e.activation(
            mcol2[:, 1:2], mcol2[:, 1:2], AF.Sqrt,
            bias=aps["sb_misc"][:, 0:1], scale=-1.0),
            waits=[t_v1, toks["misc_ready"]], inc="act")
        t_v3 = P.dve(lambda e: e.reciprocal(mcol2[:, 1:2], mcol2[:, 1:2]),
                     waits=[t_v2], inc="dve")
        t_n = P.dve(lambda e: e.tensor_scalar(
            x2[:], x2[:], mcol2[:, 0:1], mcol2[:, 1:2], ALU.subtract, ALU.mult),
            waits=[t_v3], inc="dve")
        t_gm = P.dve(lambda e: e.tensor_mul(x2[:], x2[:], gb_bc[:, 0:1024]),
                     waits=[t_n, t_gb], inc="dve")
        t_bt = P.dve(lambda e: e.tensor_add(x2[:], x2[:], gb_bc[:, 1024:2048]),
                     waits=[t_gm], inc="dve")
        t_y = P.dve(lambda e, cb=cb: e.tensor_add(yc[cb][:], x2[:], xc[cb][:]),
                    waits=[t_bt, ev.get(("store", ci - 2))], inc="dve")
        t_store = P.sp(lambda e, cb=cb, row=row: e.dma_start(
            x_out[row, :], yc[cb][:]), waits=[t_y], stream=f"st{cb}")
        ev[("store", ci)] = t_store

    fin = [P.tok("dma_st0"), P.tok("dma_st1")]
    fin = [t for t in fin if t[1] > 0]
    P.emit("sp", lambda e: e.nop(), waits=fin)
    return fin


def emit_l0_attn(P, st, aps, toks, S_len, dbg=None):
    import os
    L0CUT = int(os.environ.get("L0CUT", "99"))
    nc = P.nc
    x_in = aps["x_in"]
    x_out = aps["x_out"]
    ar = aps["arena"]
    ar.reset()
    sb = lambda n_, s_, d_=F32: ar.sb(n_, s_, d_)
    pp = aps["psum"]
    NSUP = S_len // 512
    NCTRL = 6688
    NG = 53

    wcbf = nc.dram_tensor("wcbf", [1024, NCTRL], BF16, kind="Internal").ap()
    ogbuf = nc.dram_tensor("ogbuf", [S_len, 1024], BF16, kind="Internal").ap()

    wband = [sb(f"a0_wband{i}", [128, 1024], F32) for i in range(2)]
    wo = sb("a0_wo", [128, 16 * 1024], BF16)
    wcp = [sb(f"a0_wcp{i}", [128, 8 * 128], BF16) for i in range(2)]
    bgates = sb("a0_bgates", [1, 3616], BF16)
    bout = sb("a0_bout", [1, 1024], BF16)
    rmm = sb("a0_rmm", [128, 1], F32)
    rmv = sb("a0_rmv", [128, 8], F32)
    wcv = sb("a0_wcv", [128, 3 * 32], F32)
    bq24r = sb("a0_bq24r", [24, 128], F32)
    bq24 = sb("a0_bq24", [128, 24], F32)
    sbias = sb("a0_sbias", [128, 24], F32)
    wbc = sb("a0_wbc", [128, 16], F32)
    bt4 = [sb(f"a0_bt{i}", [128, 4 * 128], BF16) for i in range(4)]
    win = [sb(f"a0_win{i}", [128, 8 * 515], BF16) for i in range(3)]
    xc = [sb(f"a0_xc{i}", [128, 1024], F32) for i in range(2)]
    xtreg = sb("a0_xtreg", [128, 2048], F32)
    xT4 = [xtreg[:, i * 512:(i + 1) * 512].bitcast(BF16)[:, 0:1024]
           for i in range(4)]
    xlnb = sb("a0_xlnb", [128, 1024], BF16)
    mcol = sb("a0_mcol", [128, 2], F32)
    sqs = sb("a0_sqs", [128, 1024], F32)
    upd4 = sb("a0_upd4", [128, 4 * 1024], BF16)
    cnd4 = sb("a0_cnd4", [128, 4 * 1024], BF16)
    ang4 = sb("a0_ang4", [128, 4 * 512], BF16)
    ogst = sb("a0_ogst", [128, 1024], BF16)
    abe = sb("a0_abe", [128, 4 * 32], F32)
    qkv256 = sb("a0_qkv256", [128, 256], BF16)
    rc = sb("a0_rc", [128, 1024], F32)
    trg = sb("a0_trg", [128, 1024], F32)
    wbandb = trg[:, 0:512].bitcast(BF16)[:, 0:1024]

    vin = [sb(f"a0_vin{i}", [128, 4 * 128], F32) for i in range(4)]
    plA = [sb(f"a0_plA{i}", [128, 4 * 128], F32) for i in range(6)]
    plB4 = sb("a0_plB4", [128, 4 * 128], F32)
    plB5 = sb("a0_plB5", [128, 4 * 128], F32)
    plB = vin + [plB4, plB5]
    vx0 = sb("a0_vx0", [128, 4 * 9], F32)
    vy0 = sb("a0_vy0", [128, 4 * 9], F32)
    vecbf = [sb(f"a0_vecbf{i}", [128, 4 * 512], BF16) for i in range(2)]
    cl16 = sb("a0_cl16", [16, 128], F32)
    lnahm = sb("a0_lnahm", [16, 128], F32)
    skhm = sb("a0_skhm", [16, 128], F32)
    khat = sb("a0_khat", [128, 128], F32)
    clm16 = sb("a0_clm16", [16, 128], F32)
    hm = sb("a0_hm", [128, 128], F32)
    tok16 = sb("a0_tok16", [128, 128], F32)
    lna = sb("a0_lna", [128, 16], F32)
    beta = sb("a0_beta", [128, 16], F32)
    sktok = sb("a0_sktok", [128, 16], F32)
    fA = sb("a0_fA", [128, 16], F32)
    fash = sb("a0_fash", [128, 16], F32)
    fstate = sb("a0_fstate", [128, 16], F32)
    small6 = sb("a0_small6", [128, 16], F32)
    NTp = [sb(f"a0_NT{i}", [128, 128], F32) for i in range(2)]
    nkpP = [[sb(f"a0_nkp{p_}_{i}", [128, 128], F32) for i in range(2)]
            for p_ in range(2)]
    nksP = [[sb(f"a0_nks{p_}_{i}", [128, 128], F32) for i in range(2)]
            for p_ in range(2)]
    lamzP = [sb(f"a0_lamz{i}", [128, 128], BF16) for i in range(2)]
    vTg = sb("a0_vTg", [128, 512], F32)
    vT = vTg[:, 0:512].bitcast(BF16)[:, 0:1024]
    gate = vTg[:, 0:512]
    kTm = sb("a0_kTm", [128, 512], F32)
    kT = kTm[:, 0:512].bitcast(BF16)[:, 0:1024]
    Rt = xtreg[:, 0:1024]
    YtP = [[sb(f"a0_Yt{p_}_{i}", [128, 64], F32) for i in range(2)]
           for p_ in range(2)]
    Wt_ = sb("a0_Wt", [128, 16 * 64], BF16)
    kstP = [sb(f"a0_kst{i}", [128, 64], BF16) for i in range(2)]
    acbc = sb("a0_acbc", [128, 16], F32)
    acrow = sb("a0_acrow", [1, 16], F32)
    S0 = sb("a0_S0", [128, 8 * 64], F32)
    S0b = sb("a0_S0b", [128, 8 * 64], BF16)

    mT = xtreg[:, 1024:1536].bitcast(BF16)[:, 0:1024]
    rstdv = sb("a0_rstdv", [128, 1], F32)
    ocomb = sb("a0_ocomb", [128, 1024], F32)
    mtile = kTm[:, 0:512].bitcast(BF16)[:, 0:1024]
    acc = [ocomb[:, 0:512], ocomb[:, 512:1024]]
    yc = [rc, trg]

    reads = wband[0]
    rscr = wband[1]
    lgscrP = [rscr[:, 0:256], rscr[:, 256:512]]
    cscr1P = [rscr[:, 512:576], rscr[:, 576:640]]
    cscr2P = [rscr[:, 640:704], rscr[:, 704:768]]
    start = list(toks.get("start", []))
    ev = {}
    IDF = aps["sb_ident_f"]
    IDB = aps["sb_ident_b"]
    ONESR = aps["sb_ones_row"]
    ONESRB = aps["sb_ones_row_b"]
    ONESQ = aps["sb_onesq"]
    MISC = aps["sb_misc"]
    MSKI = aps["sb_mask_incl"]
    MSKS = aps["sb_mask_strict"]
    SELH = aps["sb_selh"]
    BLK16 = aps["sb_blk16"]

    def join(tag):
        return [P.tok(s_) for s_ in ("pe", "dve", "act", "pool")
                if P.tok(s_)[1] > 0]

    pk_toks = []
    for k in range(8):
        for cc in range(7):
            w_ = min(1024, NCTRL - cc * 1024)
            i2 = k * 7 + cc
            tl = P.sp(lambda e, k=k, cc=cc, w_=w_, i2=i2: e.dma_start(
                wband[i2 % 2][:, 0:w_],
                aps["L0_attn_W_ctrl"][k * 128:(k + 1) * 128,
                                      cc * 1024:cc * 1024 + w_]),
                waits=[pk_toks[-2][0]] if len(pk_toks) >= 2 else start,
                stream=f"wt{i2 % 2}")
            tc = P.dve(lambda e, w_=w_, i2=i2: e.tensor_copy(
                wbandb[:, 0:w_], wband[i2 % 2][:, 0:w_]),
                waits=[tl, pk_toks[-1][1] if pk_toks else None], inc="dve")
            ts_ = P.sp(lambda e, k=k, cc=cc, w_=w_: e.dma_start(
                wcbf[k * 128:(k + 1) * 128, cc * 1024:cc * 1024 + w_],
                wbandb[:, 0:w_]), waits=[tc], stream="wcst")
            pk_toks.append((tc, ts_))
    t_wc_done = pk_toks[-1][1]
    t_wc_done2 = pk_toks[-1][1]

    t_rmm0 = P.sp(lambda e: e.dma_start(
        rmm[0:64, 0:1], aps["L0_attn_rms_m_w"].unsqueeze(1)),
        waits=start, stream="cst")
    t_rmm = P.sp(lambda e: e.dma_start(
        rmm[64:128, 0:1], aps["L0_attn_rms_m_w"].unsqueeze(1)),
        waits=start, stream="cst")
    t_rmv = P.sp(lambda e: e.dma_start(
        rmv[:].rearrange("p (s g) -> p s g", s=2),
        aps["L0_attn_rms_v_w"].rearrange("(g p s) -> p s g", p=128, s=2),
        allow_slow_non_contiguous=True), waits=start, stream="cst")

    wo_toks = []
    for mt_ in range(8):
        tl = P.sp(lambda e, mt_=mt_: e.dma_start(
            wband[mt_ % 2][:], aps["L0_attn_W_out"][mt_ * 128:(mt_ + 1) * 128, :]),
            waits=[wo_toks[-2]] if len(wo_toks) >= 2 else [t_wc_done, t_rmm],
            stream=f"wt{mt_ % 2}")
        tc = P.dve(lambda e, mt_=mt_: e.tensor_scalar(
            wo[:, mt_ * 1024:(mt_ + 1) * 1024], wband[mt_ % 2][:],
            rmm[:, 0:1], None, ALU.mult), waits=[tl], inc="dve")
        wo_toks.append(tc)
    for tv in range(8):
        sig = tv // 4
        pg = tv % 4
        tl = P.sp(lambda e, sig=sig, pg=pg, tv=tv: e.dma_start(
            wband[tv % 2][:],
            aps["L0_attn_W_out"][1024 + pg * 256 + sig:
                                 1024 + (pg + 1) * 256:2, :]),
            waits=[wo_toks[-2]], stream=f"wt{tv % 2}")
        tc = P.dve(lambda e, tv=tv: e.tensor_scalar(
            wo[:, (8 + tv) * 1024:(9 + tv) * 1024], wband[tv % 2][:],
            rmv[:, tv:tv + 1], None, ALU.mult), waits=[tl, t_rmv], inc="dve")
        wo_toks.append(tc)
    t_wo = wo_toks[-1]

    t_bg1 = P.pooldma(lambda e: e.dma_start(bgates[0:1, 0:2560],
                                            aps["L0_attn_b_ctrl"][None, 0:2560]),
                      waits=start, stream="pcst")
    t_bg = P.pooldma(lambda e: e.dma_start(bgates[0:1, 2560:3616],
                                           aps["L0_attn_b_ctrl"][None, 5632:6688]),
                     waits=start, stream="pcst")
    t_bo = P.pooldma(lambda e: e.dma_start(bout[:], aps["L0_attn_b_out"][None, :]),
                     waits=start, stream="pcst")
    t_bqr = P.sp(lambda e: e.dma_start(
        bq24r[:], aps["L0_attn_b_ctrl"][2560:5632].rearrange("(j p) -> j p", p=128)),
        waits=start, stream="cst")
    t_bqt = P.pe(lambda e: e.transpose(pp[5][:, 0:24], bq24r[:],
                                       IDF[0:24, 0:24]),
                 waits=[t_bqr, toks["ident_f_ready"]], inc="pe", mmk=(0, 24))
    t_bq = P.dve(lambda e: e.tensor_copy(bq24[:], pp[5][:, 0:24]),
                 waits=[t_bqt], inc="dve")

    cv_toks = []
    for cj, nm in enumerate(["qconv_w", "kconv_w", "vconv_w"]):
        tl = P.sp(lambda e, cj=cj, nm=nm: e.dma_start(
            wcv[:, cj * 32:(cj + 1) * 32].rearrange("p (c t) -> p c t", t=4),
            aps["L0_attn_" + nm].rearrange("(c p) t -> p c t", p=128)),
            waits=start, stream="cst")
        tsum = P.dve(lambda e, cj=cj: e.tensor_reduce(
            sqs[:, cj * 8:(cj + 1) * 8].unsqueeze(2),
            wcv[:, cj * 32:(cj + 1) * 32].rearrange("p (c t) -> p c t", t=4),
            AX.X, ALU.add), waits=[tl], inc="dve")
        cv_toks.append(tsum)
    t_sbias = P.dve(lambda e: e.tensor_mul(sbias[:], bq24[:], sqs[:, 0:24]),
                    waits=cv_toks + [t_bq], inc="dve")
    t_cv = cv_toks[-1]

    t_wb = P.sp(lambda e: e.dma_start(
        wbc[:].rearrange("p (g i j) -> p g i j", i=2, j=2),
        aps["L0_attn_W_block"].rearrange("(g p) i j -> p g i j", p=128)),
        waits=start, stream="cst")
    bt_toks = []
    for ii in range(2):
        for jj in range(2):
            idx = ii * 2 + jj
            tt = P.dve(lambda e, ii=ii, jj=jj, idx=idx: e.tensor_copy(
                bt4[idx][:].rearrange("p (g t) -> p g t", t=128),
                wbc[:].rearrange("p (g i j) -> p g i j", i=2, j=2)
                [:, :, ii, jj].unsqueeze(2).broadcast_to((128, 4, 128))),
                waits=[t_wb], inc="dve")
            bt_toks.append(tt)
    t_bt = bt_toks[-1]

    t_s0 = P.dve(lambda e: e.memset(S0[:], 0.0), waits=start, inc="dve")
    t_hm0 = P.dve(lambda e: e.memset(hm[:], 0.0), waits=start, inc="dve")
    t_s0b = P.dve(lambda e: e.memset(S0b[:], 0.0), waits=start, inc="dve")
    hsl = lambda h: slice((h % 2) * 64, (h % 2) * 64 + 64)
    csl = lambda h: slice((h // 2) * 64, (h // 2) * 64 + 64)
    t_vc = P.dve(lambda e: e.memset(vx0[:], 0.0), waits=start, inc="dve")
    t_vc2 = P.dve(lambda e: e.memset(vy0[:], 0.0), waits=start, inc="dve")
    tail_toks = []
    for cj in range(3):
        tt = P.dve(lambda e, cj=cj: e.tensor_scalar(
            win[cj][:].rearrange("p (c t) -> p c t", t=515)[:, :, 0:3],
            bq24[:, cj * 8:(cj + 1) * 8].unsqueeze(2).broadcast_to((128, 8, 3)),
            -1.0, None, ALU.mult), waits=[t_bq] + start, inc="dve")
        tail_toks.append(tt)
    t_tails = tail_toks[-1]

    pp6_last = [toks["consts_ready"]]
    bank_tok = {b_: toks["consts_ready"] for b_ in (2, 3, 4, 5, 6, 7)}
    rot_a1 = BankRot(pp, bank_tok, [6, 7])
    rot_qkv = BankRot(pp, bank_tok, [6, 7])
    rot_vg = BankRot(pp, bank_tok, [4, 5])
    rot_vk = BankRot(pp, bank_tok, [6, 7])
    rot_mt = BankRot(pp, bank_tok, [6, 7])

    def dbg_store(name, buf_ap, waitl):
        if dbg is None or name not in dbg:
            return
        tok = P.sp(lambda e: e.dma_start(dbg[name], buf_ap), waits=waitl,
                   stream="dbgs")
        ev[("dbg_last",)] = tok

    cuts = [0, 1024, 1536, 2560, 3584, 4608, 5632, 5648, 5664, 6688]
    names = ["ug", "ang", "cand", "q", "k", "v", "a", "be", "og"]

    for sup in range(NSUP):
        prev_join = ev.get(("join", sup - 1), [])
        ln_toks = []
        for c in range(4):
            ci = sup * 4 + c
            t_x = P.sp(lambda e, c=c, ci=ci: e.dma_start(
                xc[c % 2][:], x_in[ci * 128:(ci + 1) * 128, :]),
                waits=[ln_toks[c - 2] if c >= 2 else None] + start
                      + ([prev_join] if c < 2 else []), stream=f"x{ci % 2}")
            t_ln, t_trl, _ = emit_ln_transpose(
                P, aps, xc[c % 2][:], xlnb, xT4[c][:], mcol, sqs, pp[6], toks,
                f"a0c{ci}",
                war_xln=prev_join if c == 0 else None,
                war_xT=prev_join if c == 0 else None,
                war_sqs=prev_join if c == 0 else None,
                t_x=t_x, rot=rot_a1)
            ln_toks.append(t_trl)
        t_lnall = ln_toks[-1]

        sched = ([(ng, c) for ng in range(44) for c in range(4)]
                 + [(ng, c) for c in range(4) for ng in range(44, NG)])
        piece_tok = {}
        load_i = [0]
        gev = {}
        seg_ev = {}
        pend_tr = []
        qkv_i = [0]
        qkv_tr = {}
        gidx = [0]
        last_piece = [None]
        for ng, c in sched:
            col0 = ng * 128
            wn = min(128, NCTRL - col0)
            pkey = (ng, c if ng >= 44 else -1)
            if last_piece[0] != pkey:
                li = load_i[0]
                tld_a = P.sp(lambda e, col0=col0, wn=wn, li=li: e.dma_start(
                    wcp[li % 2][:].rearrange("p (k n) -> p k n", n=128)
                    [:, 0:4, 0:wn],
                    wcbf[0:512, col0:col0 + wn]
                    .rearrange("(k p) n -> p k n", p=128)),
                    waits=[gev.get(gidx[0] - 2), t_wc_done, t_wc_done2],
                    stream=f"wc{(li % 2) * 2}")
                tld_b = P.sp(lambda e, col0=col0, wn=wn, li=li: e.dma_start(
                    wcp[li % 2][:].rearrange("p (k n) -> p k n", n=128)
                    [:, 4:8, 0:wn],
                    wcbf[512:1024, col0:col0 + wn]
                    .rearrange("(k p) n -> p k n", p=128)),
                    waits=[gev.get(gidx[0] - 2), t_wc_done, t_wc_done2],
                    stream=f"wc{(li % 2) * 2 + 1}")
                piece_tok[pkey] = ([tld_a, tld_b], li % 2)
                load_i[0] += 1
                last_piece[0] = pkey
            tld, pbuf = piece_tok[pkey]
            g_ = gidx[0]
            gidx[0] += 1
            bank = g_ % 4
            segs = []
            for si in range(9):
                lo = max(col0, cuts[si])
                hi = min(col0 + wn, cuts[si + 1])
                if lo < hi:
                    segs.append((names[si], lo, hi))
            has_bias = any(n_ in ("ug", "ang", "cand", "a", "be", "og")
                           for n_, _, _ in segs)
            w0 = [tld, t_lnall, gev.get(g_ - 4),
                  prev_join if ng == 0 else None]
            for k in range(8):
                last_mm = (k == 7) and not has_bias
                P.pe(lambda e, k=k, bank=bank, wn=wn, pbuf=pbuf, c=c,
                     has_bias=has_bias: e.matmul(
                    pp[bank][:, 0:wn],
                    xT4[c][:, k * 128:(k + 1) * 128],
                    wcp[pbuf][:].rearrange("p (k n) -> p k n", n=128)[:, k, 0:wn],
                    start=(k == 0), stop=(k == 7 and not has_bias)),
                    waits=w0 if k == 0 else [],
                    inc=("pe" if last_mm else None), mmk=(0, 128))
            if has_bias:
                bias_sl = []
                for n_, lo, hi in segs:
                    if n_ in ("q", "k", "v"):
                        continue
                    boff = lo if lo < 2560 else (2560 + lo - 5632)
                    bias_sl.append((lo - col0, boff, hi - lo))
                for bi, (po, boff, bw) in enumerate(bias_sl):
                    P.pe(lambda e, bank=bank, po=po, boff=boff, bw=bw,
                         last=(bi == len(bias_sl) - 1): e.matmul(
                        pp[bank][:, po:po + bw], ONESRB[0:1, 0:128],
                        bgates[0:1, boff:boff + bw], start=False, stop=last),
                        waits=[t_bg, t_bg1],
                        inc=("pe" if bi == len(bias_sl) - 1 else None),
                        mmk=(0, 1))
            tmm = P.tok("pe")
            flush, pend_tr = pend_tr, []
            for fn_ in flush:
                fn_()
            evs = []
            for n_, lo, hi in segs:
                po = lo - col0
                ww = hi - lo
                if n_ == "ug":
                    tevv = P.act(lambda e, c=c, lo=lo, po=po, ww=ww,
                                 bank=bank: e.activation(
                        upd4[:, c * 1024 + lo: c * 1024 + lo + ww],
                        pp[bank][:, po:po + ww], AF.Sigmoid),
                        waits=[tmm, prev_join if ng == 0 else None],
                        inc="act")
                elif n_ == "cand":
                    tevv = P.act(lambda e, c=c, lo=lo, po=po, ww=ww,
                                 bank=bank: e.activation(
                        cnd4[:, c * 1024 + lo - 1536:
                             c * 1024 + lo - 1536 + ww],
                        pp[bank][:, po:po + ww], AF.Silu),
                        waits=[tmm, prev_join if ng == 12 else None],
                        inc="act")
                elif n_ == "ang":
                    tevv = P.dve(lambda e, c=c, lo=lo, po=po, ww=ww,
                                 bank=bank: e.tensor_copy(
                        ang4[:, c * 512 + lo - 1024:
                             c * 512 + lo - 1024 + ww],
                        pp[bank][:, po:po + ww]),
                        waits=[tmm, prev_join if ng == 8 else None],
                        inc="dve")
                elif n_ in ("q", "k", "v"):
                    qi = qkv_i[0]
                    qkv_i[0] += 1
                    qoff = (qi % 2) * 128
                    tevv = P.dve(lambda e, po=po, ww=ww, bank=bank, qoff=qoff:
                                 e.tensor_copy(
                        qkv256[:, qoff:qoff + ww], pp[bank][:, po:po + ww]),
                        waits=[tmm, qkv_tr.get(qi - 2)], inc="dve")
                    cglob = (lo - 2560) // 128
                    cj = cglob // 8
                    c8 = cglob % 8

                    def emit_tr(tevv=tevv, cj=cj, c8=c8, c=c, ng=ng,
                                qi=qi, qoff=qoff):
                        bk, war = rot_qkv.alloc()
                        view = bk[:, 0:512].bitcast(BF16)[:, 0:128]
                        ttr = P.pe(lambda e, view=view, qoff=qoff:
                                   e.transpose(
                            view, qkv256[:, qoff:qoff + 128], IDB[:]),
                            waits=[tevv, war], inc="pe", mmk=(0, 128))
                        tte = P.dve(lambda e, cj=cj, c8=c8, c=c, view=view:
                                    e.tensor_copy(
                            win[cj][:].rearrange("p (ch t) -> p ch t", t=515)
                            [:, c8, 3 + c * 128: 3 + c * 128 + 128], view),
                            waits=[ttr, (t_tails if sup == 0 else None),
                                   prev_join if ng == 20 else None,
                                   ev.get(("dbg_last",))], inc="dve")
                        rot_qkv.set(tte)
                        qkv_tr[qi] = ttr
                        ev[("qkvtr_prev",)] = tte

                    pend_tr.append(emit_tr)
                elif n_ in ("a", "be"):
                    dst0 = c * 32 + (0 if n_ == "a" else 16)
                    tevv = P.dve(lambda e, dst0=dst0, po=po, ww=ww,
                                 bank=bank: e.tensor_copy(
                        abe[:, dst0:dst0 + ww], pp[bank][:, po:po + ww]),
                        waits=[tmm, prev_join if ng == 44 else None],
                        inc="dve")
                else:
                    tevv = P.dve(lambda e, lo=lo, po=po, ww=ww,
                                 bank=bank: e.tensor_copy(
                        ogst[:, lo - 5664: lo - 5664 + ww],
                        pp[bank][:, po:po + ww]),
                        waits=[tmm, prev_join if (ng == 44 and c == 0) else None,
                               ev.get(("ogstore", sup, c - 1)),
                               ev.get(("ogload", sup - 1, 3))],
                        inc="dve")
                    if lo + ww == 6688:
                        tso = P.sp(lambda e, c=c, sup=sup: e.dma_start(
                            ogbuf[(sup * 4 + c) * 128:
                                  (sup * 4 + c + 1) * 128, :], ogst[:]),
                            waits=[tevv], stream="og")
                        ev[("ogstore", sup, c)] = tso
                evs.append(tevv)
            gev[g_] = evs[-1]
            for n_, lo, hi in segs:
                seg_ev[(n_, c)] = evs[-1]
        for fn_ in pend_tr:
            fn_()
        pend_tr = []
        ev[("ctrl_last", sup)] = P.tok("pe")

        def vec_gen(c):
            ci = sup * 4 + c
            wv_prev = ev.get(("vecdone", ci - 1))
            w0v = [ev.get(("dn_done", ci - 1)), ev.get(("ctrl_last", sup)),
                   ev.get(("storedma", ci - 2)),
                   ev.get(("storedma", ci - 1)), wv_prev]
            vscr = vin[0][:]
            vsin = ocomb
            sq4s = trg[:, 0:512]
            cscr = trg[:, 512:528]
            t_rc = P.dve(lambda e, c=c: e.tensor_mul(
                rc[:], upd4[:, c * 1024:(c + 1) * 1024],
                cnd4[:, c * 1024:(c + 1) * 1024]),
                waits=[seg_ev[("ug", c)], seg_ev[("cand", c)]] + w0v,
                inc="dve")
            t_rc = P.dve(lambda e, c=c: e.tensor_sub(
                rc[:], cnd4[:, c * 1024:(c + 1) * 1024], rc[:]),
                waits=[t_rc], inc="dve")
            yield
            t_t1 = P.dve(lambda e, c=c: e.tensor_scalar(
                trg[:, 0:512], ang4[:, c * 512:(c + 1) * 512], INV2PI, MAGIC,
                ALU.mult, ALU.add),
                waits=[seg_ev[("ang", c)]] + w0v,
                inc="dve")
            t_t1 = P.dve(lambda e: e.tensor_scalar(
                trg[:, 0:512], trg[:, 0:512], MAGIC, TWOPI, ALU.subtract,
                ALU.mult), waits=[t_t1], inc="dve")
            t_t1 = P.dve(lambda e, c=c: e.scalar_tensor_tensor(
                trg[:, 0:512], trg[:, 0:512], -1.0,
                ang4[:, c * 512:(c + 1) * 512], ALU.mult, ALU.add),
                waits=[t_t1], inc="dve")
            t_t2 = P.dve(lambda e, c=c: e.tensor_scalar(
                trg[:, 512:1024], ang4[:, c * 512:(c + 1) * 512], 1.0, HALFPI,
                ALU.mult, ALU.add), waits=[seg_ev[("ang", c)]] + w0v,
                inc="dve")
            t_t2 = P.dve(lambda e, vscr=vscr: e.tensor_scalar(
                vscr[:, 0:512], trg[:, 512:1024], INV2PI, MAGIC, ALU.mult,
                ALU.add), waits=[t_t2, t_rc], inc="dve")
            t_t2 = P.dve(lambda e, vscr=vscr: e.tensor_scalar(
                vscr[:, 0:512], vscr[:, 0:512], MAGIC, TWOPI, ALU.subtract,
                ALU.mult), waits=[t_t2], inc="dve")
            t_t2 = P.dve(lambda e, vscr=vscr: e.tensor_sub(
                trg[:, 512:1024], trg[:, 512:1024], vscr[:, 0:512]),
                waits=[t_t2], inc="dve")
            t_sin = P.act(lambda e, vsin=vsin: e.activation(
                vsin[:, 0:512], trg[:, 0:512], AF.Sin),
                waits=[t_t1, wv_prev], inc="act")
            t_cos = P.act(lambda e, vsin=vsin: e.activation(
                vsin[:, 512:1024], trg[:, 512:1024], AF.Sin),
                waits=[t_t2], inc="act")
            yield
            vin_toks = []
            for pg in range(4):
                for sig in range(2):
                    bk, war = rot_vg.alloc()
                    view = bk[:, 0:512].bitcast(BF16)[:, 0:128]
                    tt = P.pe(lambda e, c=c, pg=pg, sig=sig, view=view:
                              e.transpose(
                        view, upd4[:, c * 1024 + pg * 256 + sig:
                                   c * 1024 + (pg + 1) * 256:2], IDB[:]),
                        waits=[war, seg_ev[("ug", c)]], inc="pe",
                        mmk=(0, 128))
                    tev = P.dve(lambda e, pg=pg, sig=sig, view=view:
                                e.tensor_copy(
                        vin[sig][:, pg * 128:(pg + 1) * 128], view),
                        waits=[tt, wv_prev, t_t2], inc="dve")
                    rot_vg.set(tev)
                    vin_toks.append(tev)
                for sig in range(2):
                    bk, war = rot_vg.alloc()
                    tt = P.pe(lambda e, pg=pg, sig=sig, bk=bk: e.transpose(
                        bk[:, 0:128],
                        rc[:, pg * 256 + sig:(pg + 1) * 256:2], IDF[:]),
                        waits=[war, t_rc], inc="pe", mmk=(0, 128))
                    tev = P.dve(lambda e, pg=pg, sig=sig, bk=bk: e.tensor_copy(
                        plA[4 + sig][:, pg * 128:(pg + 1) * 128],
                        bk[:, 0:128]), waits=[tt, wv_prev], inc="dve")
                    rot_vg.set(tev)
                    vin_toks.append(tev)
                for sig, src in ((3, 0), (2, 512)):
                    bk, war = rot_vg.alloc()
                    tt = P.pe(lambda e, pg=pg, src=src, bk=bk, vsin=vsin:
                              e.transpose(
                        bk[:, 0:128],
                        vsin[:, src + pg * 128: src + (pg + 1) * 128],
                        IDF[:]),
                        waits=[war, t_sin, t_cos], inc="pe",
                        mmk=(0, 128))
                    tev = P.dve(lambda e, pg=pg, sig=sig, bk=bk: e.tensor_copy(
                        vin[sig][:, pg * 128:(pg + 1) * 128], bk[:, 0:128]),
                        waits=[tt, wv_prev], inc="dve")
                    rot_vg.set(tev)
                    vin_toks.append(tev)
                yield
            t_vin = vin_toks[-1]

            bt_w = [t_bt, t_vin]
            pb = []
            for dst, g_, (tr1, bb1), (tr2, bb2), sgn in (
                    (0, 0, (2, 0), (3, 1), -1.0),
                    (1, 0, (2, 2), (3, 3), -1.0),
                    (2, 1, (3, 0), (2, 1), 1.0),
                    (3, 1, (3, 2), (2, 3), 1.0)):
                ta = P.dve(lambda e, tr1=tr1, bb1=bb1: e.tensor_mul(
                    plB4[:], vin[tr1][:], bt4[bb1][:]),
                    waits=bt_w + (pb[-1:] if pb else []), inc="dve")
                tb = P.dve(lambda e, tr2=tr2, bb2=bb2: e.tensor_mul(
                    plB5[:], vin[tr2][:], bt4[bb2][:]), waits=[ta], inc="dve")
                tc_ = P.dve(lambda e, sgn=sgn: e.tensor_tensor(
                    plB4[:], plB4[:], plB5[:],
                    ALU.add if sgn > 0 else ALU.subtract),
                    waits=[ta, tb], inc="dve")
                td = P.dve(lambda e, dst=dst, g_=g_: e.tensor_mul(
                    plA[dst][:], plB4[:], vin[g_][:]), waits=[tc_], inc="dve")
                pb.append(td)
                if dst == 1:
                    yield
            t_planes = pb[-1]
            yield

            def v4(t_):
                return t_[:].rearrange("p (g sub t) -> p g sub t", g=4, t=16)

            sq4 = sq4s.rearrange("p (g sub t) -> p g sub t", g=4, t=16)
            cur, nxt = plA, plB
            t_lvl = t_planes
            for s_ in (1, 2, 4, 8):
                lw = [t_lvl]
                last = None
                for pl in range(6):
                    last = P.dve(lambda e, pl=pl, s_=s_, cur=cur, nxt=nxt:
                                 e.tensor_copy(
                        v4(nxt[pl])[:, :, :, 0:s_],
                        v4(cur[pl])[:, :, :, 0:s_]), waits=lw, inc="dve")
                for d_, a_, b_, c2, d2 in ((0, 0, 0, 1, 2), (1, 0, 1, 1, 3),
                                           (2, 2, 0, 3, 2), (3, 2, 1, 3, 3)):
                    t1 = P.dve(lambda e, a_=a_, b_=b_, s_=s_, cur=cur:
                               e.tensor_mul(
                        sq4[:, :, :, s_:16],
                        v4(cur[a_])[:, :, :, s_:16],
                        v4(cur[b_])[:, :, :, 0:16 - s_]), waits=lw + [last],
                        inc="dve")
                    t2 = P.dve(lambda e, d_=d_, c2=c2, d2=d2, s_=s_, cur=cur,
                               nxt=nxt: e.tensor_mul(
                        v4(nxt[d_])[:, :, :, s_:16],
                        v4(cur[c2])[:, :, :, s_:16],
                        v4(cur[d2])[:, :, :, 0:16 - s_]), waits=lw, inc="dve")
                    last = P.dve(lambda e, d_=d_, s_=s_, nxt=nxt: e.tensor_add(
                        v4(nxt[d_])[:, :, :, s_:16],
                        v4(nxt[d_])[:, :, :, s_:16],
                        sq4[:, :, :, s_:16]), waits=[t1, t2], inc="dve")
                for d_, a_, b_ in ((4, 0, 1), (5, 2, 3)):
                    t1 = P.dve(lambda e, a_=a_, s_=s_, cur=cur: e.tensor_mul(
                        sq4[:, :, :, s_:16],
                        v4(cur[a_])[:, :, :, s_:16],
                        v4(cur[4])[:, :, :, 0:16 - s_]), waits=lw + [last],
                        inc="dve")
                    t2 = P.dve(lambda e, d_=d_, b_=b_, s_=s_, cur=cur, nxt=nxt:
                               e.tensor_mul(
                        v4(nxt[d_])[:, :, :, s_:16],
                        v4(cur[b_])[:, :, :, s_:16],
                        v4(cur[5])[:, :, :, 0:16 - s_]), waits=lw, inc="dve")
                    t3 = P.dve(lambda e, d_=d_, s_=s_, nxt=nxt: e.tensor_add(
                        v4(nxt[d_])[:, :, :, s_:16],
                        v4(nxt[d_])[:, :, :, s_:16],
                        sq4[:, :, :, s_:16]), waits=[t1, t2], inc="dve")
                    last = P.dve(lambda e, d_=d_, s_=s_, cur=cur, nxt=nxt:
                                 e.tensor_add(
                        v4(nxt[d_])[:, :, :, s_:16],
                        v4(nxt[d_])[:, :, :, s_:16],
                        v4(cur[d_])[:, :, :, s_:16]), waits=[t3], inc="dve")
                t_lvl = last
                cur, nxt = nxt, cur
                yield
            fin = cur
            oth = nxt

            vv = lambda t_: t_[:].rearrange("p (g t) -> p g t", t=128)
            vx9 = vx0[:].rearrange("p (g n) -> p g n", n=9)
            vy9 = vy0[:].rearrange("p (g n) -> p g n", n=9)
            csc = lambda a, b: cscr[:, a:b].rearrange("p (g o) -> p g o", o=1)
            t_car = t_lvl
            for s8 in range(8):
                tcol = s8 * 16 + 15
                e1 = P.dve(lambda e, s8=s8, tcol=tcol, fin=fin: e.tensor_mul(
                    csc(0, 4),
                    vv(fin[0])[:, :, tcol:tcol + 1], vx9[:, :, s8:s8 + 1]),
                    waits=[t_car], inc="dve")
                e2 = P.dve(lambda e, s8=s8, tcol=tcol, fin=fin: e.tensor_mul(
                    csc(4, 8),
                    vv(fin[1])[:, :, tcol:tcol + 1], vy9[:, :, s8:s8 + 1]),
                    waits=[e1], inc="dve")
                e3 = P.dve(lambda e: e.tensor_add(cscr[:, 0:4], cscr[:, 0:4],
                                                  cscr[:, 4:8]),
                           waits=[e1, e2], inc="dve")
                f1 = P.dve(lambda e, s8=s8, tcol=tcol, fin=fin: e.tensor_mul(
                    csc(8, 12),
                    vv(fin[2])[:, :, tcol:tcol + 1], vx9[:, :, s8:s8 + 1]),
                    waits=[e3], inc="dve")
                f2 = P.dve(lambda e, s8=s8, tcol=tcol, fin=fin: e.tensor_mul(
                    csc(12, 16),
                    vv(fin[3])[:, :, tcol:tcol + 1], vy9[:, :, s8:s8 + 1]),
                    waits=[f1], inc="dve")
                f3 = P.dve(lambda e: e.tensor_add(cscr[:, 8:12], cscr[:, 8:12],
                                                  cscr[:, 12:16]),
                           waits=[f1, f2], inc="dve")
                e4 = P.dve(lambda e, s8=s8, tcol=tcol, fin=fin: e.tensor_add(
                    vx9[:, :, s8 + 1:s8 + 2],
                    csc(0, 4),
                    vv(fin[4])[:, :, tcol:tcol + 1]), waits=[e3, f3],
                    inc="dve")
                f4 = P.dve(lambda e, s8=s8, tcol=tcol, fin=fin: e.tensor_add(
                    vy9[:, :, s8 + 1:s8 + 2],
                    csc(8, 12),
                    vv(fin[5])[:, :, tcol:tcol + 1]), waits=[f3, e4],
                    inc="dve")
                t_car = f4
                if s8 % 3 == 2:
                    yield

            v4f = v4
            vx0v = vx9[:, :, 0:8].unsqueeze(3).broadcast_to((128, 4, 8, 16))
            vy0v = vy9[:, :, 0:8].unsqueeze(3).broadcast_to((128, 4, 8, 16))
            a1 = P.dve(lambda e, fin=fin: e.tensor_mul(
                v4f(oth[2]), v4f(fin[0]), vx0v), waits=[t_car], inc="dve")
            a2 = P.dve(lambda e, fin=fin: e.tensor_mul(
                v4f(oth[3]), v4f(fin[1]), vy0v), waits=[a1], inc="dve")
            a3 = P.dve(lambda e: e.tensor_add(v4f(oth[2]), v4f(oth[2]),
                                              v4f(oth[3])),
                       waits=[a1, a2], inc="dve")
            a4 = P.dve(lambda e, fin=fin: e.tensor_add(
                v4f(oth[0]), v4f(oth[2]), v4f(fin[4])), waits=[a3], inc="dve")
            yield
            b1 = P.dve(lambda e, fin=fin: e.tensor_mul(
                v4f(oth[4]), v4f(fin[2]), vx0v), waits=[a4], inc="dve")
            b2 = P.dve(lambda e, fin=fin: e.tensor_mul(
                v4f(oth[5]), v4f(fin[3]), vy0v), waits=[b1], inc="dve")
            b3 = P.dve(lambda e: e.tensor_add(v4f(oth[4]), v4f(oth[4]),
                                              v4f(oth[5])),
                       waits=[b1, b2], inc="dve")
            b4 = P.dve(lambda e, fin=fin: e.tensor_add(
                v4f(oth[1]), v4f(oth[4]), v4f(fin[5])), waits=[b3], inc="dve")
            cx = P.dve(lambda e, c=c: e.tensor_copy(
                vecbf[0][:].rearrange("p (g t) -> p g t", t=512)
                [:, :, c * 128:(c + 1) * 128], vv(oth[0])),
                waits=[a4, prev_join if c == 0 else None], inc="dve")
            cy = P.dve(lambda e, c=c: e.tensor_copy(
                vecbf[1][:].rearrange("p (g t) -> p g t", t=512)
                [:, :, c * 128:(c + 1) * 128], vv(oth[1])),
                waits=[b4, prev_join if c == 0 else None], inc="dve")
            cc1 = P.dve(lambda e: e.tensor_copy(vx9[:, :, 0:1], vx9[:, :, 8:9]),
                        waits=[cx, b4], inc="dve")
            cc2 = P.dve(lambda e: e.tensor_copy(vy9[:, :, 0:1], vy9[:, :, 8:9]),
                        waits=[cy, cc1], inc="dve")
            ev[("vecdone", ci)] = cc2
            if ci == 3:
                dbg_store("vecx", vecbf[0][:], [cc2])
                dbg_store("vecy", vecbf[1][:], [cc2])

        vec_gens = {c: vec_gen(c) for c in range(4 if L0CUT >= 2 else 0)}

        def vdrive(c, n):
            g = vec_gens.get(c)
            if g is None:
                return
            for _ in range(n):
                try:
                    next(g)
                except StopIteration:
                    vec_gens[c] = None
                    return

        if L0CUT < 4:
            for c in range(4 if L0CUT >= 2 else 0):
                vdrive(c, 99)
        if L0CUT < 3:
            for c in range(4):
                ci = sup * 4 + c
                tst = P.sp(lambda e, ci=ci, c=c: e.dma_start(
                    x_out[ci * 128:(ci + 1) * 128, :], xc[c % 2][:]),
                    waits=[P.tok("pe"), P.tok("dve"),
                           P.tok("act")],
                    stream=f"st{ci % 2}")
                ev[("store", ci)] = tst
                ev[("storedma", ci)] = tst
            ev[("join", sup)] = join(sup)
            continue
        wconv = [ev[("qkvtr_prev",)], t_cv, t_sbias,
                 ] + ev.get(("convsilu", sup - 1), [])
        silu_toks = []
        for cj in range(3):
            wv = win[cj][:].rearrange("p (ch t) -> p ch t", t=515)
            for c8 in range(8):
                ab = c8 % 2
                tprev = silu_toks[-2] if len(silu_toks) >= 2 else None
                t_ = None
                for j in range(4):
                    if j == 0:
                        t_ = P.dve(lambda e, cj=cj, c8=c8, ab=ab, wv=wv:
                                   e.tensor_scalar(
                            acc[ab][:], wv[:, c8, 0:512],
                            wcv[:, cj * 32 + c8 * 4: cj * 32 + c8 * 4 + 1],
                            None, ALU.mult), waits=wconv + [tprev], inc="dve")
                    else:
                        t_ = P.dve(lambda e, cj=cj, c8=c8, ab=ab, j=j, wv=wv:
                                   e.scalar_tensor_tensor(
                            acc[ab][:], wv[:, c8, j:j + 512],
                            wcv[:, cj * 32 + c8 * 4 + j:
                                 cj * 32 + c8 * 4 + j + 1],
                            acc[ab][:], ALU.mult, ALU.add), waits=[t_],
                            inc="dve")
                t_tl = P.dve(lambda e, cj=cj, c8=c8, wv=wv: e.tensor_copy(
                    wv[:, c8, 0:3], wv[:, c8, 512:515]), waits=[t_], inc="dve")
                tsl = P.act(lambda e, cj=cj, c8=c8, ab=ab, wv=wv: e.activation(
                    wv[:, c8, 3:515], acc[ab][:], AF.Silu,
                    bias=sbias[:, cj * 8 + c8: cj * 8 + c8 + 1]),
                    waits=[t_, t_tl], inc="act")
                silu_toks.append(tsl)
        t_conv = silu_toks[-1]
        ev[("convsilu", sup)] = silu_toks[-2:]
        if sup == 0:
            dbg_store("win_q", win[0][:], [t_conv])
            dbg_store("win_k", win[1][:], [t_conv])
            dbg_store("upd", upd4[:], [t_conv])
            dbg_store("ang", ang4[:], [t_conv])

        if L0CUT < 4:
            for c in range(4):
                ci = sup * 4 + c
                tst = P.sp(lambda e, ci=ci, c=c: e.dma_start(
                    x_out[ci * 128:(ci + 1) * 128, :], xc[c % 2][:]),
                    waits=[P.tok("pe"), P.tok("dve"),
                           P.tok("act")],
                    stream=f"st{ci % 2}")
                ev[("store", ci)] = tst
                ev[("storedma", ci)] = tst
            ev[("join", sup)] = join(sup)
            continue
        for c in range(4):
            ci = sup * 4 + c
            qwv = win[0][:].rearrange("p (ch t) -> p ch t", t=515)
            kwv = win[1][:].rearrange("p (ch t) -> p ch t", t=515)
            vwv = win[2][:].rearrange("p (ch t) -> p ch t", t=515)
            tc0 = 3 + c * 128
            dprev = ev.get(("dn_done", ci - 1))
            wD = [t_conv, dprev]

            def cutstore(tokl):
                tst = P.sp(lambda e, ci=ci, c=c: e.dma_start(
                    x_out[ci * 128:(ci + 1) * 128, :], xc[c % 2][:]),
                    waits=tokl + [P.tok("pe"),
                                  P.tok("act"),
                                  P.tok("dve")],
                    stream=f"st{ci % 2}")
                ev[("store", ci)] = tst
                ev[("storedma", ci)] = tst
                ev[("dn_done", ci)] = P.tok("dve")

            l1 = P.dve(lambda e, c=c: e.tensor_scalar(
                small6[:], abe[:, c * 32:c * 32 + 16], 0.0, None, ALU.min),
                waits=wD, inc="dve")
            l2 = P.act(lambda e, c=c: e.activation(
                lna[:], abe[:, c * 32:c * 32 + 16], AF.Abs), waits=wD,
                inc="act")
            l3 = P.act(lambda e: e.activation(lna[:], lna[:], AF.Exp,
                                              scale=-1.0), waits=[l2],
                       inc="act")
            l4 = P.act(lambda e: e.activation(lna[:], lna[:], AF.Ln, bias=1.0),
                       waits=[l3], inc="act")
            l5 = P.dve(lambda e: e.tensor_sub(lna[:], small6[:], lna[:]),
                       waits=[l1, l4], inc="dve")
            t_beta = P.act(lambda e, c=c: e.activation(
                beta[:], abe[:, c * 32 + 16:c * 32 + 32], AF.Sigmoid),
                waits=wD, inc="act")

            t_lt = P.pe(lambda e: e.transpose(pp[2][0:16, 0:128], lna[:],
                                              IDF[:]),
                        waits=[l5, dprev], inc="pe", mmk=(0, 128))
            t_le = P.dve(lambda e: e.tensor_copy(lnahm[:],
                                                 pp[2][0:16, 0:128]),
                         waits=[t_lt, t_hm0], inc="dve")
            t_cl = P.dve(lambda e: e.tensor_tensor_scan(
                cl16[:], ONESQ[0:16, :], lnahm[:], 0.0, ALU.mult,
                ALU.add), waits=[t_le, toks["onesq_ready"]], inc="dve")
            t_clm = P.dve(lambda e: e.tensor_sub(clm16[:], cl16[:],
                                                 lnahm[:]),
                          waits=[t_cl], inc="dve")
            if L0CUT == 40:
                cutstore([t_clm, t_beta])
                continue

            t_qs = P.dve(lambda e, c=c, qwv=qwv, tc0=tc0: e.tensor_mul(
                sqs[:].rearrange("p (ch t) -> p ch t", t=128),
                qwv[:, :, tc0:tc0 + 128], qwv[:, :, tc0:tc0 + 128]),
                waits=wD, inc="dve")
            for i8 in range(8):
                P.pe(lambda e, i8=i8: e.matmul(
                    pp[4][0:16, 0:128], BLK16[:, i8 * 16:(i8 + 1) * 16],
                    sqs[:, i8 * 128:(i8 + 1) * 128],
                    start=(i8 == 0), stop=(i8 == 7)),
                    waits=([t_qs, toks["blk16_ready"], dprev]
                           if i8 == 0 else []),
                    inc=("pe" if i8 == 7 else None), mmk=(0, 128))
            t_qsm = P.tok("pe")
            h3 = P.dve(lambda e: e.tensor_scalar(
                hm[64:80, :], pp[4][0:16, 0:128], 1e-12, None, ALU.add),
                waits=[t_qsm], inc="dve")
            h3b = P.dve(lambda e: e.reciprocal(hm[64:80, :], hm[64:80, :]),
                        waits=[h3], inc="dve")
            h3c = P.act(lambda e: e.activation(hm[64:80, :], hm[64:80, :],
                                               AF.Sqrt), waits=[h3b],
                        inc="act")
            t_ks = P.dve(lambda e, c=c, kwv=kwv, tc0=tc0: e.tensor_mul(
                sqs[:].rearrange("p (ch t) -> p ch t", t=128),
                kwv[:, :, tc0:tc0 + 128], kwv[:, :, tc0:tc0 + 128]),
                waits=[h3], inc="dve")
            for i8 in range(8):
                P.pe(lambda e, i8=i8: e.matmul(
                    pp[4][32:48, 0:128], BLK16[:, i8 * 16:(i8 + 1) * 16],
                    sqs[:, i8 * 128:(i8 + 1) * 128],
                    start=(i8 == 0), stop=(i8 == 7)),
                    waits=([t_ks] if i8 == 0 else []),
                    inc=("pe" if i8 == 7 else None), mmk=(0, 128))
            t_ksm = P.tok("pe")
            h2 = P.dve(lambda e: e.tensor_scalar(
                hm[32:48, :], pp[4][32:48, 0:128], 1e-12, None, ALU.add),
                waits=[t_ksm], inc="dve")
            h2b = P.dve(lambda e: e.reciprocal(hm[32:48, :], hm[32:48, :]),
                        waits=[h2], inc="dve")
            if L0CUT == 41:
                cutstore([h2b])
                continue
            t_skhm = P.act(lambda e: e.activation(skhm[:], hm[32:48, :],
                                                  AF.Sqrt), waits=[h2b],
                           inc="act")
            h1 = P.dve(lambda e: e.tensor_copy(hm[0:16, :], cl16[:]),
                       waits=[t_cl, h2b, h3c], inc="dve")
            t_ht = P.pe(lambda e: e.transpose(pp[2][:, 0:128], hm[:], IDF[:]),
                        waits=[h1, t_le], inc="pe", mmk=(0, 128))
            t_he = P.dve(lambda e: e.tensor_copy(tok16[:], pp[2][:, 0:128]),
                         waits=[t_ht], inc="dve")
            t_sk = P.act(lambda e: e.activation(sktok[:], tok16[:, 32:48],
                                                AF.Sqrt), waits=[t_he],
                         inc="act")
            if L0CUT == 42:
                cutstore([t_sk])
                continue
            t_fA = P.act(lambda e: e.activation(fA[:], tok16[:, 0:16], AF.Exp),
                         waits=[t_he], inc="act")
            t_f1 = P.dve(lambda e: e.tensor_sub(fash[:], tok16[:, 0:16],
                                                lna[:]),
                         waits=[t_he, l5], inc="dve")
            t_fash = P.act(lambda e: e.activation(fash[:], fash[:], AF.Exp),
                           waits=[t_f1], inc="act")
            t_f2 = P.dve(lambda e: e.tensor_mul(fash[:], fash[:], sktok[:]),
                         waits=[t_fash, t_sk], inc="dve")
            t_c127 = P.pe(lambda e: e.transpose(pp[2][0:1, 0:16],
                                                cl16[:, 127:128],
                                                IDF[0:16, 0:16]),
                          waits=[t_cl, t_he], inc="pe", mmk=(0, 16))
            t_c127e = P.dve(lambda e: e.tensor_copy(acrow[:],
                                                    pp[2][0:1, 0:16]),
                            waits=[t_c127], inc="dve")
            t_acm = P.pe(lambda e: e.matmul(
                pp[2][:, 0:16], ONESR[0:1, 0:128], acrow[:], start=True,
                stop=True), waits=[t_c127e], inc="pe", mmk=(0, 1))
            t_fs1 = P.dve(lambda e: e.tensor_sub(fstate[:], pp[2][:, 0:16],
                                                 tok16[:, 0:16]),
                          waits=[t_acm], inc="dve")
            t_fs2 = P.act(lambda e: e.activation(fstate[:], fstate[:], AF.Exp),
                          waits=[t_fs1], inc="act")
            t_fs = P.dve(lambda e: e.tensor_mul(fstate[:], fstate[:],
                                                sktok[:]),
                         waits=[t_fs2, t_sk], inc="dve")
            t_ace = P.act(lambda e: e.activation(acrow[:], acrow[:], AF.Exp),
                          waits=[t_fs1], inc="act")
            t_acb = P.pe(lambda e: e.matmul(
                pp[2][:, 16:32], ONESR[0:1, 0:128], acrow[:], start=True,
                stop=True), waits=[t_ace, t_fs1], inc="pe", mmk=(0, 1))
            t_acbe = P.dve(lambda e: e.tensor_copy(acbc[:],
                                                   pp[2][:, 16:32]),
                           waits=[t_acb], inc="dve")

            if L0CUT == 43:
                cutstore([t_acbe, t_fs])
                continue
            vk_toks = []
            for wvv, dst in ((vwv, vT), (kwv, kT)):
                for c8 in range(8):
                    bk, war = rot_vk.alloc()
                    view = bk[:, 0:512].bitcast(BF16)[:, 0:128]
                    tt = P.pe(lambda e, wvv=wvv, c8=c8, tc0=tc0, view=view:
                              e.transpose(
                        view, wvv[:, c8, tc0:tc0 + 128], IDB[:]),
                        waits=[war, t_conv], inc="pe", mmk=(0, 128))
                    tev = P.dve(lambda e, dst=dst, c8=c8, view=view:
                                e.tensor_copy(
                        dst[:, c8 * 128:(c8 + 1) * 128], view),
                        waits=[tt, dprev, ev.get(("dbg_last",))], inc="dve")
                    rot_vk.set(tev)
                    vk_toks.append(tev)
            t_vk = vk_toks[-1]

            if L0CUT == 44:
                cutstore([t_vk])
                continue
            wave_toks = []
            for wv in range(2):
                for pi, hh in enumerate([0, 2, 4, 6, 1, 3, 5, 7]):
                    h = wv * 8 + hh
                    ht = h // 2
                    hb = (h % 2) * 64
                    slot = h % 8
                    P.pe(lambda e, h=h, ht=ht, hb=hb, slot=slot,
                         kwv=kwv, tc0=tc0: e.matmul(
                        pp[7][:, slot * 64:(slot + 1) * 64],
                        kwv[hb:hb + 64, ht, tc0:tc0 + 128],
                        S0b[hsl(h), csl(h)], start=True, stop=True),
                        waits=([t_conv, t_s0b, dprev, t_vk,
                                ev.get(("ctrl_last", sup)),
                                ev.get(("dbg_last",))]
                               if (wv == 0 and pi == 0)
                               else ([wave_toks[0]] if pi == 0 else [])),
                        inc=("pe" if pi == 7 else None), mmk=(hb, 64))
                t_pw = P.tok("pe")
                rw = P.dve(lambda e, wv=wv: e.tensor_tensor(
                    rscr[:, wv * 512:(wv + 1) * 512]
                    .rearrange("p (h j) -> p h j", j=64),
                    pp[7][:, 0:512].rearrange("p (h j) -> p h j", j=64),
                    fash[:, wv * 8:(wv + 1) * 8].unsqueeze(2)
                    .broadcast_to((128, 8, 64)),
                    ALU.mult), waits=[t_pw, t_f2], inc="dve")
                wave_toks.append(rw)
            r4 = P.dve(lambda e: e.tensor_sub(Rt[:], vT[:], rscr[:]),
                       waits=wave_toks + [t_vk], inc="dve")
            bank_tok[7] = r4
            if ci == 0:
                dbg_store("Rt0", Rt[:], [r4])
                dbg_store("cl0", cl16[:], [t_cl, r4])
                dbg_store("tok160", tok16[:], [t_he, r4])

            if L0CUT < 5:
                tst = P.sp(lambda e, ci=ci, c=c: e.dma_start(
                    x_out[ci * 128:(ci + 1) * 128, :], xc[c % 2][:]),
                    waits=[r4, P.tok("pe"),
                           P.tok("act")],
                    stream=f"st{ci % 2}")
                ev[("store", ci)] = tst
                ev[("storedma", ci)] = tst
                ev[("dn_done", ci)] = r4
                continue
            vdrive(c, 4)
            rd_prev = [None, None]
            st_prev = [None, None]
            hprevP = [[r4], [r4]]
            for pr in range(8):
                hs = [2 * pr, 2 * pr + 1]
                QcP = [qwv[(h % 2) * 64:(h % 2) * 64 + 64, h // 2,
                           tc0:tc0 + 128] for h in hs]
                KcP = [kwv[(h % 2) * 64:(h % 2) * 64 + 64, h // 2,
                           tc0:tc0 + 128] for h in hs]
                t_bl = {}
                t_bge = {}
                t_bsk2 = {}
                for q in range(2):
                    h = hs[q]
                    t_bl[q] = P.pe(lambda e, h=h, q=q: e.matmul(
                        pp[2 + q][:, 0:128], SELH[:, h * 128:(h + 1) * 128],
                        cl16[:], start=True, stop=True),
                        waits=[t_cl, toks["selh_ready"], t_acbe,
                               ev.get(("dbg_last",))] + hprevP[q],
                        inc="pe", mmk=(0, 16))
                    t_bge[q] = P.pe(lambda e, h=h, q=q: e.matmul(
                        pp[2 + q][:, 128:256], SELH[:, h * 128:(h + 1) * 128],
                        clm16[:], start=True, stop=True), waits=[t_clm],
                        inc="pe", mmk=(0, 16))
                    t_bsk2[q] = P.pe(lambda e, h=h, q=q: e.matmul(
                        pp[4 + q][:, 128:256], SELH[:, h * 128:(h + 1) * 128],
                        skhm[:], start=True, stop=True),
                        waits=[t_skhm], inc="pe", mmk=(0, 16))
                t_kh = {}
                s_lam = {}
                s_gam = {}
                t_exp = {}
                for q in range(2):
                    h = hs[q]
                    hb = (h % 2) * 64
                    t_kh[q] = P.dve(lambda e, q=q, hb=hb, Kc=KcP[q]:
                                    e.tensor_tensor(
                        khat[hb:hb + 64, :], Kc, pp[4 + q][hb:hb + 64, 128:256],
                        ALU.mult), waits=[t_bsk2[q]], inc="dve")
                    s_lam[q] = P.dve(lambda e, h=h, q=q:
                                     e.scalar_tensor_tensor(
                        lgscrP[q][:, 0:128], pp[2 + q][:, 0:128],
                        tok16[:, h:h + 1], MSKI[:], ALU.subtract, ALU.add),
                        waits=[t_bl[q], t_he, toks["mask_incl_ready"]],
                        inc="dve")
                    s_gam[q] = P.dve(lambda e, h=h, q=q:
                                     e.scalar_tensor_tensor(
                        lgscrP[q][:, 128:256], pp[2 + q][:, 128:256],
                        tok16[:, h:h + 1], MSKS[:], ALU.subtract, ALU.add),
                        waits=[t_bge[q], toks["mask_strict_ready"]],
                        inc="dve")
                for q in range(2):
                    t_exp[q] = P.act(lambda e, q=q: e.activation(
                        lgscrP[q][:], lgscrP[q][:], AF.Exp),
                        waits=[s_lam[q], s_gam[q]], inc="act")
                t_G = {}
                t_Z = {}
                t_qm = {}
                for q in range(2):
                    h = hs[q]
                    hb = (h % 2) * 64
                    t_G[q] = P.pe(lambda e, q=q, hb=hb: e.matmul(
                        pp[2 + q][:, 256:384], khat[hb:hb + 64, :],
                        khat[hb:hb + 64, :], start=True, stop=True),
                        waits=[s_lam[q], s_gam[q], t_kh[q]], inc="pe",
                        mmk=(hb, 64))
                    t_Z[q] = P.pe(lambda e, q=q, Kc=KcP[q], Qc=QcP[q]:
                                  e.matmul(
                        pp[2 + q][:, 384:512], Kc, Qc, start=True, stop=True),
                        waits=[], inc="pe", mmk=(hb, 64))
                    t_qm[q] = P.pe(lambda e, h=h, q=q, Qc=QcP[q]: e.matmul(
                        pp[4 + q][:, 256:320], Qc, S0b[hsl(h), csl(h)],
                        start=True, stop=True),
                        waits=[t_kh[q], rd_prev[q]], inc="pe",
                        mmk=(hb, 64))
                t_NT = {}
                t_lz = {}
                for q in range(2):
                    h = hs[q]
                    t_NT[q] = P.dve(lambda e, h=h, q=q:
                                    e.scalar_tensor_tensor(
                        NTp[q][:], pp[2 + q][:, 256:384], beta[:, h:h + 1],
                        lgscrP[q][:, 128:256], ALU.mult, ALU.mult),
                        waits=[t_Z[q], t_exp[q], t_beta], inc="dve")
                    t_lz[q] = P.dve(lambda e, h=h, q=q:
                                    e.scalar_tensor_tensor(
                        lamzP[q][:], pp[2 + q][:, 384:512],
                        sktok[:, h:h + 1], lgscrP[q][:, 0:128],
                        ALU.mult, ALU.mult),
                        waits=[t_Z[q], t_exp[q], t_sk], inc="dve")
                vdrive(c, 1)
                t_Nt = {}
                t_N = {}
                t_y = {}
                for q in range(2):
                    t_Nt[q] = P.pe(lambda e, q=q: e.transpose(
                        pp[0 + q][:, 256:384], NTp[q][:], IDF[:]),
                        waits=[t_NT[q]], inc="pe", mmk=(0, 128))
                for q in range(2):
                    t_N[q] = P.act(lambda e, q=q: e.activation(
                        nkpP[q][0][:], pp[0 + q][:, 256:384], AF.Copy),
                        waits=[t_Nt[q]], inc="act")
                    t_y[q] = P.dve(lambda e, q=q, h=hs[q]: e.tensor_copy(
                        YtP[q][0][:], Rt[:, h * 64:(h + 1) * 64]),
                        waits=[r4], inc="dve")
                ycur = [0, 0]
                e_ev = {0: [t_N[0]], 1: [t_N[1]]}
                for lvl in range(7):
                    ma = {}
                    for q in range(2):
                        Bt_l = NTp[q] if lvl == 0 else nksP[q][lvl % 2]
                        ma[q] = P.pe(lambda e, q=q, Bt_l=Bt_l,
                                     yc_=ycur[q]: e.matmul(
                            pp[4 + q][:, 64:128], Bt_l[:], YtP[q][yc_][:],
                            start=True, stop=True),
                            waits=[t_y[q], t_NT[q], t_qm[q]] + e_ev[q],
                            inc="pe", mmk=(0, 128))
                    if lvl < 6:
                        for q in range(2):
                            A_l = nkpP[q][lvl % 2] if lvl else nkpP[q][0]
                            Bt_l = NTp[q] if lvl == 0 else nksP[q][lvl % 2]
                            nxt = (lvl + 1) % 2
                            m1 = P.pe(lambda e, q=q, A_l=A_l, Bt_l=Bt_l:
                                      e.matmul(
                                pp[0 + q][:, 0:128], Bt_l[:], A_l[:],
                                start=True, stop=True),
                                waits=e_ev[q], inc="pe", mmk=(0, 128))
                            m2 = P.pe(lambda e, q=q, A_l=A_l, Bt_l=Bt_l:
                                      e.matmul(
                                pp[0 + q][:, 128:256], A_l[:], Bt_l[:],
                                start=True, stop=True),
                                waits=[], inc="pe", mmk=(0, 128))
                            e1_ = P.act(lambda e, q=q, nxt=nxt: e.activation(
                                nkpP[q][nxt][:], pp[0 + q][:, 0:128],
                                AF.Copy), waits=[m2], inc="act")
                            e2_ = P.act(lambda e, q=q, nxt=nxt: e.activation(
                                nksP[q][nxt][:], pp[0 + q][:, 128:256],
                                AF.Copy), waits=[e1_], inc="act")
                            e_ev[q] = [e1_, e2_]
                    for q in range(2):
                        if lvl == 0:
                            t_y[q] = P.dve(lambda e, q=q, yc_=ycur[q]:
                                           e.tensor_sub(
                                YtP[q][1 - yc_][:], YtP[q][yc_][:],
                                pp[4 + q][:, 64:128]),
                                waits=[ma[q]], inc="dve")
                        else:
                            t_y[q] = P.dve(lambda e, q=q, yc_=ycur[q]:
                                           e.tensor_add(
                                YtP[q][1 - yc_][:], pp[4 + q][:, 64:128],
                                YtP[q][yc_][:]),
                                waits=[ma[q]], inc="dve")
                        ycur[q] = 1 - ycur[q]
                vdrive(c, 1)
                t_w = {}
                for q in range(2):
                    h = hs[q]
                    t_w[q] = P.dve(lambda e, h=h, q=q, yc_=ycur[q]:
                                   e.tensor_scalar(
                        Wt_[:, h * 64:(h + 1) * 64], YtP[q][yc_][:],
                        beta[:, h:h + 1], None, ALU.mult),
                        waits=[t_y[q], t_beta], inc="dve")
                    if dbg is not None and h == 1 and ci == 0:
                        dbg_store("Yfin", YtP[q][ycur[q]][:], [t_y[q]])
                        dbg_store("NT0", NTp[q][:], [t_NT[q]])
                        dbg_store("W0", Wt_[:, 64:128], [t_w[q]])
                t_rm = {}
                for q in range(2):
                    h = hs[q]
                    t_rm[q] = P.pe(lambda e, h=h, q=q: e.matmul(
                        pp[4 + q][:, 320:384], lamzP[q][:],
                        Wt_[:, h * 64:(h + 1) * 64],
                        start=True, stop=True),
                        waits=[t_lz[q], t_w[q]], inc="pe", mmk=(0, 128))
                for q in range(2):
                    h = hs[q]
                    c1 = P.dve(lambda e, h=h, q=q: e.tensor_scalar(
                        cscr1P[q][:], pp[4 + q][:, 256:320], fA[:, h:h + 1],
                        None, ALU.mult), waits=[t_rm[q], t_fA], inc="dve")
                    c2 = P.dve(lambda e, q=q: e.tensor_add(
                        cscr2P[q][:], cscr1P[q][:], pp[4 + q][:, 320:384]),
                        waits=[c1], inc="dve")
                    c3 = P.dve(lambda e, h=h, q=q: e.tensor_scalar(
                        reads[:, h * 64:(h + 1) * 64], cscr2P[q][:],
                        tok16[:, 64 + h:64 + h + 1], None, ALU.mult),
                        waits=[c2], inc="dve")
                    rd_prev[q] = c3
                t_kst = {}
                for q in range(2):
                    h = hs[q]
                    t_kst[q] = P.dve(lambda e, h=h, q=q: e.tensor_scalar(
                        kstP[q][:], kT[:, h * 64:(h + 1) * 64],
                        fstate[:, h:h + 1], None, ALU.mult),
                        waits=[t_vk, t_fs, rd_prev[q]], inc="dve")
                t_sm = {}
                for q in range(2):
                    h = hs[q]
                    hb = (h % 2) * 64
                    t_sm[q] = P.pe(lambda e, h=h, q=q, hb=hb: e.matmul(
                        pp[4 + q][hb:hb + 64, 0:64], kstP[q][:],
                        Wt_[:, h * 64:(h + 1) * 64],
                        start=True, stop=True,
                        tile_position=((0, hb) if hb else None)),
                        waits=[t_kst[q], t_w[q], st_prev[q]],
                        inc="pe", mmk=(0, 128, hb))
                for q in range(2):
                    h = hs[q]
                    hb = (h % 2) * 64
                    t_su = P.dve(lambda e, h=h, q=q, hb=hb:
                                 e.scalar_tensor_tensor(
                        S0[hsl(h), csl(h)], S0[hsl(h), csl(h)],
                        acbc[hsl(h), h:h + 1], pp[4 + q][hb:hb + 64, 0:64],
                        ALU.mult, ALU.add),
                        waits=[t_sm[q], t_acbe], inc="dve")
                    t_sb = P.dve(lambda e, h=h: e.tensor_copy(
                        S0b[hsl(h), csl(h)], S0[hsl(h), csl(h)]),
                        waits=[t_su], inc="dve")
                    st_prev[q] = t_sb
                    hprevP[q] = [t_sb]
                vdrive(c, 1)
            vdrive(c, 99)
            t_rdall = [rd_prev[0], rd_prev[1]]
            t_stall = [st_prev[0], st_prev[1]]
            if ci == 0:
                dbg_store("Wt0", Wt_[:], t_stall)
                dbg_store("reads0", reads[:], t_rdall)
                dbg_store("S0a", S0[:], t_stall)

            t_msq = P.act(lambda e: e.activation(sqs[:], reads[:], AF.Square),
                          waits=[t_rdall], inc="act")
            t_mrd = P.dve(lambda e: e.tensor_reduce(
                small6[:].unsqueeze(2),
                sqs[:].rearrange("p (h j) -> p h j", j=64), AX.X, ALU.add),
                waits=[t_msq], inc="dve")
            t_mr2 = P.dve(lambda e: e.tensor_scalar(
                small6[:], small6[:], 1.0 / 64.0, None, ALU.mult),
                waits=[t_mrd], inc="dve")
            t_mr3 = P.act(lambda e: e.activation(
                small6[:], small6[:], AF.Sqrt, bias=MISC[:, 1:2]),
                waits=[t_mr2, toks["misc_ready"]], inc="act")
            t_mr4 = P.dve(lambda e: e.reciprocal(small6[:], small6[:]),
                          waits=[t_mr3], inc="dve")
            t_m = P.dve(lambda e: e.tensor_tensor(
                mtile[:].rearrange("p (h j) -> p h j", j=64),
                reads[:].rearrange("p (h j) -> p h j", j=64),
                small6[:].unsqueeze(2).broadcast_to((128, 16, 64)), ALU.mult),
                waits=[t_mr4, t_rdall], inc="dve")
            mt_toks = []
            for t8 in range(8):
                bk, war = rot_mt.alloc()
                view = bk[:, 0:512].bitcast(BF16)[:, 0:128]
                tt = P.pe(lambda e, t8=t8, view=view: e.transpose(
                    view, mtile[:, t8 * 128:(t8 + 1) * 128], IDB[:]),
                    waits=[t_m, war], inc="pe", mmk=(0, 128))
                tev = P.dve(lambda e, t8=t8, view=view: e.tensor_copy(
                    mT[:, t8 * 128:(t8 + 1) * 128], view),
                    waits=[tt, ev.get(("epi_mT", ci - 1))], inc="dve")
                rot_mt.set(tev)
                mt_toks.append(tev)
            ev[("epi_mT", ci)] = mt_toks[-1]
            if ci == 0:
                dbg_store("m0", mtile[:], [t_m, mt_toks[-1]])

            vxq = P.act(lambda e, c=c: e.activation(
                sqs[:, 0:512].rearrange("p (g t) -> p g t", t=128),
                vecbf[0][:].rearrange("p (g t) -> p g t", t=512)
                [:, :, c * 128:(c + 1) * 128], AF.Square),
                waits=[ev[("vecdone", ci)], t_msq, t_mrd, t_m], inc="act")
            vyq = P.act(lambda e, c=c: e.activation(
                sqs[:, 512:1024].rearrange("p (g t) -> p g t", t=128),
                vecbf[1][:].rearrange("p (g t) -> p g t", t=512)
                [:, :, c * 128:(c + 1) * 128], AF.Square),
                waits=[vxq], inc="act")
            vm1 = P.pe(lambda e: e.matmul(
                pp[5][0:1, 0:512], aps["sb_ones_col"][:, 0:1],
                sqs[:, 0:512], start=True, stop=False),
                waits=[vxq, vyq, toks["ones_col_ready"], t_stall], inc=None,
                mmk=(0, 128))
            vm2 = P.pe(lambda e: e.matmul(
                pp[5][0:1, 0:512], aps["sb_ones_col"][:, 0:1],
                sqs[:, 512:1024], start=False, stop=True), inc="pe",
                mmk=(0, 128))
            t_vm = P.tok("pe")
            vr1 = P.dve(lambda e: e.tensor_reduce(
                hm[0:1, 0:128].unsqueeze(2),
                pp[5][0:1, 0:512].rearrange("o (g t) -> o t g", g=4),
                AX.X, ALU.add), waits=[t_vm], inc="dve")
            vr2 = P.dve(lambda e: e.tensor_scalar(
                hm[0:1, 0:128], hm[0:1, 0:128], 1.0 / 1024.0, None, ALU.mult),
                waits=[vr1], inc="dve")
            vr3 = P.act(lambda e: e.activation(
                hm[0:1, 0:128], hm[0:1, 0:128], AF.Sqrt, bias=MISC[0:1, 1:2]),
                waits=[vr2], inc="act")
            vr4 = P.dve(lambda e: e.reciprocal(hm[0:1, 0:128],
                                               hm[0:1, 0:128]),
                        waits=[vr3], inc="dve")
            vr5 = P.pe(lambda e: e.transpose(pp[4][:, 0:1],
                                             hm[0:1, 0:128], IDF[0:1, 0:1]),
                       waits=[vr4, t_vm, t_stall], inc="pe", mmk=(0, 1))
            vr6 = P.dve(lambda e: e.tensor_copy(rstdv[:], pp[4][:, 0:1]),
                        waits=[vr5], inc="dve")

            mm_toks = {}
            for g in range(2):
                for part, bank0 in (("m", 0), ("v", 2)):
                    bank = bank0 + g
                    w0 = [mt_toks[-1], t_wo, ev.get(("vecdone", ci)),
                          ev.get(("epi_ev", ci - 1, part, g)),
                          ev.get(("ctrl_last", sup)), t_stall]
                    for kk in range(8):
                        if part == "m":
                            lhs = lambda e, kk=kk: mT[:, kk * 128:(kk + 1) * 128]
                            wok = kk
                            is_last = False
                        else:
                            sig = kk // 4
                            pg = kk % 4
                            lhs = (lambda e, sig=sig, pg=pg, c=c:
                                   vecbf[sig][:].rearrange(
                                       "p (g t) -> p g t", t=512)
                                   [:, pg, c * 128:(c + 1) * 128])
                            wok = 8 + kk
                            is_last = (kk == 7)
                        P.pe(lambda e, lhs=lhs, wok=wok, g=g, bank=bank,
                             kk=kk, is_last=is_last: e.matmul(
                            pp[bank][:, 0:512], lhs(e),
                            wo[:, wok * 1024 + g * 512:
                               wok * 1024 + (g + 1) * 512],
                            start=(kk == 0), stop=is_last),
                            waits=w0 if kk == 0 else [],
                            inc=("pe" if is_last else None), mmk=(0, 128))
                    if part == "m":
                        P.pe(lambda e, g=g, bank=bank: e.matmul(
                            pp[bank][:, 0:512], ONESRB[0:1, 0:128],
                            bout[0:1, g * 512:(g + 1) * 512], start=False,
                            stop=True), waits=[t_bo], inc="pe", mmk=(0, 1))
                    mm_toks[(part, g)] = P.tok("pe")

            tld_og = P.sp(lambda e, ci=ci: e.dma_start(
                ogst[:], ogbuf[ci * 128:(ci + 1) * 128, :]),
                waits=[ev[("ogstore", sup, 3)],
                       ev.get(("ogate", ci - 1))], stream="og")
            ev[("ogload", sup, c)] = tld_og
            t_xr = P.sp(lambda e, ci=ci: e.dma_start(
                sqs[:], x_in[ci * 128:(ci + 1) * 128, :]),
                waits=[t_vm, vyq], stream=f"xr{ci % 2}")
            st_evs = []
            o4prev = [None]
            for g in range(2):
                o1 = P.dve(lambda e, g=g: e.tensor_scalar(
                    ocomb[:, g * 512:(g + 1) * 512], pp[2 + g][:, 0:512],
                    rstdv[:, 0:1], None, ALU.mult),
                    waits=[mm_toks[("v", g)], vr6], inc="dve")
                o2 = P.dve(lambda e, g=g: e.tensor_add(
                    ocomb[:, g * 512:(g + 1) * 512],
                    ocomb[:, g * 512:(g + 1) * 512], pp[g][:, 0:512]),
                    waits=[o1, mm_toks[("m", g)]], inc="dve")
                ev[("epi_ev", ci, "v", g)] = o1
                ev[("epi_ev", ci, "m", g)] = o2
                o3 = P.act(lambda e, g=g: e.activation(
                    gate[:], ogst[:, g * 512:(g + 1) * 512], AF.Sigmoid),
                    waits=[tld_og, vyq, o4prev[0]], inc="act")
                o4 = P.dve(lambda e, g=g: e.tensor_mul(
                    ocomb[:, g * 512:(g + 1) * 512],
                    ocomb[:, g * 512:(g + 1) * 512], gate[:]),
                    waits=[o2, o3], inc="dve")
                o4prev[0] = o4
                o5 = P.dve(lambda e, g=g, ci=ci: e.tensor_add(
                    yc[ci % 2][:, g * 512:(g + 1) * 512],
                    ocomb[:, g * 512:(g + 1) * 512],
                    sqs[:, g * 512:(g + 1) * 512]),
                    waits=[o4, t_xr, ev.get(("storedma", ci - 2))], inc="dve")
                st_evs.append(o5)
            ev[("ogate", ci)] = o3
            t_store = P.sp(lambda e, ci=ci: e.dma_start(
                x_out[ci * 128:(ci + 1) * 128, :], yc[ci % 2][:]),
                waits=st_evs, stream=f"st{ci % 2}")
            ev[("storedma", ci)] = t_store
            ev[("store", ci)] = t_store
            ev[("dn_done", ci)] = st_evs[-1]
        ev[("join", sup)] = join(sup)

    fin = [P.tok("dma_st0"), P.tok("dma_st1")]
    fin = [t for t in fin if t[1] > 0]
    if fin:
        P.emit("sp", lambda e: e.nop(), waits=fin)
    return fin


def build_program(S_len, stages, io_kind="external", dbg_spec=None):
    nc = bass.Bass("TRN2", target_bir_lowering=False)
    aps = {}
    for name, shape in INPUT_SHAPES.items():
        aps[name] = nc.dram_tensor(name, list(shape), F32,
                                   kind="ExternalInput").ap()
    for name, shape in CONST_SHAPES.items():
        dt = BF16 if name.endswith("_b") else F32
        aps[name] = nc.dram_tensor(name, list(shape), dt,
                                   kind="ExternalInput").ap()
    aps["x0"] = nc.dram_tensor("x0", [S_len, D], F32, kind="ExternalInput").ap()
    aps["xout"] = nc.dram_tensor("xout", [S_len, D], F32,
                                 kind="ExternalOutput").ap()
    dbg = None
    if dbg_spec:
        dbg = {}
        for name, shape, dt in dbg_spec:
            dbg[name] = nc.dram_tensor("dbg_" + name, list(shape), dt,
                                       kind="ExternalOutput").ap()
    n_inter = len(stages) - 1
    inter = [nc.dram_tensor(f"xbuf{i}", [S_len, D], F32, kind="Internal").ap()
             for i in range(n_inter)]

    with contextlib.ExitStack() as st:
        P = Prog(nc)
        aps["psum"] = [st.enter_context(nc.psum_tensor(f"pp{i}", [128, 512], F32))
                       for i in range(8)]
        aps["arena"] = Arena(nc, st, 49500)
        toks = emit_prologue(P, st, aps)
        toks["start"] = [toks["consts_ready"]]
        bufs = ["x0"] + [f"xbuf{i}" for i in range(n_inter)] + ["xout"]
        buf_aps = [aps["x0"]] + inter + [aps["xout"]]
        aps["dbg"] = dbg
        for si, stage in enumerate(stages):
            aps["x_in"] = buf_aps[si]
            aps["x_out"] = buf_aps[si + 1]
            if stage == "l0_attn":
                fin = STAGE_EMITTERS[stage](P, st, aps, toks, S_len, dbg=dbg)
            else:
                fin = STAGE_EMITTERS[stage](P, st, aps, toks, S_len)
            toks["start"] = fin
        P.build(st)
    return nc


INPUT_SHAPES = {
    "L0_attn_W_ctrl": (1024, 6688), "L0_attn_b_ctrl": (6688,),
    "L0_attn_W_block": (512, 2, 2), "L0_attn_qconv_w": (1024, 4),
    "L0_attn_kconv_w": (1024, 4), "L0_attn_vconv_w": (1024, 4),
    "L0_attn_rms_m_w": (64,), "L0_attn_rms_v_w": (1024,),
    "L0_attn_W_out": (2048, 1024), "L0_attn_b_out": (1024,),
    "L0_ffn_router_w": (8, 1024), "L0_ffn_router_b": (8,),
    "L0_ffn_w1": (8, 1024, 4096), "L0_ffn_b1": (8, 4096),
    "L0_ffn_w2": (8, 4096, 1024), "L0_ffn_b2": (8, 1024),
    "L0_ffn_gamma": (1024,), "L0_ffn_beta": (1024,),
    "L1_attn_A": (1024,), "L1_attn_B": (1024, 1024),
    "L1_attn_W": (1024, 1024), "L1_attn_b": (1024,),
    "L1_ffn_kernel": (1024, 1024), "L1_ffn_bias": (1024,),
    "L1_ffn_a": (1024, 16), "L1_ffn_b": (16, 1024),
}

STAGE_EMITTERS = {
    "l0_attn": emit_l0_attn,
    "l1_ffn": emit_l1_ffn,
    "l1_attn": emit_l1_attn,
    "l0_ffn": emit_l0_ffn,
}


def kernel(**inputs):
    S_len = inputs["L0_attn_hidden"].shape[1]
    nc = build_program(S_len, ["l0_attn", "l0_ffn", "l1_attn", "l1_ffn"])
    consts = host_constants()
    in_maps = []
    for core in range(NCORES):
        m = {k: np.ascontiguousarray(inputs[k]) for k in INPUT_SHAPES}
        m.update(consts)
        m["x0"] = np.ascontiguousarray(inputs["L0_attn_hidden"][core])
        in_maps.append(m)
    res = run_bass_kernel_spmd(nc, in_maps, core_ids=list(range(NCORES)))
    out = np.stack([res.results[i]["xout"] for i in range(NCORES)], axis=0)
    return out.astype(np.float32)
